# version 04
# baseline (speedup 1.0000x reference)
import numpy as np
import ml_dtypes
import concourse.bass as bass
import concourse.mybir as mybir
from concourse.bass_utils import run_bass_kernel_spmd

F32 = mybir.dt.float32
BF16 = mybir.dt.bfloat16
AF = mybir.ActivationFunctionType
ALU = mybir.AluOpType

BF = ml_dtypes.bfloat16

L = 2048
D = 2048
DH = 128
FF = 5632
GU = 2 * FF
EPS = 1e-5
MAGIC = 12582912.0

NTT = 4
TT = 512
NSB = 4
SBT = 512
NCI = 8
C = 128
TAU = 64
HALF = 1024


def _prep(inputs):
    f32 = np.float32
    x = np.ascontiguousarray(np.asarray(inputs["L0_attn_x"], f32))
    Wq = np.asarray(inputs["L0_attn_Wq"], f32)
    Wk = np.asarray(inputs["L0_attn_Wk"], f32)
    Wv = np.asarray(inputs["L0_attn_Wv"], f32)
    Wb = np.asarray(inputs["L0_attn_Wb"], f32)
    Wa = np.asarray(inputs["L0_attn_Wa"], f32)
    Wg = np.asarray(inputs["L0_attn_Wg"], f32)
    Wo = np.asarray(inputs["L0_attn_Wo"], f32)
    A_log = np.asarray(inputs["L0_attn_A_log"], f32)
    dtb = np.asarray(inputs["L0_attn_dt_bias"], f32)
    cq = np.asarray(inputs["L0_attn_conv_q"], f32)
    ck = np.asarray(inputs["L0_attn_conv_k"], f32)
    cv = np.asarray(inputs["L0_attn_conv_v"], f32)
    onw = np.asarray(inputs["L0_attn_o_norm_w"], f32)
    wgu = np.asarray(inputs["L0_ffn_w_gate_up"], f32)
    wd = np.asarray(inputs["L0_ffn_w_down"], f32)
    nw = np.asarray(inputs["L0_ffn_norm_weight"], f32)

    iden = np.eye(128, dtype=f32)
    sidx = np.arange(C)
    tau = np.arange(TAU)
    shared = {
        "wguT": np.ascontiguousarray(wgu.T),
        "wdT": np.ascontiguousarray(wd.T),
        "woT": np.ascontiguousarray(Wo.T),
        "nw2": np.ascontiguousarray(nw.reshape(16, 128).T),
        "onw_tile": np.ascontiguousarray(np.tile(onw, (128, 4))),
        "iden_f": iden,
        "iden_h": np.ascontiguousarray(iden * 0.5),
        "iden_b": iden.astype(BF),
        "ones1": np.ones((1, 128), f32),
        "ones128": np.ones((128, 1), f32),
        "maskA": np.where(sidx[None, :] < sidx[:, None], 0.0, 200.0
                          ).astype(f32),
        "maskQ": np.where(sidx[:, None] <= 2 * tau[None, :] + 1, 0.0, -200.0
                          ).astype(f32),
    }

    def diag_taps(cw, rows, nchunks):
        out = np.zeros((nchunks, 4, 128, 128), f32)
        w = cw[rows]
        for mc in range(nchunks):
            for t in range(4):
                np.fill_diagonal(out[mc, t], w[mc * 128:(mc + 1) * 128, t])
        return np.ascontiguousarray(
            out.transpose(2, 0, 1, 3).reshape(128, nchunks * 4 * 128)
        ).astype(BF)

    percore = []
    for c in range(8):
        b, hg = c // 2, c % 2
        hsl = slice(hg * 512, hg * 512 + 512)
        krows = np.concatenate(
            [n * 1024 + hg * 512 + np.arange(512) for n in range(2)])
        brows = np.concatenate(
            [n * 8 + hg * 4 + np.arange(4) for n in range(2)])
        arows = np.arange(hg * 4, hg * 4 + 4)
        xb = x[b]
        pc = {
            "xT": np.ascontiguousarray(xb.T),
            "xhT": np.ascontiguousarray(xb[hg * HALF:(hg + 1) * HALF].T),
            "wqT": np.ascontiguousarray(Wq[hsl].T),
            "wkT": np.ascontiguousarray(Wk[krows].T),
            "wvT": np.ascontiguousarray(Wv[krows].T),
            "wbT": np.ascontiguousarray(Wb[brows].T),
            "waT": np.ascontiguousarray(Wa[arows].T),
            "wgT": np.ascontiguousarray(Wg[hsl].T),
            "alog": np.ascontiguousarray(A_log[arows].reshape(4, 1)),
            "dtb": np.ascontiguousarray(dtb[arows].reshape(4, 1)),
            "mask_me": np.full((128, 1), 1.0 - hg, f32),
            "mask_pa": np.full((128, 1), float(hg), f32),
            "dq": diag_taps(cq, np.arange(hg * 512, hg * 512 + 512), 4),
            "dk": diag_taps(ck, krows, 8),
            "dv": diag_taps(cv, krows, 8),
        }
        percore.append(pc)
    return shared, percore


class Sch:
    def __init__(self, nc, stack):
        self.nc = nc
        self.stack = stack
        self.sem = {}
        self.cnt = {}

    def s(self, name):
        if name not in self.sem:
            self.sem[name] = self.stack.enter_context(self.nc.semaphore(name))
            self.cnt[name] = 0
        return self.sem[name]

    def inc(self, instr, name, amt=1):
        self.s(name)
        self.cnt[name] += amt
        instr.then_inc(self.sem[name], amt)
        return self.cnt[name]

    def dinc(self, instr, name):
        return self.inc(instr, name, 16)

    def wait(self, eng, name, val=None):
        self.s(name)
        v = self.cnt[name] if val is None else val
        if v > 0:
            eng.wait_ge(self.sem[name], v)

    def now(self, name):
        self.s(name)
        return self.cnt[name]

    def barrier(self, engines):
        names = list(self.sem.keys())
        for eng in engines:
            for nm in names:
                if self.cnt[nm] > 0:
                    eng.wait_ge(self.sem[nm], self.cnt[nm])


class Const:
    def __init__(self, nc, stack, capacity=24):
        self.nc = nc
        self.tile = stack.enter_context(
            nc.sbuf_tensor("constpool", [128, capacity], F32))
        self.vals = {}
        self.cap = capacity

    def fill(self, v):
        if v not in self.vals:
            i = len(self.vals)
            assert i < self.cap
            self.vals[v] = i
            self.nc.vector.memset(self.tile[:, i:i + 1], float(v))
        return self.tile[:, self.vals[v]:self.vals[v] + 1]

    def ap(self, v):
        i = self.vals[v]
        return self.tile[:, i:i + 1]


DBG_SPECS = {
    "xn": ([D, L], BF16),
    "beta": ([8, L], F32),
    "g": ([4, L], F32),
    "silu_gg": ([128, 16 * 512], BF16),
    "G_S": ([128, 128], F32),
    "beta_S": ([128, 128], F32),
    "ehat_S": ([128, 128], F32),
    "lamC": ([128, 128], F32),
    "lamT": ([128, 128], F32),
    "ehatT": ([128, 128], F32),
    "gd": ([128, 128], F32),
    "gd2": ([128, 128], F32),
    "ogT": ([128, 8192], BF16),
    "qf": ([128, 8 * 1024], BF16),
    "kf": ([128, 8 * 2048], BF16),
    "vf": ([128, 8 * 2048], BF16),
    "LS": ([128, 128], F32),
    "U4": ([128, 512], BF16),
    "S4": ([128, 512], F32),
    "nA4": ([128, 512], BF16),
    "AT4": ([128, 256], BF16),
    "QB4": ([128, 256], BF16),
    "KH4": ([128, 512], BF16),
    "U4d": ([128, 512], BF16),
    "S4d": ([128, 512], F32),
    "Zf0": ([128, 1024], BF16),
    "PN2d": ([128, 512], BF16),
    "sT": ([D, HALF], BF16),
    "nT": ([D, HALF], BF16),
    "mh": ([128, 1024], F32),
    "macol": ([128, 8], F32),
    "fscols": ([128, 288], F32),
    "h0": ([128, 1024], F32),
    "nh": ([128, 45056], BF16),
    "guT": ([128, 1024], F32),
    "psibc": ([128, 1024], F32),
    "dqbc": ([128, 1024], F32),
    "colx": ([128, 40], F32),
    "hfull": ([5632, 1024], F32),
    "p5b_g": ([128, 512], F32),
    "p5b_u": ([128, 512], F32),
    "p5b_h": ([128, 512], F32),
    "p5b_t": ([128, 8192], BF16),
    "p5b_t2": ([128, 8192], BF16),
    "p5b_l": ([128, 1024], F32),
    "p5b_mp": ([128, 1024], BF16),
    "p5b_mm": ([128, 1024], BF16),
    "p5b_n": ([128, 16384], BF16),


}

ENG = ("vector", "scalar", "tensor", "gpsimd", "sync")


class Build:
    def __init__(self, dbg=()):
        self.nc = bass.Bass("TRN2", target_bir_lowering=False, debug=False,
                            detect_race_conditions=False)
        from contextlib import ExitStack
        self.stack = ExitStack()
        self.sch = Sch(self.nc, self.stack)
        self.cst = Const(self.nc, self.stack)
        self.din = {}
        self.sb = {}
        self.ps = {}
        self.scr = {}
        self.dbg = dbg
        self.dbg_t = {}
        self._declare()

    def I(self, name, shape, dt=F32):
        self.din[name] = self.nc.dram_tensor(
            name, shape, dt, kind="ExternalInput").ap()

    def S(self, name, shape, dt=F32, stack=None):
        (stack or self.stack).enter_context
        st = stack or self.stack
        self.sb[name] = st.enter_context(self.nc.sbuf_tensor(name, shape, dt))
        return self.sb[name]

    def P(self, name, shape, dt=F32, stack=None):
        st = stack or self.stack
        self.ps[name] = st.enter_context(self.nc.psum_tensor(name, shape, dt))
        return self.ps[name]

    def _declare(self):
        nc = self.nc
        I = self.I
        I("xT", [D, L]); I("xhT", [D, HALF])
        I("wqT", [D, 512]); I("wkT", [D, 1024]); I("wvT", [D, 1024])
        I("wbT", [D, 8]); I("waT", [D, 4]); I("wgT", [D, 512])
        I("woT", [1024, D]); I("wguT", [D, GU]); I("wdT", [FF, D])
        I("nw2", [128, 16]); I("onw_tile", [128, 512])
        I("alog", [4, 1]); I("dtb", [4, 1])
        I("iden_f", [128, 128]); I("iden_h", [128, 128])
        I("iden_b", [128, 128], BF16)
        I("ones1", [1, 128]); I("ones128", [128, 1])
        I("maskA", [128, 128]); I("maskQ", [128, TAU])
        I("mask_me", [128, 1]); I("mask_pa", [128, 1])
        I("dq", [128, 4 * 4 * 128], BF16)
        I("dk", [128, 8 * 4 * 128], BF16)
        I("dv", [128, 8 * 4 * 128], BF16)

        self.out_t = nc.dram_tensor("outT", [D, HALF], F32,
                                    kind="ExternalOutput").ap()
        for name in self.dbg:
            shape, dt = DBG_SPECS[name]
            self.dbg_t[name] = nc.dram_tensor(
                "dbg_" + name, shape, dt, kind="ExternalOutput").ap()

        scr = self.scr
        scr["xn"] = nc.dram_tensor("xn_scr", [D, L], BF16).ap()
        scr["ba"] = nc.dram_tensor("ba_scr", [12, L], F32).ap()
        scr["n"] = nc.dram_tensor("n_scr", [D, HALF], BF16).ap()
        scr["h"] = nc.dram_tensor("h_scr", [FF, HALF], F32).ap()
        scr["s"] = nc.dram_tensor("s_scr", [D, HALF], BF16).ap()
        scr["ogi"] = nc.dram_tensor("og_in", [512, L], BF16)
        scr["ogo"] = nc.dram_tensor("og_out", [1024, L], BF16)

        S = self.S
        S("iden_f_sb", [128, 128]); S("iden_h_sb", [128, 128])
        S("iden_b_sb", [128, 128], BF16)
        S("ones1_sb", [1, 128]); S("ones128_sb", [128, 1])
        S("maskA_sb", [128, 128]); S("maskQ_sb", [128, TAU])
        S("maskme_sb", [128, 1]); S("maskpa_sb", [128, 1])
        S("onw_sb", [128, 512]); S("nw_sb", [128, 16])
        S("alog_sb", [4, 1]); S("dtb_sb", [4, 1]); S("negea", [4, 1])
        for nm in ("GT", "lamT", "G_S", "beta_S", "ehat_S", "L_S", "GpS",
                   "nbrn_S", "ehr_S", "blr_S", "lamC", "bl_S", "LT_all"):
            S(nm, [128, 128])
        S("wld0", [128, 1024]); S("wld1", [128, 1024])

    def engines(self):
        nc = self.nc
        return nc.vector, nc.scalar, nc.tensor, nc.gpsimd, nc.sync

    def phase0(self):
        V, A_, T_, G_, SY = self.engines()
        sch, cst, sb, din = self.sch, self.cst, self.sb, self.din
        for v in (0.0, 1.0, 0.5, -0.5, 2.0, EPS, 1.0 / 2048.0, MAGIC,
                  1e-24, 1e-5, 127.0, 1.0 / 127.0, 0.03577708764, 200.0,
                  -200.0, 1.0 / 128.0, -1.0):
            cst.fill(v)
        ld = V.memset(cst.tile[:, 23:24], 0.0)
        sch.inc(ld, "dve")

        small = [("iden_f", "iden_f_sb"), ("iden_h", "iden_h_sb"),
                 ("iden_b", "iden_b_sb"),
                 ("ones1", "ones1_sb"), ("ones128", "ones128_sb"),
                 ("maskA", "maskA_sb"), ("maskQ", "maskQ_sb"),
                 ("mask_me", "maskme_sb"), ("mask_pa", "maskpa_sb"),
                 ("onw_tile", "onw_sb"), ("nw2", "nw_sb"),
                 ("alog", "alog_sb"), ("dtb", "dtb_sb"),
                 ("dq", "dq_sb"), ("dk", "dk_sb"), ("dv", "dv_sb")]
        for src, dst in small:
            if dst not in sb:
                continue
            d = SY.dma_start(out=sb[dst][:], in_=din[src])
            sch.dinc(d, "dsm")
        self.n_small = sch.now("dsm")

        A_.wait_ge(sch.s("dsm"), self.n_small)
        e1 = A_.activation(sb["negea"][:], sb["alog_sb"][:], AF.Exp)
        sch.inc(e1, "act")
        V.wait_ge(sch.s("act"), sch.now("act"))
        sch.wait(V, "dve")
        n1 = V.tensor_single_scalar(sb["negea"][:], sb["negea"][:], -1.0,
                                    ALU.mult)
        sch.inc(n1, "dve")

    def phase1(self, stA):
        V, A_, T_, G_, SY = self.engines()
        sch, cst, sb, ps, din = self.sch, self.cst, self.sb, self.ps, self.din
        scr, dbg_t = self.scr, self.dbg_t
        S = self.S
        for nm in ("pb0", "pb1", "pb2", "pb3"):
            self.P(nm, [128, 512], stack=stA)
        S("x_sb", [128, 16 * TT], stack=stA)
        S("sq0", [128, TT], stack=stA); S("sq1", [128, TT], stack=stA)
        S("xn_t", [128, 16 * TT], BF16, stack=stA)
        S("rows", [1, 6 * TT], stack=stA)
        S("beta_sb", [8, L], stack=stA); S("g_sb", [4, L], stack=stA)
        S("spA", [4, TT], stack=stA); S("spB", [4, TT], stack=stA)

        R = sb["rows"]
        def row(i):
            return R[0:1, i * TT:(i + 1) * TT]

        sch.wait(T_, "wcast")

        for tt in range(NTT):
            if tt > 0:
                SY.wait_ge(sch.s("pool"), sch.now("pool"))
                SY.wait_ge(sch.s("dve"), sch.now("dve"))
                SY.wait_ge(sch.s("pe"), sch.now("pe"))
            d = SY.dma_start(
                out=sb["x_sb"][:].rearrange("p (f t) -> p f t", f=16),
                in_=din["xT"][:, tt * TT:(tt + 1) * TT].rearrange(
                    "(f p) t -> p f t", p=128))
            sch.dinc(d, "dx")
            T_.wait_ge(sch.s("dx"), sch.now("dx"))
            A_.wait_ge(sch.s("dx"), sch.now("dx"))
            sqb = [sb["sq0"], sb["sq1"]]
            for fc in range(16):
                xc = sb["x_sb"][:, fc * TT:(fc + 1) * TT]
                tgt = sch.now("pe") - 1
                if tgt > 0:
                    A_.wait_ge(sch.s("pe"), tgt)
                q = A_.activation(sqb[fc % 2][:], xc, AF.Square)
                sch.inc(q, "act")
                T_.matmul(ps["pb0"][0:1, :], sb["ones128_sb"][:], xc,
                          start=(fc == 0), stop=(fc == 15))
                T_.wait_ge(sch.s("act"), sch.now("act"))
                mm = T_.matmul(ps["pb1"][0:1, :], sb["ones128_sb"][:],
                               sqb[fc % 2][:], start=(fc == 0),
                               stop=(fc == 15))
                sch.inc(mm, "pe")
            A_.wait_ge(sch.s("pe"), sch.now("pe"))
            m1 = A_.activation(row(0), ps["pb0"][0:1, :], AF.Copy,
                               scale=1.0 / 2048.0)
            m2 = A_.activation(row(2), ps["pb1"][0:1, :], AF.Copy,
                               scale=1.0 / 2048.0)
            sch.inc(m2, "act")
            V.wait_ge(sch.s("act"), sch.now("act"))
            V.tensor_tensor(row(4), row(0), row(0), ALU.mult)
            V.tensor_tensor(row(4), row(2), row(4), ALU.subtract)
            v3 = V.tensor_single_scalar(row(4), row(4), EPS, ALU.add)
            sch.inc(v3, "dve")
            A_.wait_ge(sch.s("dve"), sch.now("dve"))
            s1 = A_.activation(row(1), row(4), AF.Sqrt)
            sch.inc(s1, "act")
            V.wait_ge(sch.s("act"), sch.now("act"))
            r1 = V.reciprocal(row(1), row(1))
            sch.inc(r1, "dve")
            T_.wait_ge(sch.s("dve"), sch.now("dve"))
            T_.matmul(ps["pb2"][:], sb["ones1_sb"][:], row(0), start=True,
                      stop=True)
            bb = T_.matmul(ps["pb3"][:], sb["ones1_sb"][:], row(1),
                           start=True, stop=True)
            sch.inc(bb, "pe")
            A_.wait_ge(sch.s("pe"), sch.now("pe"))
            A_.copy(sb["sq0"][:], ps["pb2"][:])
            c2 = A_.copy(sb["sq1"][:], ps["pb3"][:])
            sch.inc(c2, "act")
            G_.wait_ge(sch.s("act"), sch.now("act"))
            for fc in range(16):
                xc = sb["x_sb"][:, fc * TT:(fc + 1) * TT]
                G_.tensor_tensor(xc, xc, sb["sq0"][:], ALU.subtract)
                g2 = G_.tensor_tensor(sb["xn_t"][:, fc * TT:(fc + 1) * TT],
                                      xc, sb["sq1"][:], ALU.mult)
                sch.inc(g2, "pool")
            SY.wait_ge(sch.s("pool"), sch.now("pool"))
            d2 = SY.dma_start(
                out=scr["xn"][:, tt * TT:(tt + 1) * TT].rearrange(
                    "(f p) t -> p f t", p=128),
                in_=sb["xn_t"][:].rearrange("p (f t) -> p f t", f=16))
            sch.dinc(d2, "dxn")
            T_.wait_ge(sch.s("pool"), sch.now("pool"))
            T_.wait_ge(sch.s("act"), sch.now("act"))
            for fc in range(16):
                xnc = sb["xn_t"][:, fc * TT:(fc + 1) * TT]
                T_.matmul(ps["pb0"][0:8, :],
                          sb["wba1_sb"][:, fc * 8:(fc + 1) * 8],
                          xnc, start=(fc == 0), stop=(fc == 15))
                mm = T_.matmul(ps["pb1"][0:4, :],
                               sb["wba2_sb"][:, fc * 4:(fc + 1) * 4],
                               xnc, start=(fc == 0), stop=(fc == 15))
                sch.inc(mm, "pe")
            A_.wait_ge(sch.s("pe"), sch.now("pe"))
            a0 = A_.activation(sb["beta_sb"][:, tt * TT:(tt + 1) * TT],
                               ps["pb0"][0:8, :], AF.Sigmoid)
            sch.inc(a0, "act")
            V.wait_ge(sch.s("act"), sch.now("act"))
            a1 = V.tensor_scalar(sb["spA"][:], ps["pb1"][0:4, :],
                                 sb["dtb_sb"][:], None, ALU.add)
            b2m = V.tensor_single_scalar(
                sb["beta_sb"][:, tt * TT:(tt + 1) * TT],
                sb["beta_sb"][:, tt * TT:(tt + 1) * TT], 2.0, ALU.mult)
            sch.inc(b2m, "dve")
            A_.wait_ge(sch.s("dve"), sch.now("dve"))
            A_.activation(sb["spB"][:], sb["spA"][:], AF.Abs)
            A_.activation(sb["spB"][:], sb["spB"][:], AF.Exp, scale=-1.0)
            u3 = A_.activation(sb["spB"][:], sb["spB"][:], AF.Ln,
                               bias=cst.ap(1.0)[0:4, :])
            sch.inc(u3, "act")
            V.wait_ge(sch.s("act"), sch.now("act"))
            V.tensor_single_scalar(sb["spA"][:], sb["spA"][:], 0.0, ALU.max)
            V.tensor_tensor(sb["spA"][:], sb["spA"][:], sb["spB"][:], ALU.add)
            u6 = V.tensor_scalar(sb["g_sb"][:, tt * TT:(tt + 1) * TT],
                                 sb["spA"][:], sb["negea"][:], None, ALU.mult)
            sch.inc(u6, "dve")
            for st in range(4):
                pg = ps["pb2"] if st % 2 == 0 else ps["pb3"]
                T_.wait_ge(sch.s("act"), sch.now("act"))
                T_.wait_ge(sch.s("dve"), sch.now("dve"))
                for fc in range(16):
                    xnc = sb["xn_t"][:, fc * TT + st * 128:
                                     fc * TT + st * 128 + 128]
                    mm = T_.matmul(pg[:], xnc,
                                   sb["wg_sb"][:, fc * 512:(fc + 1) * 512],
                                   start=(fc == 0), stop=(fc == 15))
                sch.inc(mm, "pe")
                A_.wait_ge(sch.s("pe"), sch.now("pe"))
                A_.wait_ge(sch.s("dve"), sch.now("dve"))
                scrt = sb["sq0"] if st % 2 == 0 else sb["sq1"]
                sl = A_.activation(scrt[:], pg[:], AF.Silu)
                sch.inc(sl, "act")
                V.wait_ge(sch.s("act"), sch.now("act"))
                gw = V.tensor_tensor(
                    sb["silu_gg"][:,
                                  (tt * 4 + st) * 512:(tt * 4 + st + 1) * 512],
                    scrt[:], sb["onw_sb"][:], ALU.mult)
                sch.inc(gw, "dve")

        SY.wait_ge(sch.s("dve"), sch.now("dve"))
        d = SY.dma_start(out=scr["ba"][0:8, :], in_=sb["beta_sb"][:])
        sch.dinc(d, "dba")
        d = SY.dma_start(out=scr["ba"][8:12, :], in_=sb["g_sb"][:])
        sch.dinc(d, "dba")

        for nm, src in (("beta", sb["beta_sb"][:]), ("g", sb["g_sb"][:]),
                        ("silu_gg", sb["silu_gg"][:])):
            if nm in dbg_t:
                dd = SY.dma_start(out=dbg_t[nm], in_=src)
                sch.dinc(dd, "dbg")
        if "xn" in dbg_t:
            SY.wait_ge(sch.s("dxn"), sch.now("dxn"))
            dd = SY.dma_start(out=dbg_t["xn"], in_=scr["xn"][:, :])
            sch.dinc(dd, "dbg")

    def load_small_weights(self, stA):
        V, A_, T_, G_, SY = self.engines()
        sch, sb, din = self.sch, self.sb, self.din
        S = self.S
        S("wba1_sb", [128, 16 * 8], BF16, stack=stA)
        S("wba2_sb", [128, 16 * 4], BF16, stack=stA)
        S("wg_sb", [128, 16 * 512], BF16, stack=stA)
        lands = [sb["wld0"], sb["wld1"]]
        widx = 0
        jobs = [("wgT", "wg_sb", 512), ("wbT", "wba1_sb", 8),
                ("waT", "wba2_sb", 4)]
        for name, dst, ncol in jobs:
            for kc in range(16):
                land = lands[widx % 2]
                if widx >= 2:
                    SY.wait_ge(sch.s("wcast"), widx - 1)
                d = SY.dma_start(out=land[:, 0:ncol],
                                 in_=din[name][kc * 128:(kc + 1) * 128, :])
                sem = f"dwm{widx % 2}"
                sch.dinc(d, sem)
                V.wait_ge(sch.s(sem), sch.now(sem))
                cc = V.tensor_copy(sb[dst][:, kc * ncol:(kc + 1) * ncol],
                                   land[:, 0:ncol])
                sch.inc(cc, "wcast")
                widx += 1
        self.widx = widx

    def phase2(self, stA):
        V, A_, T_, G_, SY = self.engines()
        sch, cst, sb, ps = self.sch, self.cst, self.sb, self.ps
        scr, dbg_t = self.scr, self.dbg_t
        S = self.S
        S("g0", [128, 128], stack=stA)
        S("bTt", [128, 128], stack=stA)

        S("gstg", [128, 192], stack=stA)
        z = V.memset(sb["g0"][:], 0.0)
        sch.inc(z, "dve")
        G_.wait_ge(sch.s("dba"), sch.now("dba"))
        for j in range(4):
            d = G_.dma_start(
                out=sb["gstg"][j * 32:(j + 1) * 32, 0:64],
                in_=scr["ba"][8 + j:9 + j, :].rearrange(
                    "r (i t) -> (r i) t", t=64))
            sch.dinc(d, "dgr")
        for n in range(2):
            for j in range(4):
                d = G_.dma_start(
                    out=sb["gstg"][j * 32:(j + 1) * 32,
                                   64 + n * 64:128 + n * 64],
                    in_=scr["ba"][n * 4 + j:n * 4 + j + 1, :].rearrange(
                        "r (i t) -> (r i) t", t=64))
                sch.dinc(d, "dgr")
        V.wait_ge(sch.s("dgr"), sch.now("dgr"))
        V.tensor_copy(sb["g0"][:].rearrange("p (t n) -> p n t", n=2)[:, 0, :],
                      sb["gstg"][:, 0:64])
        V.tensor_copy(sb["bTt"][:].rearrange("p (t n) -> p n t", n=2)[:, 0, :],
                      sb["gstg"][:, 64:128])
        cpy = V.tensor_copy(
            sb["bTt"][:].rearrange("p (t n) -> p n t", n=2)[:, 1, :],
            sb["gstg"][:, 128:192])
        sch.inc(cpy, "dve")
        sc = V.tensor_tensor_scan(sb["GT"][:], sb["g0"][:], sb["g0"][:],
                                  0.0, ALU.add, ALU.bypass)
        sch.inc(sc, "dve")
        sch.wait(T_, "dve")
        T_.transpose(ps["pb0"][:, 0:128], sb["GT"][:], sb["iden_f_sb"][:])
        t2 = T_.transpose(ps["pb1"][:, 0:128], sb["bTt"][:],
                          sb["iden_f_sb"][:])
        sch.inc(t2, "pe")
        sch.wait(A_, "pe")
        A_.copy(sb["G_S"][:], ps["pb0"][:, 0:128])
        e2 = A_.copy(sb["beta_S"][:], ps["pb1"][:, 0:128])
        sch.inc(e2, "act")
        gc_col = sb["GT"][:, 127:128]
        V.wait_ge(sch.s("dve"), sch.now("dve"))
        v1 = V.tensor_scalar(sb["g0"][:], sb["GT"][:], gc_col, None,
                             ALU.subtract)
        sch.inc(v1, "dve")
        if "gd2" in dbg_t:
            sch.wait(SY, "dve")
            dd = SY.dma_start(out=dbg_t["gd2"], in_=sb["g0"][:])
            sch.dinc(dd, "dbg")
            A_.wait_ge(sch.s("dbg"), sch.now("dbg"))
        sch.wait(A_, "dve")
        A_.activation(sb["g0"][:], sb["g0"][:], AF.Exp, scale=-1.0)
        a2 = A_.activation(sb["lamT"][:], sb["GT"][:], AF.Exp)
        sch.inc(a2, "act")
        sch.wait(T_, "act")
        t3 = T_.transpose(ps["pb2"][:, 0:128], sb["g0"][:],
                          sb["iden_f_sb"][:])
        t4 = T_.matmul(ps["pb3"][:, 0:128],
                       sb["lamT"][:, 127:128].broadcast_to((128, 128)),
                       sb["iden_f_sb"][:], start=True, stop=True)
        sch.inc(t4, "pe")
        sch.wait(A_, "pe")
        A_.copy(sb["ehat_S"][:], ps["pb2"][:, 0:128])
        e4 = A_.copy(sb["lamC"][:], ps["pb3"][:, 0:128])
        sch.inc(e4, "act")
        sch.wait(V, "act")
        lmS = V.tensor_scalar(sb["bl_S"][:], sb["G_S"][:], 0.0, None, ALU.add)
        sch.inc(lmS, "dve")
        sch.wait(A_, "dve")
        a5 = A_.activation(sb["bl_S"][:], sb["bl_S"][:], AF.Exp)
        sch.inc(a5, "act")
        sch.wait(V, "act")
        v5 = V.tensor_tensor(sb["bl_S"][:], sb["bl_S"][:], sb["beta_S"][:],
                             ALU.mult)
        sch.inc(v5, "dve")

        for nm, src in (("ehatT", sb["g0"]), ("gd", sb["GT"]),
                        ("G_S", sb["G_S"]), ("beta_S", sb["beta_S"]),
                        ("ehat_S", sb["ehat_S"]), ("lamC", sb["lamC"]),
                        ("lamT", sb["lamT"])):
            if nm in dbg_t:
                sch.wait(SY, "act"); sch.wait(SY, "dve")
                dd = SY.dma_start(out=dbg_t[nm], in_=src[:])
                sch.dinc(dd, "dbg")


    def phase3(self, stB):
        V, A_, T_, G_, SY = self.engines()
        nc = self.nc
        sch, cst, sb, ps, din = self.sch, self.cst, self.sb, self.ps, self.din
        scr, dbg_t = self.scr, self.dbg_t
        S = self.S
        P = self.P
        S("wq3", [128, 16 * 512], BF16, stack=stB)
        S("wk3", [128, 16 * 1024], BF16, stack=stB)
        S("wv3", [128, 16 * 1024], BF16, stack=stB)
        S("dq_sb", [128, 4 * 4 * 128], BF16, stack=stB)
        S("dk_sb", [128, 8 * 4 * 128], BF16, stack=stB)
        S("dv_sb", [128, 8 * 4 * 128], BF16, stack=stB)
        S("xn3", [128, 16 * SBT], BF16, stack=stB)
        S("qf", [128, 4 * SBT], BF16, stack=stB)
        S("kf", [128, 4 * 2 * SBT], BF16, stack=stB)
        S("vf", [128, 4 * 2 * SBT], BF16, stack=stB)
        S("ring", [128, SBT + 4], BF16, stack=stB)
        S("tails", [128, 60], BF16, stack=stB)
        S("rn_S", [128, 128], stack=stB)
        S("Za", [128, 1024], BF16, stack=stB)
        S("Zb", [128, 1024], BF16, stack=stB)
        S("PAa", [128, 512], BF16, stack=stB)
        S("PAb", [128, 512], BF16, stack=stB)
        S("PNa", [128, 512], BF16, stack=stB)
        S("PNb", [128, 512], BF16, stack=stB)
        S("nA4", [128, 512], BF16, stack=stB)
        S("KH4", [128, 512], BF16, stack=stB)
        S("AT4", [128, 256], BF16, stack=stB)
        S("QB4", [128, 256], BF16, stack=stB)
        S("nTW4", [128, 512], BF16, stack=stB)
        S("U4", [128, 512], BF16, stack=stB)
        S("S4", [128, 512], stack=stB)
        S("Sb4", [128, 512], BF16, stack=stB)
        S("sc0", [128, 512], stack=stB)
        S("sc1", [128, 256], stack=stB)
        S("sqk", [128, 128], stack=stB)
        S("osb", [64, 512], stack=stB)
        S("osq", [64, 128], stack=stB)
        S("ocol", [64, 12], stack=stB)
        S("ogbf", [64, 512], BF16, stack=stB)
        S("ggs", [64, 512], BF16, stack=stB)
        S("ogst", [128, 256], BF16, stack=stB)
        P("pz", [128, 1024], stack=stB)
        P("pq1", [128, 512], stack=stB)
        P("pq2", [128, 512], stack=stB)
        P("pq3", [128, 512], stack=stB)
        P("ptAB", [128, 1024], BF16, stack=stB)
        P("ptCD", [128, 1024], BF16, stack=stB)
        P("pO", [64, 512], stack=stB)

        iden = sb["iden_f_sb"]

        lands = [sb["wld0"], sb["wld1"]]
        widx = [self.widx]
        def wload(name, dst, ncol, parts):
            for kc in range(16):
                for p_ in range(parts):
                    land = lands[widx[0] % 2]
                    if widx[0] >= 2:
                        SY.wait_ge(sch.s("wcast"), widx[0] - 1)
                    cw = ncol // parts
                    d = SY.dma_start(
                        out=land[:, 0:cw],
                        in_=din[name][kc * 128:(kc + 1) * 128,
                                      p_ * cw:(p_ + 1) * cw])
                    sem = f"dwm{widx[0] % 2}"
                    sch.dinc(d, sem)
                    V.wait_ge(sch.s(sem), sch.now(sem))
                    cc = V.tensor_copy(
                        sb[dst][:, kc * ncol + p_ * cw:kc * ncol + (p_ + 1) * cw],
                        land[:, 0:cw])
                    sch.inc(cc, "wcast")
                    widx[0] += 1
        wload("wqT", "wq3", 512, 1)
        wload("wkT", "wk3", 1024, 1)
        wload("wvT", "wv3", 1024, 1)
        for nm, dst in (("dq", "dq_sb"), ("dk", "dk_sb"), ("dv", "dv_sb")):
            d = SY.dma_start(out=sb[dst][:], in_=din[nm])
            sch.dinc(d, "dsm")

        z1 = V.memset(sb["tails"][:], 0.0)
        z2 = V.memset(sb["S4"][:], 0.0)
        z3 = V.memset(sb["ocol"][:], 0.0)
        V.memset(sb["L_S"][:], 0.0)
        z4 = V.memset(sb["rn_S"][:], 0.0)
        sch.inc(z4, "dve")
        sch.wait(T_, "wcast"); sch.wait(T_, "dsm"); sch.wait(T_, "dve")
        sch.wait(G_, "dve")

        kf3 = sb["kf"]
        vf3 = sb["vf"]
        qf3 = sb["qf"][:].rearrange("p (j t) -> p j t", j=4)

        for sbk in range(NSB):
            self._sb_proj(sbk)
            for ci in range(NCI):
                self._chunk_iter(sbk, ci, kf3, vf3, qf3)
                self.stat_emit(9)
            if "qf" in dbg_t:
                sch.wait(SY, "act")
                dd = SY.dma_start(out=dbg_t["qf"][:, sbk * 2048:(sbk + 1) * 2048],
                                  in_=sb["qf"][:])
                sch.dinc(dd, "dbg")
                dd = SY.dma_start(out=dbg_t["kf"][:, sbk * 4096:(sbk + 1) * 4096],
                                  in_=sb["kf"][:])
                sch.dinc(dd, "dbg")
                dd = SY.dma_start(out=dbg_t["vf"][:, sbk * 4096:(sbk + 1) * 4096],
                                  in_=sb["vf"][:])
                sch.dinc(dd, "dbg")
                SY.wait_ge(sch.s("dbg"), sch.now("dbg"))
        for nm, src in (("LS", sb["L_S"]),
                        ("S4", sb["S4"])):
            if nm in dbg_t:
                sch.wait(SY, "act"); sch.wait(SY, "dve"); sch.wait(SY, "pe")
                dd = SY.dma_start(out=dbg_t[nm], in_=src[:])
                sch.dinc(dd, "dbg")

    def _sb_proj(self, sbk):
        V, A_, T_, G_, SY = self.engines()
        sch, sb, ps, din = self.sch, self.sb, self.ps, self.din
        scr = self.scr
        SY.wait_ge(sch.s("pe"), sch.now("pe"))
        d = SY.dma_start(
            out=sb["xn3"][:].rearrange("p (f t) -> p f t", f=16),
            in_=scr["xn"][:, sbk * SBT:(sbk + 1) * SBT].rearrange(
                "(f p) t -> p f t", p=128))
        sch.dinc(d, "dx3")
        T_.wait_ge(sch.s("dx3"), sch.now("dx3"))
        jobs = []
        for j in range(4):
            jobs.append(("wq3", 512, j, "dq_sb", 4, sb["qf"], j * SBT, j))
        for mc in range(8):
            n, j = mc // 4, mc % 4
            jobs.append(("wk3", 1024, mc, "dk_sb", 8, ("kf", j, n), None, mc))
        for mc in range(8):
            n, j = mc // 4, mc % 4
            jobs.append(("wv3", 1024, mc, "dv_sb", 8, ("vf", j, n), None, mc))
        tails_off = 0
        for jb_i, (wname, wcols, mc, dname, nch, dst, doff, tidx) in \
                enumerate(jobs):
            tail_ix = jb_i * 3
            pg = (ps["pq1"] if jb_i % 2 == 0 else ps["pq2"])
            pc = (ps["pq3"][:, 0:SBT] if jb_i % 2 == 0
                  else ps["pz"][:, 0:SBT])
            T_.wait_ge(sch.s("act"), sch.now("act"))
            for kc in range(16):
                mm = T_.matmul(
                    pg[:, 0:SBT] if pg.shape[1] > SBT else pg[:, :],
                    sb[wname][:, kc * wcols + mc * 128:
                              kc * wcols + (mc + 1) * 128],
                    sb["xn3"][:, kc * SBT:(kc + 1) * SBT],
                    start=(kc == 0), stop=(kc == 15))
            sch.inc(mm, "pe")
            A_.wait_ge(sch.s("pe"), sch.now("pe"))
            if sch.now("pe2") > 0:
                A_.wait_ge(sch.s("pe2"), sch.now("pe2"))
            A_.wait_ge(sch.s("dve"), sch.now("dve"))
            ev = A_.copy(sb["ring"][:, 3:3 + SBT], pg[:, 0:SBT])
            sch.inc(ev, "act")
            V.wait_ge(sch.s("act"), sch.now("act"))
            hv = V.tensor_copy(sb["ring"][:, 0:3],
                               sb["tails"][:, tail_ix:tail_ix + 3])
            sch.inc(hv, "dve")
            T_.wait_ge(sch.s("dve"), sch.now("dve"))
            T_.wait_ge(sch.s("act"), sch.now("act"))
            for tap in range(4):
                mm2 = T_.matmul(
                    pc,
                    sb[dname][:, (mc * 4 + tap) * 128:(mc * 4 + tap + 1) * 128],
                    sb["ring"][:, tap:tap + SBT],
                    start=(tap == 0), stop=(tap == 3))
            sch.inc(mm2, "pe2")
            V.wait_ge(sch.s("pe2"), sch.now("pe2"))
            tl = V.tensor_copy(sb["tails"][:, tail_ix:tail_ix + 3],
                               sb["ring"][:, SBT:SBT + 3])
            sch.inc(tl, "dve")
            A_.wait_ge(sch.s("pe2"), sch.now("pe2"))
            if doff is None:
                tname, jj, nn = dst
                dap = sb[tname][:, jj * 2 * SBT:(jj + 1) * 2 * SBT].rearrange(
                    "p (t n) -> p n t", n=2)[:, nn, :]
            else:
                dap = dst[:, doff:doff + SBT]
            sl = A_.activation(dap, pc, AF.Silu)
            sch.inc(sl, "act")

    def _chunk_iter(self, sbk, ci, kf3, vf3, qf3):
        V, A_, T_, G_, SY = self.engines()
        sch, cst, sb, ps = self.sch, self.cst, self.sb, self.ps
        scr = self.scr
        i_glob = sbk * NCI + ci
        t0 = ci * TAU
        iden = sb["iden_f_sb"]
        idb = sb["iden_b_sb"]
        idh = sb["iden_h_sb"]

        insts = [j * 32 + i_glob for j in range(4)]

        def kch(j):
            return kf3[:, j * 2 * SBT + 2 * t0:j * 2 * SBT + 2 * t0 + 128]

        def vch(j):
            return vf3[:, j * 2 * SBT + 2 * t0:j * 2 * SBT + 2 * t0 + 128]

        def qch(j):
            return qf3[:, j, t0:t0 + TAU]

        sch.wait(T_, "act")
        sch.wait(T_, "dve")
        for j in range(4):
            T_.transpose(ps["ptAB"][:, j * 128:(j + 1) * 128], kch(j), idb[:, :])
            T_.transpose(ps["ptAB"][:, 512 + j * 128:512 + (j + 1) * 128], vch(j), idb[:, :])
            mm = T_.transpose(ps["ptCD"][0:64, j * 128:(j + 1) * 128], qch(j), idb[:, :])
        sch.inc(mm, "pe")
        sch.wait(A_, "pe")
        for j in range(4):
            a = A_.activation(sb["sqk"][:], ps["ptAB"][:, j * 128:(j + 1) * 128],
                              AF.Square,
                              accum_out=sb["L_S"][:, insts[j]:insts[j] + 1])
            a2 = A_.activation(sb["osq"][:],
                               ps["ptCD"][0:64, j * 128:(j + 1) * 128],
                               AF.Square, scale=0.03577708764,
                               accum_out=sb["ocol"][0:64, 4 + j:5 + j])
        sch.inc(a2, "act")
        cs = sb["L_S"][:].rearrange("p (a i) -> p a i", i=32)[:, :, i_glob]
        rn = sb["rn_S"][:].rearrange("p (a i) -> p a i", i=32)[:, :, i_glob]
        gp = sb["GpS"][:].rearrange("p (a i) -> p a i", i=32)[:, :, i_glob]
        gs = sb["G_S"][:].rearrange("p (a i) -> p a i", i=32)[:, :, i_glob]
        nb = sb["nbrn_S"][:].rearrange("p (a i) -> p a i", i=32)[:, :, i_glob]
        bs = sb["beta_S"][:].rearrange("p (a i) -> p a i", i=32)[:, :, i_glob]
        er = sb["ehr_S"][:].rearrange("p (a i) -> p a i", i=32)[:, :, i_glob]
        es = sb["ehat_S"][:].rearrange("p (a i) -> p a i", i=32)[:, :, i_glob]
        br = sb["blr_S"][:].rearrange("p (a i) -> p a i", i=32)[:, :, i_glob]
        bl = sb["bl_S"][:].rearrange("p (a i) -> p a i", i=32)[:, :, i_glob]
        sch.wait(V, "act")
        u = V.tensor_single_scalar(cs, cs, 1e-24, ALU.max)
        sch.inc(u, "dve")
        sch.wait(A_, "dve")
        ln = A_.activation(cs, cs, AF.Ln)
        rne = A_.activation(rn, cs, AF.Exp, scale=-0.5)
        sch.inc(rne, "act")
        sch.wait(V, "act")
        V.scalar_tensor_tensor(nb, bs, -1.0, rn, ALU.mult, ALU.mult)
        V.tensor_tensor(er, es, rn, ALU.mult)
        lv = V.tensor_tensor(br, bl, rn, ALU.mult)
        sch.inc(lv, "dve")
        sch.wait(T_, "dve"); sch.wait(T_, "act")
        tl = T_.transpose(ps["pq3"][:, 0:128], sb["rn_S"][:], iden[:, :])
        sch.inc(tl, "pe")
        sch.wait(A_, "pe")
        le = A_.copy(sb["LT_all"][:], ps["pq3"][:, 0:128])
        sch.inc(le, "act")

        sch.wait(T_, "act")
        for j in range(4):
            T_.matmul(ps["pq1"][:, j * 128:(j + 1) * 128], kch(j), kch(j),
                      start=True, stop=True)
            T_.matmul(ps["pq2"][:, j * 128:(j + 1) * 128],
                      iden[:, insts[j]:insts[j] + 1].broadcast_to((128, 128)),
                      sb["GT"][:], start=True, stop=True)
            mm = T_.matmul(ps["pz"][:, 512 + j * 128:512 + (j + 1) * 128],
                           iden[:, insts[j]:insts[j] + 1].broadcast_to(
                               (128, 128)),
                           sb["LT_all"][:], start=True, stop=True)
        sch.inc(mm, "pe")
        sch.wait(V, "pe")
        for j in range(4):
            st1 = V.scalar_tensor_tensor(
                sb["sc0"][:, j * 128:(j + 1) * 128],
                ps["pq2"][:, j * 128:(j + 1) * 128],
                sb["G_S"][:, insts[j]:insts[j] + 1],
                sb["maskA_sb"][:], ALU.subtract, ALU.max)
        sch.inc(st1, "dve")
        sch.wait(A_, "dve")
        ae = A_.activation(sb["sc0"][:], sb["sc0"][:], AF.Exp, scale=-1.0)
        sch.inc(ae, "act")
        sch.wait(V, "act")
        for j in range(4):
            st2 = V.tensor_tensor(
                sb["sc0"][:, j * 128:(j + 1) * 128],
                sb["sc0"][:, j * 128:(j + 1) * 128],
                ps["pz"][:, 512 + j * 128:512 + (j + 1) * 128], ALU.mult)
        V.wait_ge(sch.s("dve"), sch.now("dve"))
        for j in range(4):
            st2 = V.scalar_tensor_tensor(
                sb["nA4"][:, j * 128:(j + 1) * 128],
                sb["sc0"][:, j * 128:(j + 1) * 128],
                sb["nbrn_S"][:, insts[j]:insts[j] + 1],
                ps["pq1"][:, j * 128:(j + 1) * 128], ALU.mult, ALU.mult)
        sch.inc(st2, "dve")

        sch.wait(T_, "dve")
        lam_odd = sb["lamT"][:].rearrange("p (t n) -> p t n", n=2)[:, :, 1]
        gt_odd = sb["GT"][:].rearrange("p (t n) -> p t n", n=2)[:, :, 1]
        for j in range(4):
            T_.matmul(ps["pq3"][:, j * 64:(j + 1) * 64], kch(j), qch(j),
                      start=True, stop=True)
            T_.matmul(ps["pq3"][:, 256 + j * 64:256 + (j + 1) * 64],
                      iden[:, insts[j]:insts[j] + 1].broadcast_to((128, 128)),
                      gt_odd, start=True, stop=True)
            mm = T_.matmul(ps["pz"][:, j * 64:(j + 1) * 64],
                           iden[:, insts[j]:insts[j] + 1].broadcast_to(
                               (128, 128)),
                           lam_odd, start=True, stop=True)
        sch.inc(mm, "pe")
        sch.wait(V, "pe")
        for j in range(4):
            st3 = V.scalar_tensor_tensor(
                sb["sc1"][:, j * 64:(j + 1) * 64],
                ps["pq3"][:, 256 + j * 64:256 + (j + 1) * 64],
                sb["G_S"][:, insts[j]:insts[j] + 1],
                sb["maskQ_sb"][:], ALU.subtract, ALU.min)
        sch.inc(st3, "dve")
        sch.wait(A_, "dve")
        aq = A_.activation(sb["sc1"][:], sb["sc1"][:], AF.Exp)
        sch.inc(aq, "act")
        sch.wait(V, "act")
        for j in range(4):
            tt1 = V.scalar_tensor_tensor(
                sb["AT4"][:, j * 64:(j + 1) * 64],
                sb["sc1"][:, j * 64:(j + 1) * 64],
                sb["rn_S"][:, insts[j]:insts[j] + 1],
                ps["pq3"][:, j * 64:(j + 1) * 64], ALU.mult, ALU.mult)
            tt2 = V.tensor_tensor(sb["QB4"][:, j * 64:(j + 1) * 64],
                                  qch(j),
                                  ps["pz"][:, j * 64:(j + 1) * 64],
                                  ALU.mult)
        sch.inc(tt2, "dve")

        sch.wait(A_, "dve")
        for j in range(4):
            A_.activation(sb["Za"][:, j * 256:j * 256 + 128],
                          ps["ptAB"][:, 512 + j * 128:512 + (j + 1) * 128], AF.Copy,
                          scale=sb["beta_S"][:, insts[j]:insts[j] + 1])
            A_.activation(sb["Za"][:, j * 256 + 128:(j + 1) * 256],
                          ps["ptAB"][:, j * 128:(j + 1) * 128], AF.Copy,
                          scale=sb["blr_S"][:, insts[j]:insts[j] + 1])
            kh = A_.activation(sb["KH4"][:, j * 128:(j + 1) * 128],
                               ps["ptAB"][:, j * 128:(j + 1) * 128], AF.Copy,
                               scale=sb["ehr_S"][:, insts[j]:insts[j] + 1])
        sch.inc(kh, "act")

        sch.wait(T_, "dve"); sch.wait(T_, "act")
        for j in range(4):
            tr = T_.transpose(ps["ptCD"][:, 512 + j * 128:512 + (j + 1) * 128],
                              sb["nA4"][:, j * 128:(j + 1) * 128], idb[:, :])
        sch.inc(tr, "pe")
        sch.wait(A_, "pe")
        pn1 = A_.copy(sb["PNa"][:], ps["ptCD"][:, 512:1024])
        sch.inc(pn1, "act")

        PA = [sb["nA4"], sb["PAa"], sb["PAb"]]
        PN = [sb["PNa"], sb["PNb"]]
        Z = [sb["Za"], sb["Zb"]]
        zi = 0
        pa_cur, pn_cur = 0, 0
        for step, k in enumerate([1, 2, 4, 8, 16, 32, 64]):
            need_pa = (k < 32)
            sch.wait(T_, "act"); sch.wait(T_, "dve")
            for j in range(4):
                mm = T_.matmul(ps["pz"][:, j * 256:(j + 1) * 256],
                               PN[pn_cur][:, j * 128:(j + 1) * 128],
                               Z[zi][:, j * 256:(j + 1) * 256],
                               start=True, stop=True)
            sch.inc(mm, "pe")
            if k < 64:
                pa_src = PA[0] if step == 0 else PA[1 + (pa_cur % 2)]
                pn_src = PN[pn_cur]
                if need_pa:
                    for j in range(4):
                        mm = T_.matmul(ps["pq1"][:, j * 128:(j + 1) * 128],
                                       pn_src[:, j * 128:(j + 1) * 128],
                                       pa_src[:, j * 128:(j + 1) * 128],
                                       start=True, stop=True)
                    sch.inc(mm, "pe")
                for j in range(4):
                    mm = T_.matmul(ps["pq2"][:, j * 128:(j + 1) * 128],
                                   pa_src[:, j * 128:(j + 1) * 128],
                                   pn_src[:, j * 128:(j + 1) * 128],
                                   start=True, stop=True)
                sch.inc(mm, "pe")
            sch.wait(V, "pe")
            for j in range(4):
                zt = V.tensor_tensor(
                    Z[1 - zi][:, j * 256:(j + 1) * 256],
                    ps["pz"][:, j * 256:(j + 1) * 256],
                    Z[zi][:, j * 256:(j + 1) * 256], ALU.add)
            sch.inc(zt, "dve")
            if k < 64:
                sch.wait(A_, "pe")
                if need_pa:
                    pa_nxt = PA[1 + ((pa_cur + 1) % 2)]
                    pe1 = A_.copy(pa_nxt[:], ps["pq1"][:])
                    sch.inc(pe1, "act")
                    pa_cur += 1
                pn_nxt = PN[1 - pn_cur]
                pe2 = A_.copy(pn_nxt[:], ps["pq2"][:])
                sch.inc(pe2, "act")
                pn_cur = 1 - pn_cur
            zi = 1 - zi
        Zf = Z[zi]

        sch.wait(T_, "dve")
        for j in range(4):
            tw = T_.transpose(ps["ptCD"][:, 512 + j * 128:512 + (j + 1) * 128],
                              Zf[:, j * 256 + 128:(j + 1) * 256], idb[:, :])
        sch.inc(tw, "pe")
        sch.wait(A_, "pe")
        ng = A_.activation(sb["nTW4"][:], ps["ptCD"][:, 512:1024], AF.Copy,
                           scale=cst.ap(-1.0))
        sch.inc(ng, "act")

        sch.wait(V, "pe"); sch.wait(V, "act")
        cb = V.tensor_copy(sb["Sb4"][:], sb["S4"][:])
        sch.inc(cb, "dve")
        sch.wait(T_, "act"); sch.wait(T_, "dve")
        for j in range(4):
            mm = T_.matmul(ps["pq1"][:, j * 128:(j + 1) * 128],
                           sb["nTW4"][:, j * 128:(j + 1) * 128],
                           sb["Sb4"][:, j * 128:(j + 1) * 128],
                           start=True, stop=True)
        sch.inc(mm, "pe")
        sch.wait(V, "pe")
        uo = V.tensor_tensor(sb["U4"][:].rearrange("p (j d) -> p j d", d=128),
                             ps["pq1"][:].rearrange("p (j d) -> p j d", d=128),
                             Zf[:].rearrange("p (j c) -> p j c",
                                             c=256)[:, :, 0:128],
                             ALU.add)
        sch.inc(uo, "dve")
        sch.wait(T_, "dve")
        for j in range(4):
            mm = T_.matmul(ps["pq2"][:, j * 128:(j + 1) * 128],
                           sb["KH4"][:, j * 128:(j + 1) * 128],
                           sb["U4"][:, j * 128:(j + 1) * 128],
                           start=True, stop=True)
            T_.matmul(ps["pO"][:, j * 128:(j + 1) * 128],
                      sb["QB4"][:, j * 64:(j + 1) * 64],
                      sb["Sb4"][:, j * 128:(j + 1) * 128],
                      start=True, stop=False)
            mo = T_.matmul(ps["pO"][:, j * 128:(j + 1) * 128],
                           sb["AT4"][:, j * 64:(j + 1) * 64],
                           sb["U4"][:, j * 128:(j + 1) * 128],
                           start=False, stop=True)
        sch.inc(mo, "pe")
        sch.wait(V, "pe")
        for j in range(4):
            su = V.scalar_tensor_tensor(
                sb["S4"][:, j * 128:(j + 1) * 128],
                sb["S4"][:, j * 128:(j + 1) * 128],
                sb["lamC"][:, insts[j]:insts[j] + 1],
                ps["pq2"][:, j * 128:(j + 1) * 128], ALU.mult, ALU.add)
        sch.inc(su, "dve")
        sch.wait(A_, "pe")
        for j in range(4):
            A_.copy(sb["osb"][:, j * 128:(j + 1) * 128],
                    ps["pO"][:, j * 128:(j + 1) * 128])
            oa = A_.activation(sb["osq"][:], ps["pO"][:, j * 128:(j + 1) * 128],
                               AF.Square,
                               accum_out=sb["ocol"][0:64, j:j + 1])
        sch.inc(oa, "act")
        sch.wait(V, "act")
        rv = V.scalar_tensor_tensor(sb["ocol"][0:64, 8:12],
                                    sb["ocol"][0:64, 0:4], 1.0 / 128.0,
                                    sb["ocol"][0:64, 4:8],
                                    ALU.mult, ALU.add)
        sch.inc(rv, "dve")
        sch.wait(A_, "dve")
        rsq = A_.activation(sb["ocol"][0:64, 8:12], sb["ocol"][0:64, 8:12],
                            AF.Sqrt)
        sch.inc(rsq, "act")
        sch.wait(V, "act")
        rrec = V.reciprocal(sb["ocol"][0:64, 8:12], sb["ocol"][0:64, 8:12])
        sch.inc(rrec, "dve")
        V.wait_ge(sch.s("dve"), sch.now("dve"))
        tok0 = (sbk * SBT + ci * TAU)
        gbase = (tok0 % 128)
        gtile = tok0 // 128
        gcp = V.tensor_copy(
            sb["ggs"][:],
            sb["silu_gg"][gbase:gbase + 64, gtile * 512:(gtile + 1) * 512])
        sch.inc(gcp, "dve")
        V.wait_ge(sch.s("dve"), sch.now("dve"))
        for j in range(4):
            og = V.scalar_tensor_tensor(
                sb["ogbf"][:, j * 128:(j + 1) * 128],
                sb["osb"][:, j * 128:(j + 1) * 128],
                sb["ocol"][0:64, 8 + j:9 + j],
                sb["ggs"][:, j * 128:(j + 1) * 128], ALU.mult, ALU.mult)
        sch.inc(og, "dve")
        sch.wait(T_, "dve")
        for j in range(4):
            to = T_.transpose(ps["ptCD"][:, 512 + j * 128:512 + j * 128 + 64],
                              sb["ogbf"][:, j * 128:(j + 1) * 128],
                              idb[0:64, 0:64])
        sch.inc(to, "pe")
        sch.wait(A_, "pe")
        A_.wait_ge(sch.s("dog"), sch.now("dog"))
        for j in range(4):
            oe = A_.copy(sb["ogst"][:, j * 64:(j + 1) * 64],
                         ps["ptCD"][:, 512 + j * 128:512 + j * 128 + 64])
        sch.inc(oe, "act")
        SY.wait_ge(sch.s("act"), sch.now("act"))
        dg = SY.dma_start(
            out=scr["ogi"].ap().rearrange("(j p) t -> p j t",
                                          p=128)[:, :, tok0:tok0 + 64],
            in_=sb["ogst"][:].rearrange("p (j t) -> p j t", t=64))
        sch.dinc(dg, "dog")
        if sbk == 0 and ci == 0:
            dbg_t = self.dbg_t
            for nm, src in (("nA4", sb["nA4"][:]), ("AT4", sb["AT4"][:]),
                            ("QB4", sb["QB4"][:]), ("KH4", sb["KH4"][:]),
                            ("U4d", sb["U4"][:]), ("S4d", sb["S4"][:]),
                            ("Zf0", Zf[:]), ("PN2d", sb["PNb"][:])):
                if nm in dbg_t:
                    sch.wait(SY, "act"); sch.wait(SY, "dve")
                    dd = SY.dma_start(out=dbg_t[nm], in_=src)
                    sch.dinc(dd, "dbg")
            if sch.now("dbg"):
                SY.wait_ge(sch.s("dbg"), sch.now("dbg"))


    def stat_prep(self):
        S = self.S
        S("fabs", [128, 1024])
        S("fscols", [128, 288])
        S("macol", [128, 8])
        z = self.nc.vector.memset(self.sb["fscols"][:], 0.0)
        self.sch.inc(z, "dve")
        self.stat_jobs = []
        ci = 0
        for kc in range(16):
            for p_ in range(11):
                self.stat_jobs.append(("wguT", kc * 128, p_ * 1024, 1024, ci))
                ci += 1
        for kc in range(44):
            for p_ in range(2):
                self.stat_jobs.append(("wdT", kc * 128, p_ * 1024, 1024, ci))
                ci += 1
        self.stat_pos = 0
        self.n_gu_jobs = 176

    def stat_emit(self, count):
        V, A_, T_, G_, SY = self.engines()
        sch, sb, din = self.sch, self.sb, self.din
        lands = [sb["wld0"], sb["wld1"]]
        for _ in range(count):
            if self.stat_pos >= len(self.stat_jobs):
                return
            name, r0, c0, w, slot = self.stat_jobs[self.stat_pos]
            land = lands[self.stat_pos % 2]
            if self.stat_pos < 2:
                SY.wait_ge(sch.s("wcast"), sch.now("wcast"))
            else:
                SY.wait_ge(sch.s("fs_done"), self.stat_pos - 1)
            d = SY.dma_start(out=land[:, 0:w],
                             in_=din[name][r0:r0 + 128, c0:c0 + w])
            dsem = f"dfs{self.stat_pos % 2}"
            sch.dinc(d, dsem)
            if self.stat_pos % 2 == 0:
                A_.wait_ge(sch.s(dsem), sch.now(dsem))
                a = A_.activation(sb["fabs"][:, 0:w], land[:, 0:w], AF.Abs,
                                  accum_out=sb["fscols"][:, slot:slot + 1])
                sch.inc(a, "fs_done")
            else:
                V.wait_ge(sch.s(dsem), sch.now(dsem))
                g2 = V.scalar_tensor_tensor(
                    sb["fabs"][:, 0:w], land[:, 0:w], -1.0, land[:, 0:w],
                    ALU.mult, ALU.max,
                    accum_out=sb["fscols"][:, slot:slot + 1])
                sch.inc(g2, "fs_done")
            self.stat_pos += 1

    def stat_final(self):
        V, A_, T_, G_, SY = self.engines()
        sch, cst, sb, ps = self.sch, self.cst, self.sb, self.ps
        sch.wait(A_, "fs_done"); sch.wait(A_, "dve"); sch.wait(A_, "pool")
        st = A_.copy(sb["fabs"][:, 0:264], sb["fscols"][:, 0:264])
        sch.inc(st, "act")
        sch.wait(V, "act")
        r1 = V.tensor_reduce(sb["macol"][:, 6:7], sb["fabs"][:, 0:176],
                             mybir.AxisListType.X, ALU.add)
        r2 = V.tensor_reduce(sb["macol"][:, 7:8], sb["fabs"][:, 176:264],
                             mybir.AxisListType.X, ALU.add)
        sch.inc(r2, "dve")
        sch.wait(T_, "dve")
        t1 = T_.matmul(ps["pw0"][0:1, 0:2], sb["ones128_sb"][:],
                       sb["macol"][:, 6:8], start=True, stop=True)
        sch.inc(t1, "pe")
        sch.wait(A_, "pe")
        ev = A_.copy(sb["macol"][0:1, 4:6], ps["pw0"][0:1, 0:2])
        sch.inc(ev, "act")
        sch.wait(T_, "act")
        t2 = T_.matmul(ps["pw0"][:, 0:2], sb["ones1_sb"][:],
                       sb["macol"][0:1, 4:6], start=True, stop=True)
        sch.inc(t2, "pe")
        sch.wait(V, "pe")
        c1 = V.tensor_scalar(sb["macol"][:, 0:2], ps["pw0"][:, 0:2],
                             1.0 / 23068672.0, 1e-5, ALU.mult, ALU.max)
        c2 = V.tensor_scalar(sb["macol"][:, 1:2], ps["pw0"][:, 1:2],
                             1.0 / 11534336.0, 1e-5, ALU.mult, ALU.max)
        sch.inc(c2, "dve")
        A_.wait_ge(sch.s("dve"), sch.now("dve"))
        A_.activation(sb["macol"][:, 2:3], sb["macol"][:, 0:1], AF.Copy,
                      scale=0.5)
        A_.activation(sb["macol"][:, 3:4], sb["macol"][:, 0:1], AF.Copy,
                      scale=-0.5)
        A_.activation(sb["macol"][:, 4:5], sb["macol"][:, 1:2], AF.Copy,
                      scale=0.5)
        c6 = A_.activation(sb["macol"][:, 5:6], sb["macol"][:, 1:2], AF.Copy,
                           scale=-0.5)
        sch.inc(c6, "macol_r")

    def phase4(self, stC):
        V, A_, T_, G_, SY = self.engines()
        nc = self.nc
        sch, cst, sb, ps, din = self.sch, self.cst, self.sb, self.ps, self.din
        scr = self.scr
        S, P = self.S, self.P
        S("wo_b", [128, 16384], BF16, stack=stC)
        S("worhs", [128, 8192], BF16, stack=stC)
        S("bl0", [128, 512], BF16, stack=stC)
        S("bl1", [128, 512], BF16, stack=stC)
        S("sevac", [128, 512], BF16, stack=stC)
        P("pw0", [128, 512], stack=stC); P("pw1", [128, 512], stack=stC)
        self.stat_final()

        G_.wait_ge(sch.s("dog"), sch.now("dog"))
        cc = G_.collective_compute(
            "AllGather", ALU.bypass,
            replica_groups=[[0, 1], [2, 3], [4, 5], [6, 7]],
            ins=[scr["ogi"].ap().opt()],
            outs=[scr["ogo"].ap().opt()])
        cc.then_inc(sch.s("ccs"))
        sch.cnt["ccs"] += 1
        lands = [sb["wld0"], sb["wld1"]]
        for kc in range(8):
            for p_ in range(2):
                li = (kc * 2 + p_)
                land = lands[li % 2]
                if li >= 2:
                    SY.wait_ge(sch.s("wocast"), li - 1)
                wsem = f"dwo{li % 2}"
                dd = SY.dma_start(
                    out=land[:, 0:1024],
                    in_=din["woT"][kc * 128:(kc + 1) * 128,
                                   p_ * 1024:(p_ + 1) * 1024])
                sch.dinc(dd, wsem)
                V.wait_ge(sch.s(wsem), sch.now(wsem))
                cv = V.tensor_copy(
                    sb["wo_b"][:, kc * 2048 + p_ * 1024:
                               kc * 2048 + (p_ + 1) * 1024],
                    land[:, 0:1024])
                sch.inc(cv, "wocast")
        SY.wait_ge(sch.s("ccs"), sch.now("ccs"))
        for kc in range(8):
            for tokt in range(2):
                dd = SY.dma_start(
                    out=sb["bl0"][:],
                    in_=scr["ogo"].ap()[kc * 128:(kc + 1) * 128,
                                        tokt * 512:(tokt + 1) * 512])
                sch.dinc(dd, "dgo")
                dd = SY.dma_start(
                    out=sb["bl1"][:],
                    in_=scr["ogo"].ap()[kc * 128:(kc + 1) * 128,
                                        1024 + tokt * 512:
                                        1024 + (tokt + 1) * 512])
                sch.dinc(dd, "dgo")
                G_.wait_ge(sch.s("dgo"), sch.now("dgo"))
                gb = G_.tensor_tensor(
                    sb["bl0"][:], sb["bl0"][:],
                    sb["maskme_sb"][:].broadcast_to((128, 512)), ALU.mult)
                sch.inc(gb, "pool")
                V.wait_ge(sch.s("pool"), sch.now("pool"))
                V.wait_ge(sch.s("dgo"), sch.now("dgo"))
                bv = V.scalar_tensor_tensor(
                    sb["worhs"][:, kc * 1024 + tokt * 512:
                                kc * 1024 + (tokt + 1) * 512],
                    sb["bl1"][:], sb["maskpa_sb"][:], sb["bl0"][:],
                    ALU.mult, ALU.add)
                sch.inc(bv, "dve")
                SY.wait_ge(sch.s("dve"), sch.now("dve"))
        sch.wait(T_, "dve"); sch.wait(T_, "wocast")
        for m in range(16):
            for tokt in range(2):
                pw = ps["pw0"] if (m * 2 + tokt) % 2 == 0 else ps["pw1"]
                if m * 2 + tokt >= 2:
                    T_.wait_ge(sch.s("act"), sch.now("act"))
                for kc in range(8):
                    mm = T_.matmul(
                        pw[:],
                        sb["wo_b"][:, kc * 2048 + m * 128:
                                   kc * 2048 + (m + 1) * 128],
                        sb["worhs"][:, kc * 1024 + tokt * 512:
                                    kc * 1024 + (tokt + 1) * 512],
                        start=(kc == 0), stop=(kc == 7))
                sch.inc(mm, "pe")
                A_.wait_ge(sch.s("pe"), sch.now("pe"))
                ev = A_.copy(sb["sevac"][:], pw[:])
                sch.inc(ev, "act")
                SY.wait_ge(sch.s("act"), sch.now("act"))
                dd = SY.dma_start(
                    out=scr["s"][m * 128:(m + 1) * 128,
                                 tokt * 512:(tokt + 1) * 512],
                    in_=sb["sevac"][:])
                sch.dinc(dd, "dso")
                A_.wait_ge(sch.s("dso"), sch.now("dso"))
        if "sT" in self.dbg_t:
            SY.wait_ge(sch.s("dso"), sch.now("dso"))
            dd = SY.dma_start(out=self.dbg_t["sT"], in_=scr["s"][:, :])
            sch.dinc(dd, "dbg")
            SY.wait_ge(sch.s("dbg"), sch.now("dbg"))

    def phase5(self, stD):
        V, A_, T_, G_, SY = self.engines()
        sch, cst, sb, ps, din = self.sch, self.cst, self.sb, self.ps, self.din
        scr, dbg_t = self.scr, self.dbg_t
        S, P = self.S, self.P
        S("xh", [128, 8192], stack=stD)
        S("nsb", [128, 8192], BF16, stack=stD)
        S("Mx", [128, 1024], stack=stD)

        S("rows5", [1, 3072], stack=stD)
        S("mubc", [128, 512], stack=stD)
        S("fpbc", [128, 512], stack=stD)
        S("sxbc", [128, 1024], stack=stD)
        S("sqs", [128, 512], stack=stD)
        R5 = sb["rows5"]
        def row5(i):
            return R5[0:1, i * 512:(i + 1) * 512]
        zm = V.memset(sb["Mx"][:], 0.0)
        sch.inc(zm, "dve")

        for tht in range(2):
            SY.wait_ge(sch.s("dve"), sch.now("dve"))
            SY.wait_ge(sch.s("pool"), sch.now("pool"))
            d = SY.dma_start(
                out=sb["xh"][:].rearrange("p (f t) -> p f t", f=16),
                in_=din["xhT"][:, tht * 512:(tht + 1) * 512].rearrange(
                    "(f p) t -> p f t", p=128))
            sch.dinc(d, "dx5")
            T_.wait_ge(sch.s("dx5"), sch.now("dx5"))
            A_.wait_ge(sch.s("dx5"), sch.now("dx5"))
            for fc in range(16):
                xc = sb["xh"][:, fc * 512:(fc + 1) * 512]
                tgt = sch.now("pe") - 1
                if tgt > 0:
                    A_.wait_ge(sch.s("pe"), tgt)
                q = A_.activation(sb["sqs"][:], xc, AF.Square)
                sch.inc(q, "act")
                T_.matmul(ps["pg0"][0:1, :], sb["ones128_sb"][:], xc,
                          start=(fc == 0), stop=(fc == 15))
                T_.wait_ge(sch.s("act"), sch.now("act"))
                mm = T_.matmul(ps["pg1"][0:1, :], sb["ones128_sb"][:],
                               sb["sqs"][:], start=(fc == 0), stop=(fc == 15))
                sch.inc(mm, "pe")
            A_.wait_ge(sch.s("pe"), sch.now("pe"))
            m1 = A_.activation(row5(0), ps["pg0"][0:1, :], AF.Copy,
                               scale=1.0 / 2048.0)
            m2 = A_.activation(row5(2), ps["pg1"][0:1, :], AF.Copy,
                               scale=1.0 / 2048.0)
            sch.inc(m2, "act")
            V.wait_ge(sch.s("act"), sch.now("act"))
            V.tensor_tensor(row5(4), row5(0), row5(0), ALU.mult)
            V.tensor_tensor(row5(4), row5(2), row5(4), ALU.subtract)
            v3 = V.tensor_single_scalar(row5(3), row5(4), EPS, ALU.add)
            sch.inc(v3, "dve")
            A_.wait_ge(sch.s("dve"), sch.now("dve"))
            s1 = A_.activation(row5(1), row5(3), AF.Sqrt)
            sch.inc(s1, "act")
            V.wait_ge(sch.s("act"), sch.now("act"))
            r1 = V.reciprocal(row5(1), row5(1))
            sch.inc(r1, "dve")
            V.wait_ge(sch.s("dve"), sch.now("dve"))
            V.tensor_tensor(row5(2), row5(4), row5(1), ALU.mult)
            V.tensor_tensor(row5(2), row5(2), row5(1), ALU.mult)
            v5 = V.tensor_single_scalar(row5(2), row5(2), EPS, ALU.add)
            sch.inc(v5, "dve")
            A_.wait_ge(sch.s("dve"), sch.now("dve"))
            s2 = A_.activation(row5(2), row5(2), AF.Sqrt)
            sch.inc(s2, "act")
            V.wait_ge(sch.s("act"), sch.now("act"))
            V.reciprocal(row5(2), row5(2))
            v6 = V.tensor_tensor(row5(2), row5(2), row5(1), ALU.mult)
            sch.inc(v6, "dve")
            T_.wait_ge(sch.s("dve"), sch.now("dve"))
            T_.matmul(ps["pg2"][:], sb["ones1_sb"][:], row5(0), start=True,
                      stop=True)
            bb = T_.matmul(ps["pg3"][:], sb["ones1_sb"][:], row5(2),
                           start=True, stop=True)
            sch.inc(bb, "pe")
            A_.wait_ge(sch.s("pe"), sch.now("pe"))
            A_.copy(sb["mubc"][:], ps["pg2"][:])
            c2 = A_.copy(sb["fpbc"][:], ps["pg3"][:])
            sch.inc(c2, "act")
            G_.wait_ge(sch.s("act"), sch.now("act"))
            for fc in range(16):
                xc = sb["xh"][:, fc * 512:(fc + 1) * 512]
                G_.tensor_tensor(xc, xc, sb["mubc"][:], ALU.subtract)
                G_.tensor_tensor(
                    xc, xc,
                    sb["nw_sb"][:, fc:fc + 1].broadcast_to((128, 512)),
                    ALU.mult)
                g2 = G_.tensor_tensor(xc, xc, sb["fpbc"][:], ALU.mult)
                sch.inc(g2, "pool")
                V.wait_ge(sch.s("pool"), sch.now("pool"))
                V.scalar_tensor_tensor(xc, xc, -1.0, xc, ALU.mult, ALU.max)
                mv = V.tensor_tensor(sb["Mx"][:, tht * 512:(tht + 1) * 512],
                                     xc,
                                     sb["Mx"][:, tht * 512:(tht + 1) * 512],
                                     ALU.max)
                sch.inc(mv, "dve")
            sch.wait(T_, "dve")
            for tr in range(4):
                mm = T_.transpose(ps["pg4"][:, tr * 128:(tr + 1) * 128],
                                  sb["Mx"][:, tht * 512 + tr * 128:
                                           tht * 512 + (tr + 1) * 128],
                                  sb["iden_f_sb"][:])
            sch.inc(mm, "pe")
            sch.wait(V, "pe")
            for tr in range(4):
                rv = V.tensor_reduce(sb["colx"][:, tht * 4 + tr:tht * 4 + tr + 1],
                                     ps["pg4"][:, tr * 128:(tr + 1) * 128],
                                     mybir.AxisListType.X, ALU.max)
            sch.inc(rv, "dve")
            V.wait_ge(sch.s("dve"), sch.now("dve"))
            cl = V.tensor_single_scalar(
                sb["colx"][:, 8 + tht * 4:8 + tht * 4 + 4],
                sb["colx"][:, tht * 4:tht * 4 + 4], 1e-5, ALU.max)
            sch.inc(cl, "dve")
            V.wait_ge(sch.s("dve"), sch.now("dve"))
            V.tensor_single_scalar(sb["colx"][:, 16 + tht * 4:16 + tht * 4 + 4],
                                   sb["colx"][:, 8 + tht * 4:8 + tht * 4 + 4],
                                   1.0 / 127.0, ALU.mult)
            rc = V.reciprocal(sb["colx"][:, 24 + tht * 4:24 + tht * 4 + 4],
                              sb["colx"][:, 8 + tht * 4:8 + tht * 4 + 4])
            sch.inc(rc, "dve")
            V.wait_ge(sch.s("dve"), sch.now("dve"))
            sx = V.tensor_single_scalar(
                sb["colx"][:, 24 + tht * 4:24 + tht * 4 + 4],
                sb["colx"][:, 24 + tht * 4:24 + tht * 4 + 4],
                127.0, ALU.mult)
            sch.inc(sx, "dve")
            sch.wait(T_, "dve")
            for tr in range(4):
                mm = T_.matmul(
                    ps["pg5"][:, tr * 128:(tr + 1) * 128],
                    sb["colx"][:, 24 + tht * 4 + tr:25 + tht * 4 + tr
                               ].broadcast_to((128, 128)),
                    sb["iden_f_sb"][:], start=True, stop=True)
            sch.inc(mm, "pe")
            sch.wait(A_, "pe")
            sbc = A_.copy(sb["sxbc"][:, tht * 512:(tht + 1) * 512], ps["pg5"][:])
            sch.inc(sbc, "act")
            SY.wait_ge(sch.s("pool"), sch.now("pool"))
            d = SY.dma_start(
                out=sb["xh"][:].rearrange("p (f t) -> p f t", f=16),
                in_=din["xhT"][:, tht * 512:(tht + 1) * 512].rearrange(
                    "(f p) t -> p f t", p=128))
            sch.dinc(d, "dx5")
            G_.wait_ge(sch.s("dx5"), sch.now("dx5"))
            G_.wait_ge(sch.s("act"), sch.now("act"))
            for fc in range(16):
                xc = sb["xh"][:, fc * 512:(fc + 1) * 512]
                G_.tensor_tensor(xc, xc, sb["mubc"][:], ALU.subtract)
                G_.tensor_tensor(
                    xc, xc,
                    sb["nw_sb"][:, fc:fc + 1].broadcast_to((128, 512)),
                    ALU.mult)
                G_.tensor_tensor(xc, xc, sb["fpbc"][:], ALU.mult)
                g3 = G_.tensor_tensor(xc, xc,
                                      sb["sxbc"][:, tht * 512:(tht + 1) * 512],
                                      ALU.mult)
                sch.inc(g3, "pool")
                V.wait_ge(sch.s("pool"), sch.now("pool"))
                nv = V.tensor_scalar(sb["nsb"][:, fc * 512:(fc + 1) * 512],
                                     xc, MAGIC, MAGIC, ALU.add, ALU.subtract)
                sch.inc(nv, "dve")
            SY.wait_ge(sch.s("dve"), sch.now("dve"))
            d = SY.dma_start(
                out=scr["n"][:, tht * 512:(tht + 1) * 512].rearrange(
                    "(f p) t -> p f t", p=128),
                in_=sb["nsb"][:].rearrange("p (f t) -> p f t", f=16))
            sch.dinc(d, "dn5")
        if "nT" in dbg_t:
            SY.wait_ge(sch.s("dn5"), sch.now("dn5"))
            dd = SY.dma_start(out=dbg_t["nT"], in_=scr["n"][:, :])
            sch.dinc(dd, "dbg")
            SY.wait_ge(sch.s("dbg"), sch.now("dbg"))

    def phase5b(self, stD):
        V, A_, T_, G_, SY = self.engines()
        sch, cst, sb, ps, din = self.sch, self.cst, self.sb, self.ps, self.din
        scr, dbg_t = self.scr, self.dbg_t
        S = self.S
        S("ngu", [128, 16384], BF16, stack=stD)
        S("terng", [128, 8192], BF16, stack=stD)
        S("ternu", [128, 8192], BF16, stack=stD)

        S("hst", [128, 512], stack=stD)
        S("rsc", [128, 512], stack=stD)
        S("rsc2", [128, 512], stack=stD)

        zm = V.memset(sb["mh"][:], 0.0)
        sch.inc(zm, "dve")
        SY.wait_ge(sch.s("dn5"), sch.now("dn5"))
        d = SY.dma_start(out=sb["ngu"][:].rearrange("p (f t) -> p f t", f=16),
                         in_=scr["n"][:, :].rearrange("(f p) t -> p f t",
                                                     p=128))
        sch.dinc(d, "dn5")
        lands = [sb["wld0"], sb["wld1"]]
        T_.wait_ge(sch.s("dn5"), sch.now("dn5"))
        for grp in range(11):
            for kc in range(16):
                li = grp * 16 + kc
                land = lands[li % 2]
                if li >= 2:
                    SY.wait_ge(sch.s("gtern"), li - 1)
                gsem = f"dwgu{li % 2}"
                dd = SY.dma_start(
                    out=land[:, 0:512],
                    in_=din["wguT"][kc * 128:(kc + 1) * 128,
                                    grp * 512:(grp + 1) * 512])
                sch.dinc(dd, gsem)
                dd = SY.dma_start(
                    out=land[:, 512:1024],
                    in_=din["wguT"][kc * 128:(kc + 1) * 128,
                                    5632 + grp * 512:5632 + (grp + 1) * 512])
                sch.dinc(dd, gsem)
                V.wait_ge(sch.s(gsem), sch.now(gsem))
                V.wait_ge(sch.s("macol_r"), 1)
                if li >= 1:
                    V.wait_ge(sch.s("gtern"), li)
                mp = V.tensor_scalar(sb["mplus"][:], land[:, 0:1024],
                                     sb["macol"][:, 2:3], None, ALU.is_ge)
                sch.inc(mp, "dve")
                V.wait_ge(sch.s("macol_r"), 1)
                gm = V.tensor_scalar(sb["mminus"][:], land[:, 0:1024],
                                     sb["macol"][:, 3:4], None, ALU.is_le)
                sch.inc(gm, "dve")
                G_.wait_ge(sch.s("dve"), sch.now("dve"))
                if li == 0 and "p5b_l" in dbg_t:
                    for nm, src in (("p5b_l", land[:, 0:1024]),
                                    ("p5b_mp", sb["mplus"][:]),
                                    ("p5b_mm", sb["mminus"][:])):
                        dd = SY.dma_start(out=dbg_t[nm], in_=src)
                        sch.dinc(dd, "dbg")
                    SY.wait_ge(sch.s("dbg"), sch.now("dbg"))
                    V.wait_ge(sch.s("dbg"), sch.now("dbg"))
                G_.wait_ge(sch.s("pe"), sch.now("pe"))
                t1 = G_.tensor_tensor(sb["terng"][:, kc * 512:(kc + 1) * 512],
                                      sb["mplus"][:, 0:512],
                                      sb["mminus"][:, 0:512], ALU.subtract)
                t2 = G_.tensor_tensor(sb["ternu"][:, kc * 512:(kc + 1) * 512],
                                      sb["mplus"][:, 512:1024],
                                      sb["mminus"][:, 512:1024], ALU.subtract)
                sch.inc(t2, "gtern")
            for tokt in range(2):
                it = grp * 2 + tokt
                for m4 in range(4):
                    if it >= 1:
                        T_.wait_ge(sch.s("h_done"), (it - 1) * 4 + m4 + 1)
                    for kc in range(16):
                        if tokt == 0 and m4 == 0:
                            T_.wait_ge(sch.s("gtern"), grp * 16 + kc + 1)
                        T_.matmul(ps[f"pg{m4}"][:],
                                  sb["terng"][:, kc * 512 + m4 * 128:
                                              kc * 512 + (m4 + 1) * 128],
                                  sb["ngu"][:, kc * 1024 + tokt * 512:
                                            kc * 1024 + (tokt + 1) * 512],
                                  start=(kc == 0), stop=(kc == 15))
                    for kc in range(16):
                        mm = T_.matmul(
                            ps[f"pg{4 + m4}"][:],
                            sb["ternu"][:, kc * 512 + m4 * 128:
                                        kc * 512 + (m4 + 1) * 128],
                            sb["ngu"][:, kc * 1024 + tokt * 512:
                                      kc * 1024 + (tokt + 1) * 512],
                            start=(kc == 0), stop=(kc == 15))
                sch.inc(mm, "pe")
                for m4 in range(4):
                    A_.wait_ge(sch.s("pe"), sch.now("pe"))
                    A_.wait_ge(sch.s("h_done"), sch.now("h_done"))
                    ar = A_.activation(sb["rsc"][:], ps[f"pg{m4}"][:],
                                       AF.Relu)
                    sch.inc(ar, "act")
                    G_.wait_ge(sch.s("act"), sch.now("act"))
                    G_.wait_ge(sch.s("dve"), sch.now("dve"))
                    sq = G_.tensor_tensor(sb["rsc2"][:], sb["rsc"][:],
                                          sb["rsc"][:], ALU.mult)
                    sch.inc(sq, "pool")
                    V.wait_ge(sch.s("pool"), sch.now("pool"))
                    V.wait_ge(sch.s("h_dma"), sch.now("h_dma"))
                    hv = V.tensor_tensor(sb["hst"][:], sb["rsc2"][:],
                                         ps[f"pg{4 + m4}"][:], ALU.mult)
                    sch.inc(hv, "dve")
                    A_.wait_ge(sch.s("dve"), sch.now("dve"))
                    ab = A_.activation(sb["rsc"][:], sb["hst"][:], AF.Abs)
                    sch.inc(ab, "act")
                    V.wait_ge(sch.s("act"), sch.now("act"))
                    mv = V.tensor_tensor(
                        sb["mh"][:, tokt * 512:(tokt + 1) * 512],
                        sb["rsc"][:],
                        sb["mh"][:, tokt * 512:(tokt + 1) * 512], ALU.max)
                    sch.inc(mv, "h_done")
                    SY.wait_ge(sch.s("h_done"), sch.now("h_done"))
                    dd = SY.dma_start(
                        out=scr["h"][(grp * 4 + m4) * 128:
                                     (grp * 4 + m4 + 1) * 128,
                                     tokt * 512:(tokt + 1) * 512],
                        in_=sb["hst"][:])
                    sch.dinc(dd, "h_dma")
                    if grp == 0 and tokt == 0 and m4 == 0:
                        for nm, src in (("p5b_h", sb["hst"][:]),
                                        ("p5b_t", sb["terng"][:]),
                                        ("p5b_t2", sb["ternu"][:]),
                                        ("p5b_n", sb["ngu"][:])):
                            if nm in dbg_t:
                                dd = SY.dma_start(out=dbg_t[nm], in_=src)
                                sch.dinc(dd, "dbg")
                        if sch.now("dbg"):
                            SY.wait_ge(sch.s("dbg"), sch.now("dbg"))

    def phase5b_dbg(self):
        V, A_, T_, G_, SY = self.engines()
        sch, sb = self.sch, self.sb
        scr, dbg_t = self.scr, self.dbg_t
        for nm, src in (("mh", sb["mh"][:]), ("macol", sb["macol"][:]),
                        ("fscols", sb["fscols"][:])):
            if nm in dbg_t:
                sch.wait(SY, "h_done"); sch.wait(SY, "dve")
                dd = SY.dma_start(out=dbg_t[nm], in_=src)
                sch.dinc(dd, "dbg")
        if "h0" in dbg_t:
            sch.wait(SY, "h_dma")
            dd = SY.dma_start(out=dbg_t["h0"], in_=scr["h"][0:128, :])
            sch.dinc(dd, "dbg")
        if "hfull" in dbg_t:
            sch.wait(SY, "h_dma")
            dd = SY.dma_start(out=dbg_t["hfull"], in_=scr["h"][:, :])
            sch.dinc(dd, "dbg")
        if sch.now("dbg"):
            SY.wait_ge(sch.s("dbg"), sch.now("dbg"))

    def phase6(self, stE):
        V, A_, T_, G_, SY = self.engines()
        sch, cst, sb, ps, din = self.sch, self.cst, self.sb, self.ps, self.din
        scr, dbg_t = self.scr, self.dbg_t
        S = self.S
        S("nh", [128, 45056], BF16, stack=stE)
        S("dtern", [128, 22528], BF16, stack=stE)
        S("psibc", [128, 1024], stack=stE)
        S("dqbc", [128, 1024], stack=stE)
        S("hld", [128, 1024], stack=stE)
        S("hld2", [128, 1024], stack=stE)
        S("gsc", [128, 512], stack=stE)
        S("xld", [128, 512], stack=stE)
        S("sld", [128, 512], BF16, stack=stE)
        S("oev", [128, 512], stack=stE)
        sch.wait(T_, "dve")
        for tr in range(8):
            mm = T_.transpose(ps[f"pg{tr % 4}"][:, 0:128],
                              sb["mh"][:, tr * 128:(tr + 1) * 128],
                              sb["iden_f_sb"][:])
            sch.inc(mm, "pe")
            sch.wait(V, "pe")
            rv = V.tensor_reduce(sb["colx"][:, 32:33],
                                 ps[f"pg{tr % 4}"][:, 0:128],
                                 mybir.AxisListType.X, ALU.max)
            sch.inc(rv, "dve")
            V.wait_ge(sch.s("dve"), sch.now("dve"))
            V.wait_ge(sch.s("macol_r"), 1)
            p1 = V.tensor_tensor(sb["colx"][:, 33:34], sb["macol"][:, 0:1],
                                 sb["colx"][:, 16 + tr:17 + tr], ALU.mult)
            sch.inc(p1, "dve")
            V.wait_ge(sch.s("dve"), sch.now("dve"))
            p2 = V.tensor_scalar(sb["colx"][:, 34:35], sb["colx"][:, 33:34],
                                 sb["colx"][:, 33:34], sb["colx"][:, 33:34],
                                 ALU.mult, ALU.mult)
            sch.inc(p2, "dve")
            V.wait_ge(sch.s("dve"), sch.now("dve"))
            p3 = V.tensor_scalar(sb["colx"][:, 35:36], sb["colx"][:, 32:33],
                                 sb["colx"][:, 34:35], 1e-5,
                                 ALU.mult, ALU.max)
            sch.inc(p3, "dve")
            V.wait_ge(sch.s("dve"), sch.now("dve"))
            rc = V.reciprocal(sb["colx"][:, 36:37], sb["colx"][:, 35:36])
            sch.inc(rc, "dve")
            V.wait_ge(sch.s("dve"), sch.now("dve"))
            p4 = V.tensor_scalar(sb["colx"][:, 37:38], sb["colx"][:, 36:37],
                                 sb["colx"][:, 34:35], 127.0,
                                 ALU.mult, ALU.mult)
            p5 = V.tensor_scalar(sb["colx"][:, 38:39], sb["colx"][:, 35:36],
                                 sb["macol"][:, 1:2], 1.0 / 127.0,
                                 ALU.mult, ALU.mult)
            sch.inc(p5, "dve")
            V.wait_ge(sch.s("dve"), sch.now("dve"))
            sch.wait(T_, "dve")
            sch.wait(T_, "act")
            T_.matmul(ps["pg4"][:, 0:128],
                      sb["colx"][:, 37:38].broadcast_to((128, 128)),
                      sb["iden_f_sb"][:], start=True, stop=True)
            mm = T_.matmul(ps["pg5"][:, 0:128],
                           sb["colx"][:, 38:39].broadcast_to((128, 128)),
                           sb["iden_f_sb"][:], start=True, stop=True)
            sch.inc(mm, "pe")
            sch.wait(A_, "pe")
            A_.copy(sb["psibc"][:, tr * 128:(tr + 1) * 128], ps["pg4"][:, 0:128])
            ac = A_.copy(sb["dqbc"][:, tr * 128:(tr + 1) * 128],
                         ps["pg5"][:, 0:128])
            sch.inc(ac, "act")
        sch.wait(G_, "act"); sch.wait(G_, "h_dma")
        for nm in ("colx",):
            if nm in dbg_t:
                sch.wait(SY, "dve"); sch.wait(SY, "act")
                dd = SY.dma_start(out=dbg_t[nm], in_=sb[nm][:])
                sch.dinc(dd, "dbg")
        for nm in ("psibc", "dqbc"):
            if nm in dbg_t:
                sch.wait(SY, "act")
                dd = SY.dma_start(out=dbg_t[nm], in_=sb[nm][:])
                sch.dinc(dd, "dbg")
        if "nh" in dbg_t:
            sch.wait(SY, "nh_c")
            dd = SY.dma_start(out=dbg_t["nh"], in_=sb["nh"][:])
            sch.dinc(dd, "dbg")
            SY.wait_ge(sch.s("dbg"), sch.now("dbg"))
        lands = [sb["wld0"], sb["wld1"]]
        for md in range(4):
            for kc in range(44):
                li = md * 44 + kc
                land = lands[li % 2]
                if li >= 2:
                    SY.wait_ge(sch.s("dtern_c"), li - 1)
                dsem = f"dwd{li % 2}"
                dd = SY.dma_start(
                    out=land[:, 0:512],
                    in_=din["wdT"][kc * 128:(kc + 1) * 128,
                                   md * 512:(md + 1) * 512])
                sch.dinc(dd, dsem)
                V.wait_ge(sch.s(dsem), sch.now(dsem))
                V.wait_ge(sch.s("macol_r"), 1)
                mpl = sb["mplus"] if li % 2 == 0 else sb["mplus2"]
                mmn = sb["mminus"] if li % 2 == 0 else sb["mminus2"]
                if li >= 2:
                    V.wait_ge(sch.s("dtern_c"), li - 1)
                mp = V.tensor_scalar(mpl[:, 0:512], land[:, 0:512],
                                     sb["macol"][:, 4:5], None, ALU.is_ge)
                sch.inc(mp, "dve")
                V.wait_ge(sch.s("macol_r"), 1)
                gm = V.tensor_scalar(mmn[:, 0:512], land[:, 0:512],
                                     sb["macol"][:, 5:6], None, ALU.is_le)
                sch.inc(gm, "dve")
                G_.wait_ge(sch.s("dve"), sch.now("dve"))
                if md > 0:
                    G_.wait_ge(sch.s("pe"), sch.now("pe"))
                t1 = G_.tensor_tensor(sb["dtern"][:, kc * 512:(kc + 1) * 512],
                                      mpl[:, 0:512],
                                      mmn[:, 0:512], ALU.subtract)
                sch.inc(t1, "dtern_c")
                if md == 0:
                    hl = sb["hld"] if kc % 2 == 0 else sb["hld2"]
                    hsem = f"dh6{kc % 2}"
                    if kc >= 2:
                        SY.wait_ge(sch.s("nh_c"), kc - 1)
                    dd = SY.dma_start(out=hl[:],
                                      in_=scr["h"][kc * 128:(kc + 1) * 128,
                                                   :])
                    sch.dinc(dd, hsem)
                    G_.wait_ge(sch.s(hsem), sch.now(hsem))
                    gq = G_.tensor_tensor(hl[:], hl[:], sb["psibc"][:],
                                          ALU.mult)
                    sch.inc(gq, "pool")
                    V.wait_ge(sch.s("pool"), sch.now("pool"))
                    nv = V.tensor_scalar(sb["nh"][:, kc * 1024:
                                                  (kc + 1) * 1024],
                                         hl[:], MAGIC, MAGIC, ALU.add,
                                         ALU.subtract)
                    sch.inc(nv, "nh_c")
            for tokt in range(2):
                for m in range(4):
                    pgx = ps[f"pg{m}"]
                    if md + tokt > 0:
                        T_.wait_ge(sch.s("act"), sch.now("act"))
                    for kc in range(44):
                        if tokt == 0 and m == 0:
                            T_.wait_ge(sch.s("dtern_c"), md * 44 + kc + 1)
                            if md == 0:
                                T_.wait_ge(sch.s("nh_c"), kc + 1)
                        mm = T_.matmul(
                            pgx[:],
                            sb["dtern"][:, kc * 512 + m * 128:
                                        kc * 512 + (m + 1) * 128],
                            sb["nh"][:, kc * 1024 + tokt * 512:
                                     kc * 1024 + (tokt + 1) * 512],
                            start=(kc == 0), stop=(kc == 43))
                    sch.inc(mm, "pe")
                    V.wait_ge(sch.s("pe"), sch.now("pe"))
                    V.wait_ge(sch.s("act"), sch.now("act"))
                    gv = V.tensor_tensor(
                        sb["gsc"][:], ps[f"pg{m}"][:],
                        sb["dqbc"][:, tokt * 512:(tokt + 1) * 512], ALU.mult)
                    sch.inc(gv, "dve")
                    A_.wait_ge(sch.s("dve"), sch.now("dve"))
                    av = A_.activation(sb["gsc"][:], sb["gsc"][:], AF.Silu)
                    sch.inc(av, "act")
                    fc_out = md * 4 + m
                    dd = SY.dma_start(
                        out=sb["xld"][:],
                        in_=din["xhT"][fc_out * 128:(fc_out + 1) * 128,
                                       tokt * 512:(tokt + 1) * 512])
                    dd2 = SY.dma_start(
                        out=sb["sld"][:],
                        in_=scr["s"][fc_out * 128:(fc_out + 1) * 128,
                                     tokt * 512:(tokt + 1) * 512])
                    sch.dinc(dd, "dx6"); sch.dinc(dd2, "dx6")
                    V.wait_ge(sch.s("dx6"), sch.now("dx6"))
                    V.wait_ge(sch.s("act"), sch.now("act"))
                    V.wait_ge(sch.s("dout"), sch.now("dout"))
                    o1 = V.tensor_tensor(sb["oev"][:], sb["sld"][:],
                                         sb["gsc"][:], ALU.mult)
                    sch.inc(o1, "dve")
                    V.wait_ge(sch.s("dve"), sch.now("dve"))
                    ov = V.tensor_tensor(sb["oev"][:], sb["oev"][:],
                                         sb["xld"][:], ALU.add)
                    sch.inc(ov, "dve")
                    SY.wait_ge(sch.s("dve"), sch.now("dve"))
                    dd = SY.dma_start(
                        out=self.out_t[fc_out * 128:(fc_out + 1) * 128,
                                       tokt * 512:(tokt + 1) * 512],
                        in_=sb["oev"][:])
                    sch.dinc(dd, "dout")
        SY.wait_ge(sch.s("dout"), sch.now("dout"))
        for nm in ("dbg", "dba", "dxn", "dso", "dn5"):
            if sch.now(nm):
                SY.wait_ge(sch.s(nm), sch.now(nm))

    def phase3_stub(self):
        V, A_, T_, G_, SY = self.engines()
        sch, sb = self.sch, self.sb
        S = self.S
        S("zero_out", [128, HALF])
        sch.barrier([V])
        z = V.memset(sb["zero_out"][:], 0.0)
        sch.inc(z, "dve")
        SY.wait_ge(sch.s("dve"), sch.now("dve"))
        for fc in range(16):
            dd = SY.dma_start(out=self.out_t[fc * 128:(fc + 1) * 128, :],
                              in_=sb["zero_out"][:])
            sch.dinc(dd, "dout")
        SY.wait_ge(sch.s("dout"), sch.now("dout"))
        for nm in ("dbg", "dba", "dxn"):
            if sch.now(nm):
                SY.wait_ge(sch.s(nm), sch.now(nm))


def _build(dbg=()):
    b = Build(dbg=dbg)
    from contextlib import ExitStack
    with b.nc.Block() as block:
        @block.sync
        def _(sync):
            b.phase0()
            b.stat_prep()
            with ExitStack() as stM:
                b.S("silu_gg", [128, 16 * 512], BF16, stack=stM)
                with ExitStack() as stA:
                    b.load_small_weights(stA)
                    b.phase1(stA)
                    b.phase2(stA)
                    b.sch.barrier(b.engines())
                with ExitStack() as stB:
                    b.phase3(stB)
                    b.sch.barrier(b.engines())
            with ExitStack() as stC:
                b.phase4(stC)
                b.sch.barrier(b.engines())
            with ExitStack() as stP:
                for i in range(8):
                    b.P(f"pg{i}", [128, 512], stack=stP)
                b.S("colx", [128, 40], stack=stP)
                b.S("mh", [128, 1024], stack=stP)
                b.S("mplus", [128, 1024], BF16, stack=stP)
                b.S("mminus", [128, 1024], BF16, stack=stP)
                b.S("mplus2", [128, 1024], BF16, stack=stP)
                b.S("mminus2", [128, 1024], BF16, stack=stP)
                with ExitStack() as stD:
                    b.phase5(stD)
                    b.sch.barrier(b.engines())
                with ExitStack() as stD2:
                    b.phase5b(stD2)
                    b.phase5b_dbg()
                    b.sch.barrier(b.engines())
                with ExitStack() as stE:
                    b.phase6(stE)
    b.stack.close()
    return b.nc, b.dbg_t


def kernel(**inputs):
    shared, percore = _prep(inputs)
    nc, _ = _build()
    in_maps = []
    for c in range(8):
        m = dict(shared)
        m.update(percore[c])
        in_maps.append(m)
    res = run_bass_kernel_spmd(nc, in_maps, core_ids=list(range(8)))
    x = np.asarray(inputs["L0_attn_x"], np.float32)
    out = np.empty_like(x)
    for c in range(8):
        b, hg = c // 2, c % 2
        out[b, hg * HALF:(hg + 1) * HALF, :] = \
            np.asarray(res.results[c]["outT"]).T
    return out


# version 05
# speedup vs baseline: 1.1684x; 1.1684x over previous
import numpy as np
import ml_dtypes
import concourse.bass as bass
import concourse.mybir as mybir
from concourse.bass_utils import run_bass_kernel_spmd

F32 = mybir.dt.float32
BF16 = mybir.dt.bfloat16
AF = mybir.ActivationFunctionType
ALU = mybir.AluOpType

BF = ml_dtypes.bfloat16

L = 2048
D = 2048
DH = 128
FF = 5632
GU = 2 * FF
EPS = 1e-5
MAGIC = 12582912.0

NTT = 4
TT = 512
NSB = 4
SBT = 512
NCI = 8
C = 128
TAU = 64
HALF = 1024


def _prep(inputs):
    f32 = np.float32
    x = np.ascontiguousarray(np.asarray(inputs["L0_attn_x"], f32))
    Wq = np.asarray(inputs["L0_attn_Wq"], f32)
    Wk = np.asarray(inputs["L0_attn_Wk"], f32)
    Wv = np.asarray(inputs["L0_attn_Wv"], f32)
    Wb = np.asarray(inputs["L0_attn_Wb"], f32)
    Wa = np.asarray(inputs["L0_attn_Wa"], f32)
    Wg = np.asarray(inputs["L0_attn_Wg"], f32)
    Wo = np.asarray(inputs["L0_attn_Wo"], f32)
    A_log = np.asarray(inputs["L0_attn_A_log"], f32)
    dtb = np.asarray(inputs["L0_attn_dt_bias"], f32)
    cq = np.asarray(inputs["L0_attn_conv_q"], f32)
    ck = np.asarray(inputs["L0_attn_conv_k"], f32)
    cv = np.asarray(inputs["L0_attn_conv_v"], f32)
    onw = np.asarray(inputs["L0_attn_o_norm_w"], f32)
    wgu = np.asarray(inputs["L0_ffn_w_gate_up"], f32)
    wd = np.asarray(inputs["L0_ffn_w_down"], f32)
    nw = np.asarray(inputs["L0_ffn_norm_weight"], f32)

    iden = np.eye(128, dtype=f32)
    sidx = np.arange(C)
    tau = np.arange(TAU)
    shared = {
        "wguT": np.ascontiguousarray(wgu.T),
        "wdT": np.ascontiguousarray(wd.T),
        "woT": np.ascontiguousarray(Wo.T),
        "nw2": np.ascontiguousarray(nw.reshape(16, 128).T),
        "onw_tile": np.ascontiguousarray(np.tile(onw, (128, 4))),
        "iden_f": iden,
        "iden_h": np.ascontiguousarray(iden * 0.5),
        "iden_b": iden.astype(BF),
        "ones1": np.ones((1, 128), f32),
        "ones128": np.ones((128, 1), f32),
        "maskA": np.where(sidx[None, :] < sidx[:, None], 0.0, 200.0
                          ).astype(f32),
        "maskQ": np.where(sidx[:, None] <= 2 * tau[None, :] + 1, 0.0, -200.0
                          ).astype(f32),
    }

    def diag_taps(cw, rows, nchunks):
        out = np.zeros((nchunks, 4, 128, 128), f32)
        w = cw[rows]
        for mc in range(nchunks):
            for t in range(4):
                np.fill_diagonal(out[mc, t], w[mc * 128:(mc + 1) * 128, t])
        return np.ascontiguousarray(
            out.transpose(2, 0, 1, 3).reshape(128, nchunks * 4 * 128)
        ).astype(BF)

    percore = []
    for c in range(8):
        b, hg = c // 2, c % 2
        hsl = slice(hg * 512, hg * 512 + 512)
        krows = np.concatenate(
            [n * 1024 + hg * 512 + np.arange(512) for n in range(2)])
        brows = np.concatenate(
            [n * 8 + hg * 4 + np.arange(4) for n in range(2)])
        arows = np.arange(hg * 4, hg * 4 + 4)
        xb = x[b]
        pc = {
            "xT": np.ascontiguousarray(xb.T),
            "xhT": np.ascontiguousarray(xb[hg * HALF:(hg + 1) * HALF].T),
            "wqT": np.ascontiguousarray(Wq[hsl].T),
            "wkT": np.ascontiguousarray(Wk[krows].T),
            "wvT": np.ascontiguousarray(Wv[krows].T),
            "wbT": np.ascontiguousarray(Wb[brows].T),
            "waT": np.ascontiguousarray(Wa[arows].T),
            "wgT": np.ascontiguousarray(Wg[hsl].T),
            "alog": np.ascontiguousarray(A_log[arows].reshape(4, 1)),
            "dtb": np.ascontiguousarray(dtb[arows].reshape(4, 1)),
            "mask_me": np.full((128, 1), 1.0 - hg, f32),
            "mask_pa": np.full((128, 1), float(hg), f32),
            "dq": diag_taps(cq, np.arange(hg * 512, hg * 512 + 512), 4),
            "dk": diag_taps(ck, krows, 8),
            "dv": diag_taps(cv, krows, 8),
        }
        percore.append(pc)
    return shared, percore


class Sch:
    def __init__(self, nc, stack):
        self.nc = nc
        self.stack = stack
        self.sem = {}
        self.cnt = {}

    def s(self, name):
        if name not in self.sem:
            self.sem[name] = self.stack.enter_context(self.nc.semaphore(name))
            self.cnt[name] = 0
        return self.sem[name]

    def inc(self, instr, name, amt=1):
        self.s(name)
        self.cnt[name] += amt
        instr.then_inc(self.sem[name], amt)
        return self.cnt[name]

    def dinc(self, instr, name):
        return self.inc(instr, name, 16)

    def wait(self, eng, name, val=None):
        self.s(name)
        v = self.cnt[name] if val is None else val
        if v > 0:
            eng.wait_ge(self.sem[name], v)

    def now(self, name):
        self.s(name)
        return self.cnt[name]

    def barrier(self, engines):
        names = list(self.sem.keys())
        for eng in engines:
            for nm in names:
                if self.cnt[nm] > 0:
                    eng.wait_ge(self.sem[nm], self.cnt[nm])


class Const:
    def __init__(self, nc, stack, capacity=24):
        self.nc = nc
        self.tile = stack.enter_context(
            nc.sbuf_tensor("constpool", [128, capacity], F32))
        self.vals = {}
        self.cap = capacity

    def fill(self, v):
        if v not in self.vals:
            i = len(self.vals)
            assert i < self.cap
            self.vals[v] = i
            self.nc.vector.memset(self.tile[:, i:i + 1], float(v))
        return self.tile[:, self.vals[v]:self.vals[v] + 1]

    def ap(self, v):
        i = self.vals[v]
        return self.tile[:, i:i + 1]


DBG_SPECS = {
    "xn": ([D, L], BF16),
    "beta": ([8, L], F32),
    "g": ([4, L], F32),
    "silu_gg": ([128, 16 * 512], BF16),
    "G_S": ([128, 128], F32),
    "beta_S": ([128, 128], F32),
    "ehat_S": ([128, 128], F32),
    "lamC": ([128, 128], F32),
    "lamT": ([128, 128], F32),
    "ehatT": ([128, 128], F32),
    "gd": ([128, 128], F32),
    "gd2": ([128, 128], F32),
    "ogT": ([128, 8192], BF16),
    "qf": ([128, 8 * 1024], BF16),
    "kf": ([128, 8 * 2048], BF16),
    "vf": ([128, 8 * 2048], BF16),
    "LS": ([128, 128], F32),
    "U4": ([128, 512], BF16),
    "S4": ([128, 512], F32),
    "nA4": ([128, 512], BF16),
    "AT4": ([128, 256], BF16),
    "QB4": ([128, 256], BF16),
    "KH4": ([128, 512], BF16),
    "U4d": ([128, 512], BF16),
    "S4d": ([128, 512], F32),
    "Zf0": ([128, 1024], BF16),
    "PN2d": ([128, 512], BF16),
    "sT": ([D, HALF], BF16),
    "nT": ([D, HALF], BF16),
    "mh": ([128, 1024], F32),
    "macol": ([128, 8], F32),
    "fscols": ([128, 288], F32),
    "h0": ([128, 1024], F32),
    "nh": ([128, 45056], BF16),
    "guT": ([128, 1024], F32),
    "psibc": ([128, 1024], F32),
    "dqbc": ([128, 1024], F32),
    "colx": ([128, 40], F32),
    "hfull": ([5632, 1024], F32),
    "p5b_g": ([128, 512], F32),
    "p5b_u": ([128, 512], F32),
    "p5b_h": ([128, 512], F32),
    "p5b_t": ([128, 8192], BF16),
    "p5b_t2": ([128, 8192], BF16),
    "p5b_l": ([128, 1024], F32),
    "p5b_mp": ([128, 1024], BF16),
    "p5b_mm": ([128, 1024], BF16),
    "p5b_n": ([128, 16384], BF16),


}

ENG = ("vector", "scalar", "tensor", "gpsimd", "sync")


class Build:
    def __init__(self, dbg=()):
        self.nc = bass.Bass("TRN2", target_bir_lowering=False, debug=False,
                            detect_race_conditions=False)
        from contextlib import ExitStack
        self.stack = ExitStack()
        self.sch = Sch(self.nc, self.stack)
        self.cst = Const(self.nc, self.stack)
        self.din = {}
        self.sb = {}
        self.ps = {}
        self.scr = {}
        self.dbg = dbg
        self.dbg_t = {}
        self._declare()

    def I(self, name, shape, dt=F32):
        self.din[name] = self.nc.dram_tensor(
            name, shape, dt, kind="ExternalInput").ap()

    def S(self, name, shape, dt=F32, stack=None):
        (stack or self.stack).enter_context
        st = stack or self.stack
        self.sb[name] = st.enter_context(self.nc.sbuf_tensor(name, shape, dt))
        return self.sb[name]

    def P(self, name, shape, dt=F32, stack=None):
        st = stack or self.stack
        self.ps[name] = st.enter_context(self.nc.psum_tensor(name, shape, dt))
        return self.ps[name]

    def _declare(self):
        nc = self.nc
        I = self.I
        I("xT", [D, L]); I("xhT", [D, HALF])
        I("wqT", [D, 512]); I("wkT", [D, 1024]); I("wvT", [D, 1024])
        I("wbT", [D, 8]); I("waT", [D, 4]); I("wgT", [D, 512])
        I("woT", [1024, D]); I("wguT", [D, GU]); I("wdT", [FF, D])
        I("nw2", [128, 16]); I("onw_tile", [128, 512])
        I("alog", [4, 1]); I("dtb", [4, 1])
        I("iden_f", [128, 128]); I("iden_h", [128, 128])
        I("iden_b", [128, 128], BF16)
        I("ones1", [1, 128]); I("ones128", [128, 1])
        I("maskA", [128, 128]); I("maskQ", [128, TAU])
        I("mask_me", [128, 1]); I("mask_pa", [128, 1])
        I("dq", [128, 4 * 4 * 128], BF16)
        I("dk", [128, 8 * 4 * 128], BF16)
        I("dv", [128, 8 * 4 * 128], BF16)

        self.out_t = nc.dram_tensor("outT", [D, HALF], F32,
                                    kind="ExternalOutput").ap()
        for name in self.dbg:
            shape, dt = DBG_SPECS[name]
            self.dbg_t[name] = nc.dram_tensor(
                "dbg_" + name, shape, dt, kind="ExternalOutput").ap()

        scr = self.scr
        scr["xn"] = nc.dram_tensor("xn_scr", [D, L], BF16).ap()
        scr["ba"] = nc.dram_tensor("ba_scr", [12, L], F32).ap()
        scr["n"] = nc.dram_tensor("n_scr", [D, HALF], BF16).ap()
        scr["h"] = nc.dram_tensor("h_scr", [FF, HALF], F32).ap()
        scr["s"] = nc.dram_tensor("s_scr", [D, HALF], BF16).ap()
        scr["ogi"] = nc.dram_tensor("og_in", [512, L], BF16)
        scr["ogo"] = nc.dram_tensor("og_out", [1024, L], BF16)

        S = self.S
        S("iden_f_sb", [128, 128]); S("iden_h_sb", [128, 128])
        S("iden_b_sb", [128, 128], BF16)
        S("ones1_sb", [1, 128]); S("ones128_sb", [128, 1])
        S("maskA_sb", [128, 128]); S("maskQ_sb", [128, TAU])
        S("maskme_sb", [128, 1]); S("maskpa_sb", [128, 1])
        S("onw_sb", [128, 512]); S("nw_sb", [128, 16])
        S("alog_sb", [4, 1]); S("dtb_sb", [4, 1]); S("negea", [4, 1])
        for nm in ("GT", "lamT", "G_S", "beta_S", "ehat_S", "L_S", "GpS",
                   "nbrn_S", "ehr_S", "blr_S", "lamC", "bl_S", "LT_all"):
            S(nm, [128, 128])
        S("wld0", [128, 1024]); S("wld1", [128, 1024])

    def engines(self):
        nc = self.nc
        return nc.vector, nc.scalar, nc.tensor, nc.gpsimd, nc.sync

    def phase0(self):
        V, A_, T_, G_, SY = self.engines()
        sch, cst, sb, din = self.sch, self.cst, self.sb, self.din
        for v in (0.0, 1.0, 0.5, -0.5, 2.0, EPS, 1.0 / 2048.0, MAGIC,
                  1e-24, 1e-5, 127.0, 1.0 / 127.0, 0.03577708764, 200.0,
                  -200.0, 1.0 / 128.0, -1.0):
            cst.fill(v)
        ld = V.memset(cst.tile[:, 23:24], 0.0)
        sch.inc(ld, "dve")

        small = [("iden_f", "iden_f_sb"), ("iden_h", "iden_h_sb"),
                 ("iden_b", "iden_b_sb"),
                 ("ones1", "ones1_sb"), ("ones128", "ones128_sb"),
                 ("maskA", "maskA_sb"), ("maskQ", "maskQ_sb"),
                 ("mask_me", "maskme_sb"), ("mask_pa", "maskpa_sb"),
                 ("onw_tile", "onw_sb"), ("nw2", "nw_sb"),
                 ("alog", "alog_sb"), ("dtb", "dtb_sb"),
                 ("dq", "dq_sb"), ("dk", "dk_sb"), ("dv", "dv_sb")]
        for src, dst in small:
            if dst not in sb:
                continue
            d = SY.dma_start(out=sb[dst][:], in_=din[src])
            sch.dinc(d, "dsm")
        self.n_small = sch.now("dsm")

        A_.wait_ge(sch.s("dsm"), self.n_small)
        e1 = A_.activation(sb["negea"][:], sb["alog_sb"][:], AF.Exp)
        sch.inc(e1, "act")
        V.wait_ge(sch.s("act"), sch.now("act"))
        sch.wait(V, "dve")
        n1 = V.tensor_single_scalar(sb["negea"][:], sb["negea"][:], -1.0,
                                    ALU.mult)
        sch.inc(n1, "dve")

    def phase1(self, stA):
        V, A_, T_, G_, SY = self.engines()
        sch, cst, sb, ps, din = self.sch, self.cst, self.sb, self.ps, self.din
        scr, dbg_t = self.scr, self.dbg_t
        S = self.S
        for nm in ("pb0", "pb1", "pb2", "pb3"):
            self.P(nm, [128, 512], stack=stA)
        S("x_sb", [128, 16 * TT], stack=stA)
        S("sq0", [128, TT], stack=stA); S("sq1", [128, TT], stack=stA)
        S("xn_t", [128, 16 * TT], BF16, stack=stA)
        S("rows", [1, 6 * TT], stack=stA)
        S("beta_sb", [8, L], stack=stA); S("g_sb", [4, L], stack=stA)
        S("spA", [4, TT], stack=stA); S("spB", [4, TT], stack=stA)

        R = sb["rows"]
        def row(i):
            return R[0:1, i * TT:(i + 1) * TT]

        sch.wait(T_, "wcast")

        for tt in range(NTT):
            if tt > 0:
                SY.wait_ge(sch.s("pool"), sch.now("pool"))
                SY.wait_ge(sch.s("dve"), sch.now("dve"))
                SY.wait_ge(sch.s("pe"), sch.now("pe"))
            d = SY.dma_start(
                out=sb["x_sb"][:].rearrange("p (f t) -> p f t", f=16),
                in_=din["xT"][:, tt * TT:(tt + 1) * TT].rearrange(
                    "(f p) t -> p f t", p=128))
            sch.dinc(d, "dx")
            T_.wait_ge(sch.s("dx"), sch.now("dx"))
            A_.wait_ge(sch.s("dx"), sch.now("dx"))
            sqb = [sb["sq0"], sb["sq1"]]
            for fc in range(16):
                xc = sb["x_sb"][:, fc * TT:(fc + 1) * TT]
                tgt = sch.now("pe") - 1
                if tgt > 0:
                    A_.wait_ge(sch.s("pe"), tgt)
                q = A_.activation(sqb[fc % 2][:], xc, AF.Square)
                sch.inc(q, "act")
                T_.matmul(ps["pb0"][0:1, :], sb["ones128_sb"][:], xc,
                          start=(fc == 0), stop=(fc == 15))
                T_.wait_ge(sch.s("act"), sch.now("act"))
                mm = T_.matmul(ps["pb1"][0:1, :], sb["ones128_sb"][:],
                               sqb[fc % 2][:], start=(fc == 0),
                               stop=(fc == 15))
                sch.inc(mm, "pe")
            A_.wait_ge(sch.s("pe"), sch.now("pe"))
            m1 = A_.activation(row(0), ps["pb0"][0:1, :], AF.Copy,
                               scale=1.0 / 2048.0)
            m2 = A_.activation(row(2), ps["pb1"][0:1, :], AF.Copy,
                               scale=1.0 / 2048.0)
            sch.inc(m2, "act")
            V.wait_ge(sch.s("act"), sch.now("act"))
            V.tensor_tensor(row(4), row(0), row(0), ALU.mult)
            V.tensor_tensor(row(4), row(2), row(4), ALU.subtract)
            v3 = V.tensor_single_scalar(row(4), row(4), EPS, ALU.add)
            sch.inc(v3, "dve")
            A_.wait_ge(sch.s("dve"), sch.now("dve"))
            s1 = A_.activation(row(1), row(4), AF.Sqrt)
            sch.inc(s1, "act")
            V.wait_ge(sch.s("act"), sch.now("act"))
            r1 = V.reciprocal(row(1), row(1))
            sch.inc(r1, "dve")
            T_.wait_ge(sch.s("dve"), sch.now("dve"))
            T_.matmul(ps["pb2"][:], sb["ones1_sb"][:], row(0), start=True,
                      stop=True)
            bb = T_.matmul(ps["pb3"][:], sb["ones1_sb"][:], row(1),
                           start=True, stop=True)
            sch.inc(bb, "pe")
            A_.wait_ge(sch.s("pe"), sch.now("pe"))
            A_.copy(sb["sq0"][:], ps["pb2"][:])
            c2 = A_.copy(sb["sq1"][:], ps["pb3"][:])
            sch.inc(c2, "act")
            G_.wait_ge(sch.s("act"), sch.now("act"))
            for fc in range(16):
                xc = sb["x_sb"][:, fc * TT:(fc + 1) * TT]
                G_.tensor_tensor(xc, xc, sb["sq0"][:], ALU.subtract)
                g2 = G_.tensor_tensor(sb["xn_t"][:, fc * TT:(fc + 1) * TT],
                                      xc, sb["sq1"][:], ALU.mult)
                sch.inc(g2, "pool")
            SY.wait_ge(sch.s("pool"), sch.now("pool"))
            d2 = SY.dma_start(
                out=scr["xn"][:, tt * TT:(tt + 1) * TT].rearrange(
                    "(f p) t -> p f t", p=128),
                in_=sb["xn_t"][:].rearrange("p (f t) -> p f t", f=16))
            sch.dinc(d2, "dxn")
            T_.wait_ge(sch.s("pool"), sch.now("pool"))
            T_.wait_ge(sch.s("act"), sch.now("act"))
            for fc in range(16):
                xnc = sb["xn_t"][:, fc * TT:(fc + 1) * TT]
                T_.matmul(ps["pb0"][0:8, :],
                          sb["wba1_sb"][:, fc * 8:(fc + 1) * 8],
                          xnc, start=(fc == 0), stop=(fc == 15))
                mm = T_.matmul(ps["pb1"][0:4, :],
                               sb["wba2_sb"][:, fc * 4:(fc + 1) * 4],
                               xnc, start=(fc == 0), stop=(fc == 15))
                sch.inc(mm, "pe")
            A_.wait_ge(sch.s("pe"), sch.now("pe"))
            a0 = A_.activation(sb["beta_sb"][:, tt * TT:(tt + 1) * TT],
                               ps["pb0"][0:8, :], AF.Sigmoid)
            sch.inc(a0, "act")
            V.wait_ge(sch.s("act"), sch.now("act"))
            a1 = V.tensor_scalar(sb["spA"][:], ps["pb1"][0:4, :],
                                 sb["dtb_sb"][:], None, ALU.add)
            b2m = V.tensor_single_scalar(
                sb["beta_sb"][:, tt * TT:(tt + 1) * TT],
                sb["beta_sb"][:, tt * TT:(tt + 1) * TT], 2.0, ALU.mult)
            sch.inc(b2m, "dve")
            A_.wait_ge(sch.s("dve"), sch.now("dve"))
            A_.activation(sb["spB"][:], sb["spA"][:], AF.Abs)
            A_.activation(sb["spB"][:], sb["spB"][:], AF.Exp, scale=-1.0)
            u3 = A_.activation(sb["spB"][:], sb["spB"][:], AF.Ln,
                               bias=cst.ap(1.0)[0:4, :])
            sch.inc(u3, "act")
            V.wait_ge(sch.s("act"), sch.now("act"))
            V.tensor_single_scalar(sb["spA"][:], sb["spA"][:], 0.0, ALU.max)
            V.tensor_tensor(sb["spA"][:], sb["spA"][:], sb["spB"][:], ALU.add)
            u6 = V.tensor_scalar(sb["g_sb"][:, tt * TT:(tt + 1) * TT],
                                 sb["spA"][:], sb["negea"][:], None, ALU.mult)
            sch.inc(u6, "dve")
            for st in range(4):
                pg = ps["pb2"] if st % 2 == 0 else ps["pb3"]
                T_.wait_ge(sch.s("act"), sch.now("act"))
                T_.wait_ge(sch.s("dve"), sch.now("dve"))
                for fc in range(16):
                    xnc = sb["xn_t"][:, fc * TT + st * 128:
                                     fc * TT + st * 128 + 128]
                    mm = T_.matmul(pg[:], xnc,
                                   sb["wg_sb"][:, fc * 512:(fc + 1) * 512],
                                   start=(fc == 0), stop=(fc == 15))
                sch.inc(mm, "pe")
                A_.wait_ge(sch.s("pe"), sch.now("pe"))
                A_.wait_ge(sch.s("dve"), sch.now("dve"))
                scrt = sb["sq0"] if st % 2 == 0 else sb["sq1"]
                sl = A_.activation(scrt[:], pg[:], AF.Silu)
                sch.inc(sl, "act")
                V.wait_ge(sch.s("act"), sch.now("act"))
                gw = V.tensor_tensor(
                    sb["silu_gg"][:,
                                  (tt * 4 + st) * 512:(tt * 4 + st + 1) * 512],
                    scrt[:], sb["onw_sb"][:], ALU.mult)
                sch.inc(gw, "dve")

        SY.wait_ge(sch.s("dve"), sch.now("dve"))
        d = SY.dma_start(out=scr["ba"][0:8, :], in_=sb["beta_sb"][:])
        sch.dinc(d, "dba")
        d = SY.dma_start(out=scr["ba"][8:12, :], in_=sb["g_sb"][:])
        sch.dinc(d, "dba")

        for nm, src in (("beta", sb["beta_sb"][:]), ("g", sb["g_sb"][:]),
                        ("silu_gg", sb["silu_gg"][:])):
            if nm in dbg_t:
                dd = SY.dma_start(out=dbg_t[nm], in_=src)
                sch.dinc(dd, "dbg")
        if "xn" in dbg_t:
            SY.wait_ge(sch.s("dxn"), sch.now("dxn"))
            dd = SY.dma_start(out=dbg_t["xn"], in_=scr["xn"][:, :])
            sch.dinc(dd, "dbg")

    def load_small_weights(self, stA):
        V, A_, T_, G_, SY = self.engines()
        sch, sb, din = self.sch, self.sb, self.din
        S = self.S
        S("wba1_sb", [128, 16 * 8], BF16, stack=stA)
        S("wba2_sb", [128, 16 * 4], BF16, stack=stA)
        S("wg_sb", [128, 16 * 512], BF16, stack=stA)
        lands = [sb["wld0"], sb["wld1"]]
        widx = 0
        jobs = [("wgT", "wg_sb", 512), ("wbT", "wba1_sb", 8),
                ("waT", "wba2_sb", 4)]
        for name, dst, ncol in jobs:
            for kc in range(16):
                land = lands[widx % 2]
                if widx >= 2:
                    SY.wait_ge(sch.s("wcast"), widx - 1)
                d = SY.dma_start(out=land[:, 0:ncol],
                                 in_=din[name][kc * 128:(kc + 1) * 128, :])
                sem = f"dwm{widx % 2}"
                sch.dinc(d, sem)
                V.wait_ge(sch.s(sem), sch.now(sem))
                cc = V.tensor_copy(sb[dst][:, kc * ncol:(kc + 1) * ncol],
                                   land[:, 0:ncol])
                sch.inc(cc, "wcast")
                widx += 1
        self.widx = widx

    def phase2(self, stA):
        V, A_, T_, G_, SY = self.engines()
        sch, cst, sb, ps = self.sch, self.cst, self.sb, self.ps
        scr, dbg_t = self.scr, self.dbg_t
        S = self.S
        S("g0", [128, 128], stack=stA)
        S("bTt", [128, 128], stack=stA)

        S("gstg", [128, 192], stack=stA)
        z = V.memset(sb["g0"][:], 0.0)
        sch.inc(z, "dve")
        G_.wait_ge(sch.s("dba"), sch.now("dba"))
        for j in range(4):
            d = G_.dma_start(
                out=sb["gstg"][j * 32:(j + 1) * 32, 0:64],
                in_=scr["ba"][8 + j:9 + j, :].rearrange(
                    "r (i t) -> (r i) t", t=64))
            sch.dinc(d, "dgr")
        for n in range(2):
            for j in range(4):
                d = G_.dma_start(
                    out=sb["gstg"][j * 32:(j + 1) * 32,
                                   64 + n * 64:128 + n * 64],
                    in_=scr["ba"][n * 4 + j:n * 4 + j + 1, :].rearrange(
                        "r (i t) -> (r i) t", t=64))
                sch.dinc(d, "dgr")
        V.wait_ge(sch.s("dgr"), sch.now("dgr"))
        V.tensor_copy(sb["g0"][:].rearrange("p (t n) -> p n t", n=2)[:, 0, :],
                      sb["gstg"][:, 0:64])
        V.tensor_copy(sb["bTt"][:].rearrange("p (t n) -> p n t", n=2)[:, 0, :],
                      sb["gstg"][:, 64:128])
        cpy = V.tensor_copy(
            sb["bTt"][:].rearrange("p (t n) -> p n t", n=2)[:, 1, :],
            sb["gstg"][:, 128:192])
        sch.inc(cpy, "dve")
        sc = V.tensor_tensor_scan(sb["GT"][:], sb["g0"][:], sb["g0"][:],
                                  0.0, ALU.add, ALU.bypass)
        sch.inc(sc, "dve")
        sch.wait(T_, "dve")
        T_.transpose(ps["pb0"][:, 0:128], sb["GT"][:], sb["iden_f_sb"][:])
        t2 = T_.transpose(ps["pb1"][:, 0:128], sb["bTt"][:],
                          sb["iden_f_sb"][:])
        sch.inc(t2, "pe")
        sch.wait(A_, "pe")
        A_.copy(sb["G_S"][:], ps["pb0"][:, 0:128])
        e2 = A_.copy(sb["beta_S"][:], ps["pb1"][:, 0:128])
        sch.inc(e2, "act")
        gc_col = sb["GT"][:, 127:128]
        V.wait_ge(sch.s("dve"), sch.now("dve"))
        v1 = V.tensor_scalar(sb["g0"][:], sb["GT"][:], gc_col, None,
                             ALU.subtract)
        sch.inc(v1, "dve")
        if "gd2" in dbg_t:
            sch.wait(SY, "dve")
            dd = SY.dma_start(out=dbg_t["gd2"], in_=sb["g0"][:])
            sch.dinc(dd, "dbg")
            A_.wait_ge(sch.s("dbg"), sch.now("dbg"))
        sch.wait(A_, "dve")
        A_.activation(sb["g0"][:], sb["g0"][:], AF.Exp, scale=-1.0)
        a2 = A_.activation(sb["lamT"][:], sb["GT"][:], AF.Exp)
        sch.inc(a2, "act")
        sch.wait(T_, "act")
        t3 = T_.transpose(ps["pb2"][:, 0:128], sb["g0"][:],
                          sb["iden_f_sb"][:])
        t4 = T_.matmul(ps["pb3"][:, 0:128],
                       sb["lamT"][:, 127:128].broadcast_to((128, 128)),
                       sb["iden_f_sb"][:], start=True, stop=True)
        sch.inc(t4, "pe")
        sch.wait(A_, "pe")
        A_.copy(sb["ehat_S"][:], ps["pb2"][:, 0:128])
        e4 = A_.copy(sb["lamC"][:], ps["pb3"][:, 0:128])
        sch.inc(e4, "act")
        sch.wait(V, "act")
        lmS = V.tensor_scalar(sb["bl_S"][:], sb["G_S"][:], 0.0, None, ALU.add)
        sch.inc(lmS, "dve")
        sch.wait(A_, "dve")
        a5 = A_.activation(sb["bl_S"][:], sb["bl_S"][:], AF.Exp)
        sch.inc(a5, "act")
        sch.wait(V, "act")
        v5 = V.tensor_tensor(sb["bl_S"][:], sb["bl_S"][:], sb["beta_S"][:],
                             ALU.mult)
        sch.inc(v5, "dve")

        for nm, src in (("ehatT", sb["g0"]), ("gd", sb["GT"]),
                        ("G_S", sb["G_S"]), ("beta_S", sb["beta_S"]),
                        ("ehat_S", sb["ehat_S"]), ("lamC", sb["lamC"]),
                        ("lamT", sb["lamT"])):
            if nm in dbg_t:
                sch.wait(SY, "act"); sch.wait(SY, "dve")
                dd = SY.dma_start(out=dbg_t[nm], in_=src[:])
                sch.dinc(dd, "dbg")


    def phase3(self, stB):
        V, A_, T_, G_, SY = self.engines()
        nc = self.nc
        sch, cst, sb, ps, din = self.sch, self.cst, self.sb, self.ps, self.din
        scr, dbg_t = self.scr, self.dbg_t
        S = self.S
        P = self.P
        S("wq3", [128, 16 * 512], BF16, stack=stB)
        S("wk3", [128, 16 * 1024], BF16, stack=stB)
        S("wv3", [128, 16 * 1024], BF16, stack=stB)
        S("dq_sb", [128, 4 * 4 * 128], BF16, stack=stB)
        S("dk_sb", [128, 8 * 4 * 128], BF16, stack=stB)
        S("dv_sb", [128, 8 * 4 * 128], BF16, stack=stB)
        S("xn3", [128, 16 * SBT], BF16, stack=stB)
        S("qf", [128, 4 * SBT], BF16, stack=stB)
        S("kf", [128, 4 * 2 * SBT], BF16, stack=stB)
        S("vf", [128, 4 * 2 * SBT], BF16, stack=stB)
        S("ring", [128, SBT + 4], BF16, stack=stB)
        S("tails", [128, 60], BF16, stack=stB)
        S("rn_S", [128, 128], stack=stB)
        S("Za", [128, 1024], BF16, stack=stB)
        S("Zb", [128, 1024], BF16, stack=stB)
        S("PAa", [128, 512], BF16, stack=stB)
        S("PAb", [128, 512], BF16, stack=stB)
        S("PNa", [128, 512], BF16, stack=stB)
        S("PNb", [128, 512], BF16, stack=stB)
        S("nA4", [128, 512], BF16, stack=stB)
        S("KH4", [128, 512], BF16, stack=stB)
        S("AT4", [128, 256], BF16, stack=stB)
        S("QB4", [128, 256], BF16, stack=stB)
        S("nTW4", [128, 512], BF16, stack=stB)
        S("U4", [128, 512], BF16, stack=stB)
        S("S4", [128, 512], stack=stB)
        S("Sb4", [128, 512], BF16, stack=stB)
        S("sc0", [128, 512], stack=stB)
        S("sc1", [128, 256], stack=stB)
        S("sqk", [128, 128], stack=stB)
        S("osb", [64, 512], stack=stB)
        S("osq", [64, 128], stack=stB)
        S("ocol", [64, 12], stack=stB)
        S("ogbf", [64, 512], BF16, stack=stB)
        S("ggs", [64, 512], BF16, stack=stB)
        S("ogst", [128, 256], BF16, stack=stB)
        P("pz", [128, 1024], stack=stB)
        P("pq1", [128, 512], stack=stB)
        P("pq2", [128, 512], stack=stB)
        P("pq3", [128, 512], stack=stB)
        P("ptAB", [128, 1024], BF16, stack=stB)
        P("ptCD", [128, 1024], BF16, stack=stB)
        P("pO", [64, 512], stack=stB)

        iden = sb["iden_f_sb"]

        lands = [sb["wld0"], sb["wld1"]]
        widx = [self.widx]
        def wload(name, dst, ncol, parts):
            for kc in range(16):
                for p_ in range(parts):
                    land = lands[widx[0] % 2]
                    if widx[0] >= 2:
                        SY.wait_ge(sch.s("wcast"), widx[0] - 1)
                    cw = ncol // parts
                    d = SY.dma_start(
                        out=land[:, 0:cw],
                        in_=din[name][kc * 128:(kc + 1) * 128,
                                      p_ * cw:(p_ + 1) * cw])
                    sem = f"dwm{widx[0] % 2}"
                    sch.dinc(d, sem)
                    V.wait_ge(sch.s(sem), sch.now(sem))
                    cc = V.tensor_copy(
                        sb[dst][:, kc * ncol + p_ * cw:kc * ncol + (p_ + 1) * cw],
                        land[:, 0:cw])
                    sch.inc(cc, "wcast")
                    widx[0] += 1
        wload("wqT", "wq3", 512, 1)
        wload("wkT", "wk3", 1024, 1)
        wload("wvT", "wv3", 1024, 1)
        for nm, dst in (("dq", "dq_sb"), ("dk", "dk_sb"), ("dv", "dv_sb")):
            d = SY.dma_start(out=sb[dst][:], in_=din[nm])
            sch.dinc(d, "dsm")

        z1 = V.memset(sb["tails"][:], 0.0)
        z2 = V.memset(sb["S4"][:], 0.0)
        z3 = V.memset(sb["ocol"][:], 0.0)
        V.memset(sb["L_S"][:], 0.0)
        z4 = V.memset(sb["rn_S"][:], 0.0)
        sch.inc(z4, "dve")
        sch.wait(T_, "wcast"); sch.wait(T_, "dsm"); sch.wait(T_, "dve")
        sch.wait(G_, "dve")

        kf3 = sb["kf"]
        vf3 = sb["vf"]
        qf3 = sb["qf"][:].rearrange("p (j t) -> p j t", j=4)

        for sbk in range(NSB):
            self._sb_proj(sbk)
            for ci in range(NCI):
                self._chunk_iter(sbk, ci, kf3, vf3, qf3)
                self.stat_emit(9)
            if "qf" in dbg_t:
                sch.wait(SY, "act")
                dd = SY.dma_start(out=dbg_t["qf"][:, sbk * 2048:(sbk + 1) * 2048],
                                  in_=sb["qf"][:])
                sch.dinc(dd, "dbg")
                dd = SY.dma_start(out=dbg_t["kf"][:, sbk * 4096:(sbk + 1) * 4096],
                                  in_=sb["kf"][:])
                sch.dinc(dd, "dbg")
                dd = SY.dma_start(out=dbg_t["vf"][:, sbk * 4096:(sbk + 1) * 4096],
                                  in_=sb["vf"][:])
                sch.dinc(dd, "dbg")
                SY.wait_ge(sch.s("dbg"), sch.now("dbg"))
        for nm, src in (("LS", sb["L_S"]),
                        ("S4", sb["S4"])):
            if nm in dbg_t:
                sch.wait(SY, "act"); sch.wait(SY, "dve"); sch.wait(SY, "pe")
                dd = SY.dma_start(out=dbg_t[nm], in_=src[:])
                sch.dinc(dd, "dbg")

    def _sb_proj(self, sbk):
        V, A_, T_, G_, SY = self.engines()
        sch, sb, ps, din = self.sch, self.sb, self.ps, self.din
        scr = self.scr
        SY.wait_ge(sch.s("pe"), sch.now("pe"))
        d = SY.dma_start(
            out=sb["xn3"][:].rearrange("p (f t) -> p f t", f=16),
            in_=scr["xn"][:, sbk * SBT:(sbk + 1) * SBT].rearrange(
                "(f p) t -> p f t", p=128))
        sch.dinc(d, "dx3")
        T_.wait_ge(sch.s("dx3"), sch.now("dx3"))
        jobs = []
        for j in range(4):
            jobs.append(("wq3", 512, j, "dq_sb", 4, sb["qf"], j * SBT, j))
        for mc in range(8):
            n, j = mc // 4, mc % 4
            jobs.append(("wk3", 1024, mc, "dk_sb", 8, ("kf", j, n), None, mc))
        for mc in range(8):
            n, j = mc // 4, mc % 4
            jobs.append(("wv3", 1024, mc, "dv_sb", 8, ("vf", j, n), None, mc))
        tails_off = 0
        for jb_i, (wname, wcols, mc, dname, nch, dst, doff, tidx) in \
                enumerate(jobs):
            tail_ix = jb_i * 3
            pg = (ps["pq1"] if jb_i % 2 == 0 else ps["pq2"])
            pc = (ps["pq3"][:, 0:SBT] if jb_i % 2 == 0
                  else ps["pz"][:, 0:SBT])
            T_.wait_ge(sch.s("act"), sch.now("act"))
            for kc in range(16):
                mm = T_.matmul(
                    pg[:, 0:SBT] if pg.shape[1] > SBT else pg[:, :],
                    sb[wname][:, kc * wcols + mc * 128:
                              kc * wcols + (mc + 1) * 128],
                    sb["xn3"][:, kc * SBT:(kc + 1) * SBT],
                    start=(kc == 0), stop=(kc == 15))
            sch.inc(mm, "pe")
            A_.wait_ge(sch.s("pe"), sch.now("pe"))
            if sch.now("pe2") > 0:
                A_.wait_ge(sch.s("pe2"), sch.now("pe2"))
            A_.wait_ge(sch.s("dve"), sch.now("dve"))
            ev = A_.copy(sb["ring"][:, 3:3 + SBT], pg[:, 0:SBT])
            sch.inc(ev, "act")
            V.wait_ge(sch.s("act"), sch.now("act"))
            hv = V.tensor_copy(sb["ring"][:, 0:3],
                               sb["tails"][:, tail_ix:tail_ix + 3])
            sch.inc(hv, "dve")
            T_.wait_ge(sch.s("dve"), sch.now("dve"))
            T_.wait_ge(sch.s("act"), sch.now("act"))
            for tap in range(4):
                mm2 = T_.matmul(
                    pc,
                    sb[dname][:, (mc * 4 + tap) * 128:(mc * 4 + tap + 1) * 128],
                    sb["ring"][:, tap:tap + SBT],
                    start=(tap == 0), stop=(tap == 3))
            sch.inc(mm2, "pe2")
            V.wait_ge(sch.s("pe2"), sch.now("pe2"))
            tl = V.tensor_copy(sb["tails"][:, tail_ix:tail_ix + 3],
                               sb["ring"][:, SBT:SBT + 3])
            sch.inc(tl, "dve")
            A_.wait_ge(sch.s("pe2"), sch.now("pe2"))
            if doff is None:
                tname, jj, nn = dst
                dap = sb[tname][:, jj * 2 * SBT:(jj + 1) * 2 * SBT].rearrange(
                    "p (t n) -> p n t", n=2)[:, nn, :]
            else:
                dap = dst[:, doff:doff + SBT]
            sl = A_.activation(dap, pc, AF.Silu)
            sch.inc(sl, "act")

    def _chunk_iter(self, sbk, ci, kf3, vf3, qf3):
        V, A_, T_, G_, SY = self.engines()
        sch, cst, sb, ps = self.sch, self.cst, self.sb, self.ps
        scr = self.scr
        i_glob = sbk * NCI + ci
        t0 = ci * TAU
        iden = sb["iden_f_sb"]
        idb = sb["iden_b_sb"]
        idh = sb["iden_h_sb"]

        insts = [j * 32 + i_glob for j in range(4)]

        def kch(j):
            return kf3[:, j * 2 * SBT + 2 * t0:j * 2 * SBT + 2 * t0 + 128]

        def vch(j):
            return vf3[:, j * 2 * SBT + 2 * t0:j * 2 * SBT + 2 * t0 + 128]

        def qch(j):
            return qf3[:, j, t0:t0 + TAU]

        sch.wait(T_, "act")
        sch.wait(T_, "dve")
        for j in range(4):
            T_.transpose(ps["ptAB"][:, j * 128:(j + 1) * 128], kch(j), idb[:, :])
            T_.transpose(ps["ptAB"][:, 512 + j * 128:512 + (j + 1) * 128], vch(j), idb[:, :])
            mm = T_.transpose(ps["ptCD"][0:64, j * 128:(j + 1) * 128], qch(j), idb[:, :])
        sch.inc(mm, "pe")
        sch.wait(A_, "pe")
        for j in range(4):
            a = A_.activation(sb["sqk"][:], ps["ptAB"][:, j * 128:(j + 1) * 128],
                              AF.Square,
                              accum_out=sb["L_S"][:, insts[j]:insts[j] + 1])
            a2 = A_.activation(sb["osq"][:],
                               ps["ptCD"][0:64, j * 128:(j + 1) * 128],
                               AF.Square, scale=0.03577708764,
                               accum_out=sb["ocol"][0:64, 4 + j:5 + j])
        sch.inc(a2, "act")
        cs = sb["L_S"][:].rearrange("p (a i) -> p a i", i=32)[:, :, i_glob]
        rn = sb["rn_S"][:].rearrange("p (a i) -> p a i", i=32)[:, :, i_glob]
        gp = sb["GpS"][:].rearrange("p (a i) -> p a i", i=32)[:, :, i_glob]
        gs = sb["G_S"][:].rearrange("p (a i) -> p a i", i=32)[:, :, i_glob]
        nb = sb["nbrn_S"][:].rearrange("p (a i) -> p a i", i=32)[:, :, i_glob]
        bs = sb["beta_S"][:].rearrange("p (a i) -> p a i", i=32)[:, :, i_glob]
        er = sb["ehr_S"][:].rearrange("p (a i) -> p a i", i=32)[:, :, i_glob]
        es = sb["ehat_S"][:].rearrange("p (a i) -> p a i", i=32)[:, :, i_glob]
        br = sb["blr_S"][:].rearrange("p (a i) -> p a i", i=32)[:, :, i_glob]
        bl = sb["bl_S"][:].rearrange("p (a i) -> p a i", i=32)[:, :, i_glob]
        sch.wait(V, "act")
        u = V.tensor_single_scalar(cs, cs, 1e-24, ALU.max)
        sch.inc(u, "dve")
        sch.wait(A_, "dve")
        ln = A_.activation(cs, cs, AF.Ln)
        rne = A_.activation(rn, cs, AF.Exp, scale=-0.5)
        sch.inc(rne, "act")
        sch.wait(V, "act")
        V.scalar_tensor_tensor(nb, bs, -1.0, rn, ALU.mult, ALU.mult)
        V.tensor_tensor(er, es, rn, ALU.mult)
        lv = V.tensor_tensor(br, bl, rn, ALU.mult)
        sch.inc(lv, "dve")
        sch.wait(T_, "dve"); sch.wait(T_, "act")
        tl = T_.transpose(ps["pq3"][:, 0:128], sb["rn_S"][:], iden[:, :])
        sch.inc(tl, "pe")
        sch.wait(A_, "pe")
        le = A_.copy(sb["LT_all"][:], ps["pq3"][:, 0:128])
        sch.inc(le, "act")

        sch.wait(T_, "act")
        for j in range(4):
            T_.matmul(ps["pq1"][:, j * 128:(j + 1) * 128], kch(j), kch(j),
                      start=True, stop=True)
            T_.matmul(ps["pq2"][:, j * 128:(j + 1) * 128],
                      iden[:, insts[j]:insts[j] + 1].broadcast_to((128, 128)),
                      sb["GT"][:], start=True, stop=True)
            mm = T_.matmul(ps["pz"][:, 512 + j * 128:512 + (j + 1) * 128],
                           iden[:, insts[j]:insts[j] + 1].broadcast_to(
                               (128, 128)),
                           sb["LT_all"][:], start=True, stop=True)
        sch.inc(mm, "pe")
        sch.wait(V, "pe")
        for j in range(4):
            st1 = V.scalar_tensor_tensor(
                sb["sc0"][:, j * 128:(j + 1) * 128],
                ps["pq2"][:, j * 128:(j + 1) * 128],
                sb["G_S"][:, insts[j]:insts[j] + 1],
                sb["maskA_sb"][:], ALU.subtract, ALU.max)
        sch.inc(st1, "dve")
        sch.wait(A_, "dve")
        ae = A_.activation(sb["sc0"][:], sb["sc0"][:], AF.Exp, scale=-1.0)
        sch.inc(ae, "act")
        sch.wait(V, "act")
        for j in range(4):
            st2 = V.tensor_tensor(
                sb["sc0"][:, j * 128:(j + 1) * 128],
                sb["sc0"][:, j * 128:(j + 1) * 128],
                ps["pz"][:, 512 + j * 128:512 + (j + 1) * 128], ALU.mult)
        V.wait_ge(sch.s("dve"), sch.now("dve"))
        for j in range(4):
            st2 = V.scalar_tensor_tensor(
                sb["nA4"][:, j * 128:(j + 1) * 128],
                sb["sc0"][:, j * 128:(j + 1) * 128],
                sb["nbrn_S"][:, insts[j]:insts[j] + 1],
                ps["pq1"][:, j * 128:(j + 1) * 128], ALU.mult, ALU.mult)
        sch.inc(st2, "dve")

        sch.wait(T_, "dve")
        lam_odd = sb["lamT"][:].rearrange("p (t n) -> p t n", n=2)[:, :, 1]
        gt_odd = sb["GT"][:].rearrange("p (t n) -> p t n", n=2)[:, :, 1]
        for j in range(4):
            T_.matmul(ps["pq3"][:, j * 64:(j + 1) * 64], kch(j), qch(j),
                      start=True, stop=True)
            T_.matmul(ps["pq3"][:, 256 + j * 64:256 + (j + 1) * 64],
                      iden[:, insts[j]:insts[j] + 1].broadcast_to((128, 128)),
                      gt_odd, start=True, stop=True)
            mm = T_.matmul(ps["pz"][:, j * 64:(j + 1) * 64],
                           iden[:, insts[j]:insts[j] + 1].broadcast_to(
                               (128, 128)),
                           lam_odd, start=True, stop=True)
        sch.inc(mm, "pe")
        sch.wait(V, "pe")
        for j in range(4):
            st3 = V.scalar_tensor_tensor(
                sb["sc1"][:, j * 64:(j + 1) * 64],
                ps["pq3"][:, 256 + j * 64:256 + (j + 1) * 64],
                sb["G_S"][:, insts[j]:insts[j] + 1],
                sb["maskQ_sb"][:], ALU.subtract, ALU.min)
        sch.inc(st3, "dve")
        sch.wait(A_, "dve")
        aq = A_.activation(sb["sc1"][:], sb["sc1"][:], AF.Exp)
        sch.inc(aq, "act")
        sch.wait(V, "act")
        for j in range(4):
            tt1 = V.scalar_tensor_tensor(
                sb["AT4"][:, j * 64:(j + 1) * 64],
                sb["sc1"][:, j * 64:(j + 1) * 64],
                sb["rn_S"][:, insts[j]:insts[j] + 1],
                ps["pq3"][:, j * 64:(j + 1) * 64], ALU.mult, ALU.mult)
            tt2 = V.tensor_tensor(sb["QB4"][:, j * 64:(j + 1) * 64],
                                  qch(j),
                                  ps["pz"][:, j * 64:(j + 1) * 64],
                                  ALU.mult)
        sch.inc(tt2, "dve")

        sch.wait(A_, "dve")
        for j in range(4):
            A_.activation(sb["Za"][:, j * 256:j * 256 + 128],
                          ps["ptAB"][:, 512 + j * 128:512 + (j + 1) * 128], AF.Copy,
                          scale=sb["beta_S"][:, insts[j]:insts[j] + 1])
            A_.activation(sb["Za"][:, j * 256 + 128:(j + 1) * 256],
                          ps["ptAB"][:, j * 128:(j + 1) * 128], AF.Copy,
                          scale=sb["blr_S"][:, insts[j]:insts[j] + 1])
            kh = A_.activation(sb["KH4"][:, j * 128:(j + 1) * 128],
                               ps["ptAB"][:, j * 128:(j + 1) * 128], AF.Copy,
                               scale=sb["ehr_S"][:, insts[j]:insts[j] + 1])
        sch.inc(kh, "act")

        sch.wait(T_, "dve"); sch.wait(T_, "act")
        for j in range(4):
            tr = T_.transpose(ps["ptCD"][:, 512 + j * 128:512 + (j + 1) * 128],
                              sb["nA4"][:, j * 128:(j + 1) * 128], idb[:, :])
        sch.inc(tr, "pe")
        sch.wait(A_, "pe")
        pn1 = A_.copy(sb["PNa"][:], ps["ptCD"][:, 512:1024])
        sch.inc(pn1, "act")

        PA = [sb["nA4"], sb["PAa"], sb["PAb"]]
        PN = [sb["PNa"], sb["PNb"]]
        Z = [sb["Za"], sb["Zb"]]
        zi = 0
        pa_cur, pn_cur = 0, 0
        for step, k in enumerate([1, 2, 4, 8, 16, 32, 64]):
            need_pa = (k < 32)
            sch.wait(T_, "act"); sch.wait(T_, "dve")
            for j in range(4):
                mm = T_.matmul(ps["pz"][:, j * 256:(j + 1) * 256],
                               PN[pn_cur][:, j * 128:(j + 1) * 128],
                               Z[zi][:, j * 256:(j + 1) * 256],
                               start=True, stop=True)
            sch.inc(mm, "pe")
            if k < 64:
                pa_src = PA[0] if step == 0 else PA[1 + (pa_cur % 2)]
                pn_src = PN[pn_cur]
                if need_pa:
                    for j in range(4):
                        mm = T_.matmul(ps["pq1"][:, j * 128:(j + 1) * 128],
                                       pn_src[:, j * 128:(j + 1) * 128],
                                       pa_src[:, j * 128:(j + 1) * 128],
                                       start=True, stop=True)
                    sch.inc(mm, "pe")
                for j in range(4):
                    mm = T_.matmul(ps["pq2"][:, j * 128:(j + 1) * 128],
                                   pa_src[:, j * 128:(j + 1) * 128],
                                   pn_src[:, j * 128:(j + 1) * 128],
                                   start=True, stop=True)
                sch.inc(mm, "pe")
            sch.wait(V, "pe")
            for j in range(4):
                zt = V.tensor_tensor(
                    Z[1 - zi][:, j * 256:(j + 1) * 256],
                    ps["pz"][:, j * 256:(j + 1) * 256],
                    Z[zi][:, j * 256:(j + 1) * 256], ALU.add)
            sch.inc(zt, "dve")
            if k < 64:
                sch.wait(A_, "pe")
                if need_pa:
                    pa_nxt = PA[1 + ((pa_cur + 1) % 2)]
                    pe1 = A_.copy(pa_nxt[:], ps["pq1"][:])
                    sch.inc(pe1, "act")
                    pa_cur += 1
                pn_nxt = PN[1 - pn_cur]
                pe2 = A_.copy(pn_nxt[:], ps["pq2"][:])
                sch.inc(pe2, "act")
                pn_cur = 1 - pn_cur
            zi = 1 - zi
        Zf = Z[zi]

        sch.wait(T_, "dve")
        for j in range(4):
            tw = T_.transpose(ps["ptCD"][:, 512 + j * 128:512 + (j + 1) * 128],
                              Zf[:, j * 256 + 128:(j + 1) * 256], idb[:, :])
        sch.inc(tw, "pe")
        sch.wait(A_, "pe")
        ng = A_.activation(sb["nTW4"][:], ps["ptCD"][:, 512:1024], AF.Copy,
                           scale=cst.ap(-1.0))
        sch.inc(ng, "act")

        sch.wait(V, "pe"); sch.wait(V, "act")
        cb = V.tensor_copy(sb["Sb4"][:], sb["S4"][:])
        sch.inc(cb, "dve")
        sch.wait(T_, "act"); sch.wait(T_, "dve")
        for j in range(4):
            mm = T_.matmul(ps["pq1"][:, j * 128:(j + 1) * 128],
                           sb["nTW4"][:, j * 128:(j + 1) * 128],
                           sb["Sb4"][:, j * 128:(j + 1) * 128],
                           start=True, stop=True)
        sch.inc(mm, "pe")
        sch.wait(V, "pe")
        uo = V.tensor_tensor(sb["U4"][:].rearrange("p (j d) -> p j d", d=128),
                             ps["pq1"][:].rearrange("p (j d) -> p j d", d=128),
                             Zf[:].rearrange("p (j c) -> p j c",
                                             c=256)[:, :, 0:128],
                             ALU.add)
        sch.inc(uo, "dve")
        sch.wait(T_, "dve")
        for j in range(4):
            mm = T_.matmul(ps["pq2"][:, j * 128:(j + 1) * 128],
                           sb["KH4"][:, j * 128:(j + 1) * 128],
                           sb["U4"][:, j * 128:(j + 1) * 128],
                           start=True, stop=True)
            T_.matmul(ps["pO"][:, j * 128:(j + 1) * 128],
                      sb["QB4"][:, j * 64:(j + 1) * 64],
                      sb["Sb4"][:, j * 128:(j + 1) * 128],
                      start=True, stop=False)
            mo = T_.matmul(ps["pO"][:, j * 128:(j + 1) * 128],
                           sb["AT4"][:, j * 64:(j + 1) * 64],
                           sb["U4"][:, j * 128:(j + 1) * 128],
                           start=False, stop=True)
        sch.inc(mo, "pe")
        sch.wait(V, "pe")
        for j in range(4):
            su = V.scalar_tensor_tensor(
                sb["S4"][:, j * 128:(j + 1) * 128],
                sb["S4"][:, j * 128:(j + 1) * 128],
                sb["lamC"][:, insts[j]:insts[j] + 1],
                ps["pq2"][:, j * 128:(j + 1) * 128], ALU.mult, ALU.add)
        sch.inc(su, "dve")
        sch.wait(A_, "pe")
        for j in range(4):
            A_.copy(sb["osb"][:, j * 128:(j + 1) * 128],
                    ps["pO"][:, j * 128:(j + 1) * 128])
            oa = A_.activation(sb["osq"][:], ps["pO"][:, j * 128:(j + 1) * 128],
                               AF.Square,
                               accum_out=sb["ocol"][0:64, j:j + 1])
        sch.inc(oa, "act")
        sch.wait(V, "act")
        rv = V.scalar_tensor_tensor(sb["ocol"][0:64, 8:12],
                                    sb["ocol"][0:64, 0:4], 1.0 / 128.0,
                                    sb["ocol"][0:64, 4:8],
                                    ALU.mult, ALU.add)
        sch.inc(rv, "dve")
        sch.wait(A_, "dve")
        rsq = A_.activation(sb["ocol"][0:64, 8:12], sb["ocol"][0:64, 8:12],
                            AF.Sqrt)
        sch.inc(rsq, "act")
        sch.wait(V, "act")
        rrec = V.reciprocal(sb["ocol"][0:64, 8:12], sb["ocol"][0:64, 8:12])
        sch.inc(rrec, "dve")
        V.wait_ge(sch.s("dve"), sch.now("dve"))
        tok0 = (sbk * SBT + ci * TAU)
        gbase = (tok0 % 128)
        gtile = tok0 // 128
        gcp = V.tensor_copy(
            sb["ggs"][:],
            sb["silu_gg"][gbase:gbase + 64, gtile * 512:(gtile + 1) * 512])
        sch.inc(gcp, "dve")
        V.wait_ge(sch.s("dve"), sch.now("dve"))
        for j in range(4):
            og = V.scalar_tensor_tensor(
                sb["ogbf"][:, j * 128:(j + 1) * 128],
                sb["osb"][:, j * 128:(j + 1) * 128],
                sb["ocol"][0:64, 8 + j:9 + j],
                sb["ggs"][:, j * 128:(j + 1) * 128], ALU.mult, ALU.mult)
        sch.inc(og, "dve")
        sch.wait(T_, "dve")
        for j in range(4):
            to = T_.transpose(ps["ptCD"][:, 512 + j * 128:512 + j * 128 + 64],
                              sb["ogbf"][:, j * 128:(j + 1) * 128],
                              idb[0:64, 0:64])
        sch.inc(to, "pe")
        sch.wait(A_, "pe")
        A_.wait_ge(sch.s("dog"), sch.now("dog"))
        for j in range(4):
            oe = A_.copy(sb["ogst"][:, j * 64:(j + 1) * 64],
                         ps["ptCD"][:, 512 + j * 128:512 + j * 128 + 64])
        sch.inc(oe, "act")
        SY.wait_ge(sch.s("act"), sch.now("act"))
        dg = SY.dma_start(
            out=scr["ogi"].ap().rearrange("(j p) t -> p j t",
                                          p=128)[:, :, tok0:tok0 + 64],
            in_=sb["ogst"][:].rearrange("p (j t) -> p j t", t=64))
        sch.dinc(dg, "dog")
        if sbk == 0 and ci == 0:
            dbg_t = self.dbg_t
            for nm, src in (("nA4", sb["nA4"][:]), ("AT4", sb["AT4"][:]),
                            ("QB4", sb["QB4"][:]), ("KH4", sb["KH4"][:]),
                            ("U4d", sb["U4"][:]), ("S4d", sb["S4"][:]),
                            ("Zf0", Zf[:]), ("PN2d", sb["PNb"][:])):
                if nm in dbg_t:
                    sch.wait(SY, "act"); sch.wait(SY, "dve")
                    dd = SY.dma_start(out=dbg_t[nm], in_=src)
                    sch.dinc(dd, "dbg")
            if sch.now("dbg"):
                SY.wait_ge(sch.s("dbg"), sch.now("dbg"))


    def stat_prep(self):
        S = self.S
        S("fabs", [128, 1024])
        S("fscols", [128, 288])
        S("macol", [128, 8])
        z = self.nc.vector.memset(self.sb["fscols"][:], 0.0)
        self.sch.inc(z, "dve")
        self.stat_jobs = []
        ci = 0
        for kc in range(16):
            for p_ in range(11):
                self.stat_jobs.append(("wguT", kc * 128, p_ * 1024, 1024, ci))
                ci += 1
        for kc in range(44):
            for p_ in range(2):
                self.stat_jobs.append(("wdT", kc * 128, p_ * 1024, 1024, ci))
                ci += 1
        self.stat_pos = 0
        self.n_gu_jobs = 176

    def stat_emit(self, count):
        V, A_, T_, G_, SY = self.engines()
        sch, sb, din = self.sch, self.sb, self.din
        lands = [sb["wld0"], sb["wld1"]]
        for _ in range(count):
            if self.stat_pos >= len(self.stat_jobs):
                return
            name, r0, c0, w, slot = self.stat_jobs[self.stat_pos]
            land = lands[self.stat_pos % 2]
            if self.stat_pos < 2:
                SY.wait_ge(sch.s("wcast"), sch.now("wcast"))
            else:
                SY.wait_ge(sch.s("fs_done"), self.stat_pos - 1)
            d = SY.dma_start(out=land[:, 0:w],
                             in_=din[name][r0:r0 + 128, c0:c0 + w])
            dsem = f"dfs{self.stat_pos % 2}"
            sch.dinc(d, dsem)
            if self.stat_pos % 2 == 0:
                A_.wait_ge(sch.s(dsem), sch.now(dsem))
                a = A_.activation(sb["fabs"][:, 0:w], land[:, 0:w], AF.Abs,
                                  accum_out=sb["fscols"][:, slot:slot + 1])
                sch.inc(a, "fs_done")
            else:
                V.wait_ge(sch.s(dsem), sch.now(dsem))
                g2 = V.scalar_tensor_tensor(
                    sb["fabs"][:, 0:w], land[:, 0:w], -1.0, land[:, 0:w],
                    ALU.mult, ALU.max,
                    accum_out=sb["fscols"][:, slot:slot + 1])
                sch.inc(g2, "fs_done")
            self.stat_pos += 1

    def stat_final(self):
        V, A_, T_, G_, SY = self.engines()
        sch, cst, sb, ps = self.sch, self.cst, self.sb, self.ps
        sch.wait(A_, "fs_done"); sch.wait(A_, "dve"); sch.wait(A_, "pool")
        st = A_.copy(sb["fabs"][:, 0:264], sb["fscols"][:, 0:264])
        sch.inc(st, "act")
        sch.wait(V, "act")
        r1 = V.tensor_reduce(sb["macol"][:, 6:7], sb["fabs"][:, 0:176],
                             mybir.AxisListType.X, ALU.add)
        r2 = V.tensor_reduce(sb["macol"][:, 7:8], sb["fabs"][:, 176:264],
                             mybir.AxisListType.X, ALU.add)
        sch.inc(r2, "dve")
        sch.wait(T_, "dve")
        t1 = T_.matmul(ps["pw0"][0:1, 0:2], sb["ones128_sb"][:],
                       sb["macol"][:, 6:8], start=True, stop=True)
        sch.inc(t1, "pe")
        sch.wait(A_, "pe")
        ev = A_.copy(sb["macol"][0:1, 4:6], ps["pw0"][0:1, 0:2])
        sch.inc(ev, "act")
        sch.wait(T_, "act")
        t2 = T_.matmul(ps["pw0"][:, 0:2], sb["ones1_sb"][:],
                       sb["macol"][0:1, 4:6], start=True, stop=True)
        sch.inc(t2, "pe")
        sch.wait(V, "pe")
        c1 = V.tensor_scalar(sb["macol"][:, 0:2], ps["pw0"][:, 0:2],
                             1.0 / 23068672.0, 1e-5, ALU.mult, ALU.max)
        c2 = V.tensor_scalar(sb["macol"][:, 1:2], ps["pw0"][:, 1:2],
                             1.0 / 11534336.0, 1e-5, ALU.mult, ALU.max)
        sch.inc(c2, "dve")
        A_.wait_ge(sch.s("dve"), sch.now("dve"))
        A_.activation(sb["macol"][:, 2:3], sb["macol"][:, 0:1], AF.Copy,
                      scale=0.5)
        A_.activation(sb["macol"][:, 3:4], sb["macol"][:, 0:1], AF.Copy,
                      scale=-0.5)
        A_.activation(sb["macol"][:, 4:5], sb["macol"][:, 1:2], AF.Copy,
                      scale=0.5)
        c6 = A_.activation(sb["macol"][:, 5:6], sb["macol"][:, 1:2], AF.Copy,
                           scale=-0.5)
        sch.inc(c6, "macol_r")

    def phase4(self, stC):
        V, A_, T_, G_, SY = self.engines()
        nc = self.nc
        sch, cst, sb, ps, din = self.sch, self.cst, self.sb, self.ps, self.din
        scr = self.scr
        S, P = self.S, self.P
        S("wo_b", [128, 16384], BF16, stack=stC)
        S("worhs", [128, 8192], BF16, stack=stC)
        S("bl0", [128, 512], BF16, stack=stC)
        S("bl1", [128, 512], BF16, stack=stC)
        S("sevac", [128, 512], BF16, stack=stC)
        P("pw0", [128, 512], stack=stC); P("pw1", [128, 512], stack=stC)
        self.stat_final()

        G_.wait_ge(sch.s("dog"), sch.now("dog"))
        cc = G_.collective_compute(
            "AllGather", ALU.bypass,
            replica_groups=[[0, 1], [2, 3], [4, 5], [6, 7]],
            ins=[scr["ogi"].ap().opt()],
            outs=[scr["ogo"].ap().opt()])
        cc.then_inc(sch.s("ccs"))
        sch.cnt["ccs"] += 1
        lands = [sb["wld0"], sb["wld1"]]
        for kc in range(8):
            for p_ in range(2):
                li = (kc * 2 + p_)
                land = lands[li % 2]
                if li >= 2:
                    SY.wait_ge(sch.s("wocast"), li - 1)
                wsem = f"dwo{li % 2}"
                dd = SY.dma_start(
                    out=land[:, 0:1024],
                    in_=din["woT"][kc * 128:(kc + 1) * 128,
                                   p_ * 1024:(p_ + 1) * 1024])
                sch.dinc(dd, wsem)
                V.wait_ge(sch.s(wsem), sch.now(wsem))
                cv = V.tensor_copy(
                    sb["wo_b"][:, kc * 2048 + p_ * 1024:
                               kc * 2048 + (p_ + 1) * 1024],
                    land[:, 0:1024])
                sch.inc(cv, "wocast")
        SY.wait_ge(sch.s("ccs"), sch.now("ccs"))
        for kc in range(8):
            for tokt in range(2):
                dd = SY.dma_start(
                    out=sb["bl0"][:],
                    in_=scr["ogo"].ap()[kc * 128:(kc + 1) * 128,
                                        tokt * 512:(tokt + 1) * 512])
                sch.dinc(dd, "dgo")
                dd = SY.dma_start(
                    out=sb["bl1"][:],
                    in_=scr["ogo"].ap()[kc * 128:(kc + 1) * 128,
                                        1024 + tokt * 512:
                                        1024 + (tokt + 1) * 512])
                sch.dinc(dd, "dgo")
                G_.wait_ge(sch.s("dgo"), sch.now("dgo"))
                gb = G_.tensor_tensor(
                    sb["bl0"][:], sb["bl0"][:],
                    sb["maskme_sb"][:].broadcast_to((128, 512)), ALU.mult)
                sch.inc(gb, "pool")
                V.wait_ge(sch.s("pool"), sch.now("pool"))
                V.wait_ge(sch.s("dgo"), sch.now("dgo"))
                bv = V.scalar_tensor_tensor(
                    sb["worhs"][:, kc * 1024 + tokt * 512:
                                kc * 1024 + (tokt + 1) * 512],
                    sb["bl1"][:], sb["maskpa_sb"][:], sb["bl0"][:],
                    ALU.mult, ALU.add)
                sch.inc(bv, "dve")
                SY.wait_ge(sch.s("dve"), sch.now("dve"))
        sch.wait(T_, "dve"); sch.wait(T_, "wocast")
        for m in range(16):
            for tokt in range(2):
                pw = ps["pw0"] if (m * 2 + tokt) % 2 == 0 else ps["pw1"]
                if m * 2 + tokt >= 2:
                    T_.wait_ge(sch.s("act"), sch.now("act"))
                for kc in range(8):
                    mm = T_.matmul(
                        pw[:],
                        sb["wo_b"][:, kc * 2048 + m * 128:
                                   kc * 2048 + (m + 1) * 128],
                        sb["worhs"][:, kc * 1024 + tokt * 512:
                                    kc * 1024 + (tokt + 1) * 512],
                        start=(kc == 0), stop=(kc == 7))
                sch.inc(mm, "pe")
                A_.wait_ge(sch.s("pe"), sch.now("pe"))
                ev = A_.copy(sb["sevac"][:], pw[:])
                sch.inc(ev, "act")
                SY.wait_ge(sch.s("act"), sch.now("act"))
                dd = SY.dma_start(
                    out=scr["s"][m * 128:(m + 1) * 128,
                                 tokt * 512:(tokt + 1) * 512],
                    in_=sb["sevac"][:])
                sch.dinc(dd, "dso")
                A_.wait_ge(sch.s("dso"), sch.now("dso"))
        if "sT" in self.dbg_t:
            SY.wait_ge(sch.s("dso"), sch.now("dso"))
            dd = SY.dma_start(out=self.dbg_t["sT"], in_=scr["s"][:, :])
            sch.dinc(dd, "dbg")
            SY.wait_ge(sch.s("dbg"), sch.now("dbg"))

    def phase5(self, stD):
        V, A_, T_, G_, SY = self.engines()
        sch, cst, sb, ps, din = self.sch, self.cst, self.sb, self.ps, self.din
        scr, dbg_t = self.scr, self.dbg_t
        S, P = self.S, self.P
        S("xh", [128, 8192], stack=stD)
        S("nsb", [128, 8192], BF16, stack=stD)
        S("Mx", [128, 1024], stack=stD)

        S("rows5", [1, 3072], stack=stD)
        S("mubc", [128, 512], stack=stD)
        S("fpbc", [128, 512], stack=stD)
        S("sxbc", [128, 1024], stack=stD)
        S("sqs", [128, 512], stack=stD)
        R5 = sb["rows5"]
        def row5(i):
            return R5[0:1, i * 512:(i + 1) * 512]
        zm = V.memset(sb["Mx"][:], 0.0)
        sch.inc(zm, "dve")

        for tht in range(2):
            SY.wait_ge(sch.s("dve"), sch.now("dve"))
            SY.wait_ge(sch.s("pool"), sch.now("pool"))
            d = SY.dma_start(
                out=sb["xh"][:].rearrange("p (f t) -> p f t", f=16),
                in_=din["xhT"][:, tht * 512:(tht + 1) * 512].rearrange(
                    "(f p) t -> p f t", p=128))
            sch.dinc(d, "dx5")
            T_.wait_ge(sch.s("dx5"), sch.now("dx5"))
            A_.wait_ge(sch.s("dx5"), sch.now("dx5"))
            for fc in range(16):
                xc = sb["xh"][:, fc * 512:(fc + 1) * 512]
                tgt = sch.now("pe") - 1
                if tgt > 0:
                    A_.wait_ge(sch.s("pe"), tgt)
                q = A_.activation(sb["sqs"][:], xc, AF.Square)
                sch.inc(q, "act")
                T_.matmul(ps["pg0"][0:1, :], sb["ones128_sb"][:], xc,
                          start=(fc == 0), stop=(fc == 15))
                T_.wait_ge(sch.s("act"), sch.now("act"))
                mm = T_.matmul(ps["pg1"][0:1, :], sb["ones128_sb"][:],
                               sb["sqs"][:], start=(fc == 0), stop=(fc == 15))
                sch.inc(mm, "pe")
            A_.wait_ge(sch.s("pe"), sch.now("pe"))
            m1 = A_.activation(row5(0), ps["pg0"][0:1, :], AF.Copy,
                               scale=1.0 / 2048.0)
            m2 = A_.activation(row5(2), ps["pg1"][0:1, :], AF.Copy,
                               scale=1.0 / 2048.0)
            sch.inc(m2, "act")
            V.wait_ge(sch.s("act"), sch.now("act"))
            V.tensor_tensor(row5(4), row5(0), row5(0), ALU.mult)
            V.tensor_tensor(row5(4), row5(2), row5(4), ALU.subtract)
            v3 = V.tensor_single_scalar(row5(3), row5(4), EPS, ALU.add)
            sch.inc(v3, "dve")
            A_.wait_ge(sch.s("dve"), sch.now("dve"))
            s1 = A_.activation(row5(1), row5(3), AF.Sqrt)
            sch.inc(s1, "act")
            V.wait_ge(sch.s("act"), sch.now("act"))
            r1 = V.reciprocal(row5(1), row5(1))
            sch.inc(r1, "dve")
            V.wait_ge(sch.s("dve"), sch.now("dve"))
            V.tensor_tensor(row5(2), row5(4), row5(1), ALU.mult)
            V.tensor_tensor(row5(2), row5(2), row5(1), ALU.mult)
            v5 = V.tensor_single_scalar(row5(2), row5(2), EPS, ALU.add)
            sch.inc(v5, "dve")
            A_.wait_ge(sch.s("dve"), sch.now("dve"))
            s2 = A_.activation(row5(2), row5(2), AF.Sqrt)
            sch.inc(s2, "act")
            V.wait_ge(sch.s("act"), sch.now("act"))
            V.reciprocal(row5(2), row5(2))
            v6 = V.tensor_tensor(row5(2), row5(2), row5(1), ALU.mult)
            sch.inc(v6, "dve")
            T_.wait_ge(sch.s("dve"), sch.now("dve"))
            T_.matmul(ps["pg2"][:], sb["ones1_sb"][:], row5(0), start=True,
                      stop=True)
            bb = T_.matmul(ps["pg3"][:], sb["ones1_sb"][:], row5(2),
                           start=True, stop=True)
            sch.inc(bb, "pe")
            A_.wait_ge(sch.s("pe"), sch.now("pe"))
            A_.copy(sb["mubc"][:], ps["pg2"][:])
            c2 = A_.copy(sb["fpbc"][:], ps["pg3"][:])
            sch.inc(c2, "act")
            G_.wait_ge(sch.s("act"), sch.now("act"))
            for fc in range(16):
                xc = sb["xh"][:, fc * 512:(fc + 1) * 512]
                G_.tensor_tensor(xc, xc, sb["mubc"][:], ALU.subtract)
                G_.tensor_tensor(
                    xc, xc,
                    sb["nw_sb"][:, fc:fc + 1].broadcast_to((128, 512)),
                    ALU.mult)
                g2 = G_.tensor_tensor(xc, xc, sb["fpbc"][:], ALU.mult)
                sch.inc(g2, "pool")
                V.wait_ge(sch.s("pool"), sch.now("pool"))
                V.scalar_tensor_tensor(xc, xc, -1.0, xc, ALU.mult, ALU.max)
                mv = V.tensor_tensor(sb["Mx"][:, tht * 512:(tht + 1) * 512],
                                     xc,
                                     sb["Mx"][:, tht * 512:(tht + 1) * 512],
                                     ALU.max)
                sch.inc(mv, "dve")
            sch.wait(T_, "dve")
            for tr in range(4):
                mm = T_.transpose(ps["pg4"][:, tr * 128:(tr + 1) * 128],
                                  sb["Mx"][:, tht * 512 + tr * 128:
                                           tht * 512 + (tr + 1) * 128],
                                  sb["iden_f_sb"][:])
            sch.inc(mm, "pe")
            sch.wait(V, "pe")
            for tr in range(4):
                rv = V.tensor_reduce(sb["colx"][:, tht * 4 + tr:tht * 4 + tr + 1],
                                     ps["pg4"][:, tr * 128:(tr + 1) * 128],
                                     mybir.AxisListType.X, ALU.max)
            sch.inc(rv, "dve")
            V.wait_ge(sch.s("dve"), sch.now("dve"))
            cl = V.tensor_single_scalar(
                sb["colx"][:, 8 + tht * 4:8 + tht * 4 + 4],
                sb["colx"][:, tht * 4:tht * 4 + 4], 1e-5, ALU.max)
            sch.inc(cl, "dve")
            V.wait_ge(sch.s("dve"), sch.now("dve"))
            V.tensor_single_scalar(sb["colx"][:, 16 + tht * 4:16 + tht * 4 + 4],
                                   sb["colx"][:, 8 + tht * 4:8 + tht * 4 + 4],
                                   1.0 / 127.0, ALU.mult)
            rc = V.reciprocal(sb["colx"][:, 24 + tht * 4:24 + tht * 4 + 4],
                              sb["colx"][:, 8 + tht * 4:8 + tht * 4 + 4])
            sch.inc(rc, "dve")
            V.wait_ge(sch.s("dve"), sch.now("dve"))
            sx = V.tensor_single_scalar(
                sb["colx"][:, 24 + tht * 4:24 + tht * 4 + 4],
                sb["colx"][:, 24 + tht * 4:24 + tht * 4 + 4],
                127.0, ALU.mult)
            sch.inc(sx, "dve")
            sch.wait(T_, "dve")
            for tr in range(4):
                mm = T_.matmul(
                    ps["pg5"][:, tr * 128:(tr + 1) * 128],
                    sb["colx"][:, 24 + tht * 4 + tr:25 + tht * 4 + tr
                               ].broadcast_to((128, 128)),
                    sb["iden_f_sb"][:], start=True, stop=True)
            sch.inc(mm, "pe")
            sch.wait(A_, "pe")
            sbc = A_.copy(sb["sxbc"][:, tht * 512:(tht + 1) * 512], ps["pg5"][:])
            sch.inc(sbc, "act")
            SY.wait_ge(sch.s("pool"), sch.now("pool"))
            d = SY.dma_start(
                out=sb["xh"][:].rearrange("p (f t) -> p f t", f=16),
                in_=din["xhT"][:, tht * 512:(tht + 1) * 512].rearrange(
                    "(f p) t -> p f t", p=128))
            sch.dinc(d, "dx5")
            G_.wait_ge(sch.s("dx5"), sch.now("dx5"))
            G_.wait_ge(sch.s("act"), sch.now("act"))
            G_.wait_ge(sch.s("dve"), sch.now("dve"))
            gx = G_.tensor_tensor(sb["fpbc"][:], sb["fpbc"][:],
                                  sb["sxbc"][:, tht * 512:(tht + 1) * 512],
                                  ALU.mult)
            sch.inc(gx, "pool")
            V.wait_ge(sch.s("dx5"), sch.now("dx5"))
            V.wait_ge(sch.s("act"), sch.now("act"))
            for fc in range(16):
                xc = sb["xh"][:, fc * 512:(fc + 1) * 512]
                v1 = V.tensor_tensor(xc, xc, sb["mubc"][:], ALU.subtract)
                sch.inc(v1, "dve")
                G_.wait_ge(sch.s("dve"), sch.now("dve"))
                G_.tensor_tensor(
                    xc, xc,
                    sb["nw_sb"][:, fc:fc + 1].broadcast_to((128, 512)),
                    ALU.mult)
                g3 = G_.tensor_tensor(xc, xc, sb["fpbc"][:], ALU.mult)
                sch.inc(g3, "pool")
                V.wait_ge(sch.s("pool"), sch.now("pool"))
                nv = V.tensor_scalar(sb["nsb"][:, fc * 512:(fc + 1) * 512],
                                     xc, MAGIC, MAGIC, ALU.add, ALU.subtract)
                sch.inc(nv, "dve")
            SY.wait_ge(sch.s("dve"), sch.now("dve"))
            d = SY.dma_start(
                out=scr["n"][:, tht * 512:(tht + 1) * 512].rearrange(
                    "(f p) t -> p f t", p=128),
                in_=sb["nsb"][:].rearrange("p (f t) -> p f t", f=16))
            sch.dinc(d, "dn5")
        if "nT" in dbg_t:
            SY.wait_ge(sch.s("dn5"), sch.now("dn5"))
            dd = SY.dma_start(out=dbg_t["nT"], in_=scr["n"][:, :])
            sch.dinc(dd, "dbg")
            SY.wait_ge(sch.s("dbg"), sch.now("dbg"))

    def phase5b(self, stD):
        V, A_, T_, G_, SY = self.engines()
        sch, cst, sb, ps, din = self.sch, self.cst, self.sb, self.ps, self.din
        scr, dbg_t = self.scr, self.dbg_t
        S = self.S
        S("ngu", [128, 16384], BF16, stack=stD)
        S("terng", [128, 8192], BF16, stack=stD)
        S("ternu", [128, 8192], BF16, stack=stD)

        S("hst", [128, 512], stack=stD)
        S("rsc", [128, 512], stack=stD)
        S("rsc2", [128, 512], stack=stD)

        zm = V.memset(sb["mh"][:], 0.0)
        sch.inc(zm, "dve")
        SY.wait_ge(sch.s("dn5"), sch.now("dn5"))
        d = SY.dma_start(out=sb["ngu"][:].rearrange("p (f t) -> p f t", f=16),
                         in_=scr["n"][:, :].rearrange("(f p) t -> p f t",
                                                     p=128))
        sch.dinc(d, "dn5")
        lands = [sb["wld0"], sb["wld1"]]
        T_.wait_ge(sch.s("dn5"), sch.now("dn5"))
        for grp in range(11):
            for kc in range(16):
                li = grp * 16 + kc
                land = lands[li % 2]
                if li >= 2:
                    SY.wait_ge(sch.s("gtern"), li - 1)
                gsem = f"dwgu{li % 2}"
                dd = SY.dma_start(
                    out=land[:, 0:512],
                    in_=din["wguT"][kc * 128:(kc + 1) * 128,
                                    grp * 512:(grp + 1) * 512])
                sch.dinc(dd, gsem)
                dd = SY.dma_start(
                    out=land[:, 512:1024],
                    in_=din["wguT"][kc * 128:(kc + 1) * 128,
                                    5632 + grp * 512:5632 + (grp + 1) * 512])
                sch.dinc(dd, gsem)
                V.wait_ge(sch.s(gsem), sch.now(gsem))
                V.wait_ge(sch.s("macol_r"), 1)
                if li >= 1:
                    V.wait_ge(sch.s("gtern"), li)
                mp = V.tensor_scalar(sb["mplus"][:], land[:, 0:1024],
                                     sb["macol"][:, 2:3], None, ALU.is_ge)
                sch.inc(mp, "dve")
                V.wait_ge(sch.s("macol_r"), 1)
                gm = V.tensor_scalar(sb["mminus"][:], land[:, 0:1024],
                                     sb["macol"][:, 3:4], None, ALU.is_le)
                sch.inc(gm, "dve")
                G_.wait_ge(sch.s("dve"), sch.now("dve"))
                if li == 0 and "p5b_l" in dbg_t:
                    for nm, src in (("p5b_l", land[:, 0:1024]),
                                    ("p5b_mp", sb["mplus"][:]),
                                    ("p5b_mm", sb["mminus"][:])):
                        dd = SY.dma_start(out=dbg_t[nm], in_=src)
                        sch.dinc(dd, "dbg")
                    SY.wait_ge(sch.s("dbg"), sch.now("dbg"))
                    V.wait_ge(sch.s("dbg"), sch.now("dbg"))
                G_.wait_ge(sch.s("pe"), sch.now("pe"))
                t1 = G_.tensor_tensor(sb["terng"][:, kc * 512:(kc + 1) * 512],
                                      sb["mplus"][:, 0:512],
                                      sb["mminus"][:, 0:512], ALU.subtract)
                t2 = G_.tensor_tensor(sb["ternu"][:, kc * 512:(kc + 1) * 512],
                                      sb["mplus"][:, 512:1024],
                                      sb["mminus"][:, 512:1024], ALU.subtract)
                sch.inc(t2, "gtern")
            for tokt in range(2):
                it = grp * 2 + tokt
                for m4 in range(4):
                    if it >= 1:
                        T_.wait_ge(sch.s("h_done"), (it - 1) * 4 + m4 + 1)
                    for kc in range(16):
                        if tokt == 0 and m4 == 0:
                            T_.wait_ge(sch.s("gtern"), grp * 16 + kc + 1)
                        T_.matmul(ps[f"pg{m4}"][:],
                                  sb["terng"][:, kc * 512 + m4 * 128:
                                              kc * 512 + (m4 + 1) * 128],
                                  sb["ngu"][:, kc * 1024 + tokt * 512:
                                            kc * 1024 + (tokt + 1) * 512],
                                  start=(kc == 0), stop=(kc == 15))
                    for kc in range(16):
                        mm = T_.matmul(
                            ps[f"pg{4 + m4}"][:],
                            sb["ternu"][:, kc * 512 + m4 * 128:
                                        kc * 512 + (m4 + 1) * 128],
                            sb["ngu"][:, kc * 1024 + tokt * 512:
                                      kc * 1024 + (tokt + 1) * 512],
                            start=(kc == 0), stop=(kc == 15))
                sch.inc(mm, "pe")
                for m4 in range(4):
                    A_.wait_ge(sch.s("pe"), sch.now("pe"))
                    A_.wait_ge(sch.s("h_done"), sch.now("h_done"))
                    ar = A_.activation(sb["rsc"][:], ps[f"pg{m4}"][:],
                                       AF.Relu)
                    sch.inc(ar, "act")
                    G_.wait_ge(sch.s("act"), sch.now("act"))
                    G_.wait_ge(sch.s("dve"), sch.now("dve"))
                    sq = G_.tensor_tensor(sb["rsc2"][:], sb["rsc"][:],
                                          sb["rsc"][:], ALU.mult)
                    sch.inc(sq, "pool")
                    V.wait_ge(sch.s("pool"), sch.now("pool"))
                    V.wait_ge(sch.s("h_dma"), sch.now("h_dma"))
                    hv = V.tensor_tensor(sb["hst"][:], sb["rsc2"][:],
                                         ps[f"pg{4 + m4}"][:], ALU.mult)
                    sch.inc(hv, "dve")
                    A_.wait_ge(sch.s("dve"), sch.now("dve"))
                    ab = A_.activation(sb["rsc"][:], sb["hst"][:], AF.Abs)
                    sch.inc(ab, "act")
                    V.wait_ge(sch.s("act"), sch.now("act"))
                    mv = V.tensor_tensor(
                        sb["mh"][:, tokt * 512:(tokt + 1) * 512],
                        sb["rsc"][:],
                        sb["mh"][:, tokt * 512:(tokt + 1) * 512], ALU.max)
                    sch.inc(mv, "h_done")
                    SY.wait_ge(sch.s("h_done"), sch.now("h_done"))
                    dd = SY.dma_start(
                        out=scr["h"][(grp * 4 + m4) * 128:
                                     (grp * 4 + m4 + 1) * 128,
                                     tokt * 512:(tokt + 1) * 512],
                        in_=sb["hst"][:])
                    sch.dinc(dd, "h_dma")
                    if grp == 0 and tokt == 0 and m4 == 0:
                        for nm, src in (("p5b_h", sb["hst"][:]),
                                        ("p5b_t", sb["terng"][:]),
                                        ("p5b_t2", sb["ternu"][:]),
                                        ("p5b_n", sb["ngu"][:])):
                            if nm in dbg_t:
                                dd = SY.dma_start(out=dbg_t[nm], in_=src)
                                sch.dinc(dd, "dbg")
                        if sch.now("dbg"):
                            SY.wait_ge(sch.s("dbg"), sch.now("dbg"))

    def phase5b_dbg(self):
        V, A_, T_, G_, SY = self.engines()
        sch, sb = self.sch, self.sb
        scr, dbg_t = self.scr, self.dbg_t
        for nm, src in (("mh", sb["mh"][:]), ("macol", sb["macol"][:]),
                        ("fscols", sb["fscols"][:])):
            if nm in dbg_t:
                sch.wait(SY, "h_done"); sch.wait(SY, "dve")
                dd = SY.dma_start(out=dbg_t[nm], in_=src)
                sch.dinc(dd, "dbg")
        if "h0" in dbg_t:
            sch.wait(SY, "h_dma")
            dd = SY.dma_start(out=dbg_t["h0"], in_=scr["h"][0:128, :])
            sch.dinc(dd, "dbg")
        if "hfull" in dbg_t:
            sch.wait(SY, "h_dma")
            dd = SY.dma_start(out=dbg_t["hfull"], in_=scr["h"][:, :])
            sch.dinc(dd, "dbg")
        if sch.now("dbg"):
            SY.wait_ge(sch.s("dbg"), sch.now("dbg"))

    def phase6(self, stE):
        V, A_, T_, G_, SY = self.engines()
        sch, cst, sb, ps, din = self.sch, self.cst, self.sb, self.ps, self.din
        scr, dbg_t = self.scr, self.dbg_t
        S = self.S
        S("nh", [128, 45056], BF16, stack=stE)
        S("dtern", [128, 22528], BF16, stack=stE)
        S("psibc", [128, 1024], stack=stE)
        S("dqbc", [128, 1024], stack=stE)
        S("hld", [128, 1024], stack=stE)
        S("hld2", [128, 1024], stack=stE)
        S("gsc", [128, 512], stack=stE)
        S("xld", [128, 512], stack=stE)
        S("sld", [128, 512], BF16, stack=stE)
        S("oev", [128, 512], stack=stE)
        sch.wait(T_, "dve")
        for tr in range(8):
            mm = T_.transpose(ps[f"pg{tr % 4}"][:, 0:128],
                              sb["mh"][:, tr * 128:(tr + 1) * 128],
                              sb["iden_f_sb"][:])
            sch.inc(mm, "pe")
            sch.wait(V, "pe")
            rv = V.tensor_reduce(sb["colx"][:, 32:33],
                                 ps[f"pg{tr % 4}"][:, 0:128],
                                 mybir.AxisListType.X, ALU.max)
            sch.inc(rv, "dve")
            V.wait_ge(sch.s("dve"), sch.now("dve"))
            V.wait_ge(sch.s("macol_r"), 1)
            p1 = V.tensor_tensor(sb["colx"][:, 33:34], sb["macol"][:, 0:1],
                                 sb["colx"][:, 16 + tr:17 + tr], ALU.mult)
            sch.inc(p1, "dve")
            V.wait_ge(sch.s("dve"), sch.now("dve"))
            p2 = V.tensor_scalar(sb["colx"][:, 34:35], sb["colx"][:, 33:34],
                                 sb["colx"][:, 33:34], sb["colx"][:, 33:34],
                                 ALU.mult, ALU.mult)
            sch.inc(p2, "dve")
            V.wait_ge(sch.s("dve"), sch.now("dve"))
            p3 = V.tensor_scalar(sb["colx"][:, 35:36], sb["colx"][:, 32:33],
                                 sb["colx"][:, 34:35], 1e-5,
                                 ALU.mult, ALU.max)
            sch.inc(p3, "dve")
            V.wait_ge(sch.s("dve"), sch.now("dve"))
            rc = V.reciprocal(sb["colx"][:, 36:37], sb["colx"][:, 35:36])
            sch.inc(rc, "dve")
            V.wait_ge(sch.s("dve"), sch.now("dve"))
            p4 = V.tensor_scalar(sb["colx"][:, 37:38], sb["colx"][:, 36:37],
                                 sb["colx"][:, 34:35], 127.0,
                                 ALU.mult, ALU.mult)
            p5 = V.tensor_scalar(sb["colx"][:, 38:39], sb["colx"][:, 35:36],
                                 sb["macol"][:, 1:2], 1.0 / 127.0,
                                 ALU.mult, ALU.mult)
            sch.inc(p5, "dve")
            V.wait_ge(sch.s("dve"), sch.now("dve"))
            sch.wait(T_, "dve")
            sch.wait(T_, "act")
            T_.matmul(ps["pg4"][:, 0:128],
                      sb["colx"][:, 37:38].broadcast_to((128, 128)),
                      sb["iden_f_sb"][:], start=True, stop=True)
            mm = T_.matmul(ps["pg5"][:, 0:128],
                           sb["colx"][:, 38:39].broadcast_to((128, 128)),
                           sb["iden_f_sb"][:], start=True, stop=True)
            sch.inc(mm, "pe")
            sch.wait(A_, "pe")
            A_.copy(sb["psibc"][:, tr * 128:(tr + 1) * 128], ps["pg4"][:, 0:128])
            ac = A_.copy(sb["dqbc"][:, tr * 128:(tr + 1) * 128],
                         ps["pg5"][:, 0:128])
            sch.inc(ac, "act")
        sch.wait(G_, "act"); sch.wait(G_, "h_dma")
        for nm in ("colx",):
            if nm in dbg_t:
                sch.wait(SY, "dve"); sch.wait(SY, "act")
                dd = SY.dma_start(out=dbg_t[nm], in_=sb[nm][:])
                sch.dinc(dd, "dbg")
        for nm in ("psibc", "dqbc"):
            if nm in dbg_t:
                sch.wait(SY, "act")
                dd = SY.dma_start(out=dbg_t[nm], in_=sb[nm][:])
                sch.dinc(dd, "dbg")
        if "nh" in dbg_t:
            sch.wait(SY, "nh_c")
            dd = SY.dma_start(out=dbg_t["nh"], in_=sb["nh"][:])
            sch.dinc(dd, "dbg")
            SY.wait_ge(sch.s("dbg"), sch.now("dbg"))
        lands = [sb["wld0"], sb["wld1"]]
        for md in range(4):
            for kc in range(44):
                li = md * 44 + kc
                land = lands[li % 2]
                if li >= 2:
                    SY.wait_ge(sch.s("dtern_c"), li - 1)
                dsem = f"dwd{li % 2}"
                dd = SY.dma_start(
                    out=land[:, 0:512],
                    in_=din["wdT"][kc * 128:(kc + 1) * 128,
                                   md * 512:(md + 1) * 512])
                sch.dinc(dd, dsem)
                V.wait_ge(sch.s(dsem), sch.now(dsem))
                V.wait_ge(sch.s("macol_r"), 1)
                mpl = sb["mplus"] if li % 2 == 0 else sb["mplus2"]
                mmn = sb["mminus"] if li % 2 == 0 else sb["mminus2"]
                if li >= 2:
                    V.wait_ge(sch.s("dtern_c"), li - 1)
                mp = V.tensor_scalar(mpl[:, 0:512], land[:, 0:512],
                                     sb["macol"][:, 4:5], None, ALU.is_ge)
                sch.inc(mp, "dve")
                V.wait_ge(sch.s("macol_r"), 1)
                gm = V.tensor_scalar(mmn[:, 0:512], land[:, 0:512],
                                     sb["macol"][:, 5:6], None, ALU.is_le)
                sch.inc(gm, "dve")
                G_.wait_ge(sch.s("dve"), sch.now("dve"))
                if md > 0:
                    G_.wait_ge(sch.s("pe"), sch.now("pe"))
                t1 = G_.tensor_tensor(sb["dtern"][:, kc * 512:(kc + 1) * 512],
                                      mpl[:, 0:512],
                                      mmn[:, 0:512], ALU.subtract)
                sch.inc(t1, "dtern_c")
                if md == 0:
                    hl = sb["hld"] if kc % 2 == 0 else sb["hld2"]
                    hsem = f"dh6{kc % 2}"
                    if kc >= 2:
                        SY.wait_ge(sch.s("nh_c"), kc - 1)
                    dd = SY.dma_start(out=hl[:],
                                      in_=scr["h"][kc * 128:(kc + 1) * 128,
                                                   :])
                    sch.dinc(dd, hsem)
                    G_.wait_ge(sch.s(hsem), sch.now(hsem))
                    gq = G_.tensor_tensor(hl[:], hl[:], sb["psibc"][:],
                                          ALU.mult)
                    sch.inc(gq, "pool")
                    V.wait_ge(sch.s("pool"), sch.now("pool"))
                    nv = V.tensor_scalar(sb["nh"][:, kc * 1024:
                                                  (kc + 1) * 1024],
                                         hl[:], MAGIC, MAGIC, ALU.add,
                                         ALU.subtract)
                    sch.inc(nv, "nh_c")
            for tokt in range(2):
                for m in range(4):
                    pgx = ps[f"pg{m}"]
                    if md + tokt > 0:
                        T_.wait_ge(sch.s("act"), sch.now("act"))
                    for kc in range(44):
                        if tokt == 0 and m == 0:
                            T_.wait_ge(sch.s("dtern_c"), md * 44 + kc + 1)
                            if md == 0:
                                T_.wait_ge(sch.s("nh_c"), kc + 1)
                        mm = T_.matmul(
                            pgx[:],
                            sb["dtern"][:, kc * 512 + m * 128:
                                        kc * 512 + (m + 1) * 128],
                            sb["nh"][:, kc * 1024 + tokt * 512:
                                     kc * 1024 + (tokt + 1) * 512],
                            start=(kc == 0), stop=(kc == 43))
                    sch.inc(mm, "pe")
                    V.wait_ge(sch.s("pe"), sch.now("pe"))
                    V.wait_ge(sch.s("act"), sch.now("act"))
                    gv = V.tensor_tensor(
                        sb["gsc"][:], ps[f"pg{m}"][:],
                        sb["dqbc"][:, tokt * 512:(tokt + 1) * 512], ALU.mult)
                    sch.inc(gv, "dve")
                    A_.wait_ge(sch.s("dve"), sch.now("dve"))
                    av = A_.activation(sb["gsc"][:], sb["gsc"][:], AF.Silu)
                    sch.inc(av, "act")
                    fc_out = md * 4 + m
                    dd = SY.dma_start(
                        out=sb["xld"][:],
                        in_=din["xhT"][fc_out * 128:(fc_out + 1) * 128,
                                       tokt * 512:(tokt + 1) * 512])
                    dd2 = SY.dma_start(
                        out=sb["sld"][:],
                        in_=scr["s"][fc_out * 128:(fc_out + 1) * 128,
                                     tokt * 512:(tokt + 1) * 512])
                    sch.dinc(dd, "dx6"); sch.dinc(dd2, "dx6")
                    V.wait_ge(sch.s("dx6"), sch.now("dx6"))
                    V.wait_ge(sch.s("act"), sch.now("act"))
                    V.wait_ge(sch.s("dout"), sch.now("dout"))
                    o1 = V.tensor_tensor(sb["oev"][:], sb["sld"][:],
                                         sb["gsc"][:], ALU.mult)
                    sch.inc(o1, "dve")
                    V.wait_ge(sch.s("dve"), sch.now("dve"))
                    ov = V.tensor_tensor(sb["oev"][:], sb["oev"][:],
                                         sb["xld"][:], ALU.add)
                    sch.inc(ov, "dve")
                    SY.wait_ge(sch.s("dve"), sch.now("dve"))
                    dd = SY.dma_start(
                        out=self.out_t[fc_out * 128:(fc_out + 1) * 128,
                                       tokt * 512:(tokt + 1) * 512],
                        in_=sb["oev"][:])
                    sch.dinc(dd, "dout")
        SY.wait_ge(sch.s("dout"), sch.now("dout"))
        for nm in ("dbg", "dba", "dxn", "dso", "dn5"):
            if sch.now(nm):
                SY.wait_ge(sch.s(nm), sch.now(nm))

    def phase3_stub(self):
        V, A_, T_, G_, SY = self.engines()
        sch, sb = self.sch, self.sb
        S = self.S
        S("zero_out", [128, HALF])
        sch.barrier([V])
        z = V.memset(sb["zero_out"][:], 0.0)
        sch.inc(z, "dve")
        SY.wait_ge(sch.s("dve"), sch.now("dve"))
        for fc in range(16):
            dd = SY.dma_start(out=self.out_t[fc * 128:(fc + 1) * 128, :],
                              in_=sb["zero_out"][:])
            sch.dinc(dd, "dout")
        SY.wait_ge(sch.s("dout"), sch.now("dout"))
        for nm in ("dbg", "dba", "dxn"):
            if sch.now(nm):
                SY.wait_ge(sch.s(nm), sch.now(nm))


def _build(dbg=()):
    b = Build(dbg=dbg)
    from contextlib import ExitStack
    with b.nc.Block() as block:
        @block.sync
        def _(sync):
            b.phase0()
            b.stat_prep()
            with ExitStack() as stM:
                b.S("silu_gg", [128, 16 * 512], BF16, stack=stM)
                with ExitStack() as stA:
                    b.load_small_weights(stA)
                    b.phase1(stA)
                    b.phase2(stA)
                    b.sch.barrier(b.engines())
                with ExitStack() as stB:
                    b.phase3(stB)
                    b.sch.barrier(b.engines())
            with ExitStack() as stC:
                b.phase4(stC)
                b.sch.barrier(b.engines())
            with ExitStack() as stP:
                for i in range(8):
                    b.P(f"pg{i}", [128, 512], stack=stP)
                b.S("colx", [128, 40], stack=stP)
                b.S("mh", [128, 1024], stack=stP)
                b.S("mplus", [128, 1024], BF16, stack=stP)
                b.S("mminus", [128, 1024], BF16, stack=stP)
                b.S("mplus2", [128, 1024], BF16, stack=stP)
                b.S("mminus2", [128, 1024], BF16, stack=stP)
                with ExitStack() as stD:
                    b.phase5(stD)
                    b.sch.barrier(b.engines())
                with ExitStack() as stD2:
                    b.phase5b(stD2)
                    b.phase5b_dbg()
                    b.sch.barrier(b.engines())
                with ExitStack() as stE:
                    b.phase6(stE)
    b.stack.close()
    return b.nc, b.dbg_t


def kernel(**inputs):
    shared, percore = _prep(inputs)
    nc, _ = _build()
    in_maps = []
    for c in range(8):
        m = dict(shared)
        m.update(percore[c])
        in_maps.append(m)
    res = run_bass_kernel_spmd(nc, in_maps, core_ids=list(range(8)))
    x = np.asarray(inputs["L0_attn_x"], np.float32)
    out = np.empty_like(x)
    for c in range(8):
        b, hg = c // 2, c % 2
        out[b, hg * HALF:(hg + 1) * HALF, :] = \
            np.asarray(res.results[c]["outT"]).T
    return out


# version 06
# speedup vs baseline: 1.1745x; 1.0052x over previous
import numpy as np
import ml_dtypes
import concourse.bass as bass
import concourse.mybir as mybir
from concourse.bass_utils import run_bass_kernel_spmd

F32 = mybir.dt.float32
BF16 = mybir.dt.bfloat16
AF = mybir.ActivationFunctionType
ALU = mybir.AluOpType

BF = ml_dtypes.bfloat16

L = 2048
D = 2048
DH = 128
FF = 5632
GU = 2 * FF
EPS = 1e-5
MAGIC = 12582912.0

NTT = 4
TT = 512
NSB = 4
SBT = 512
NCI = 8
C = 128
TAU = 64
HALF = 1024


def _prep(inputs):
    f32 = np.float32
    x = np.ascontiguousarray(np.asarray(inputs["L0_attn_x"], f32))
    Wq = np.asarray(inputs["L0_attn_Wq"], f32)
    Wk = np.asarray(inputs["L0_attn_Wk"], f32)
    Wv = np.asarray(inputs["L0_attn_Wv"], f32)
    Wb = np.asarray(inputs["L0_attn_Wb"], f32)
    Wa = np.asarray(inputs["L0_attn_Wa"], f32)
    Wg = np.asarray(inputs["L0_attn_Wg"], f32)
    Wo = np.asarray(inputs["L0_attn_Wo"], f32)
    A_log = np.asarray(inputs["L0_attn_A_log"], f32)
    dtb = np.asarray(inputs["L0_attn_dt_bias"], f32)
    cq = np.asarray(inputs["L0_attn_conv_q"], f32)
    ck = np.asarray(inputs["L0_attn_conv_k"], f32)
    cv = np.asarray(inputs["L0_attn_conv_v"], f32)
    onw = np.asarray(inputs["L0_attn_o_norm_w"], f32)
    wgu = np.asarray(inputs["L0_ffn_w_gate_up"], f32)
    wd = np.asarray(inputs["L0_ffn_w_down"], f32)
    nw = np.asarray(inputs["L0_ffn_norm_weight"], f32)

    iden = np.eye(128, dtype=f32)
    sidx = np.arange(C)
    tau = np.arange(TAU)
    shared = {
        "wguT": np.ascontiguousarray(wgu.T),
        "wdT": np.ascontiguousarray(wd.T),
        "woT": np.ascontiguousarray(Wo.T),
        "nw2": np.ascontiguousarray(nw.reshape(16, 128).T),
        "onw_tile": np.ascontiguousarray(np.tile(onw, (128, 4))),
        "iden_f": iden,
        "iden_h": np.ascontiguousarray(iden * 0.5),
        "iden_b": iden.astype(BF),
        "ones1": np.ones((1, 128), f32),
        "ones128": np.ones((128, 1), f32),
        "maskA": np.where(sidx[None, :] < sidx[:, None], 0.0, 200.0
                          ).astype(f32),
        "maskQ": np.where(sidx[:, None] <= 2 * tau[None, :] + 1, 0.0, -200.0
                          ).astype(f32),
    }

    def diag_taps(cw, rows, nchunks):
        out = np.zeros((nchunks, 4, 128, 128), f32)
        w = cw[rows]
        for mc in range(nchunks):
            for t in range(4):
                np.fill_diagonal(out[mc, t], w[mc * 128:(mc + 1) * 128, t])
        return np.ascontiguousarray(
            out.transpose(2, 0, 1, 3).reshape(128, nchunks * 4 * 128)
        ).astype(BF)

    percore = []
    for c in range(8):
        b, hg = c // 2, c % 2
        hsl = slice(hg * 512, hg * 512 + 512)
        krows = np.concatenate(
            [n * 1024 + hg * 512 + np.arange(512) for n in range(2)])
        brows = np.concatenate(
            [n * 8 + hg * 4 + np.arange(4) for n in range(2)])
        arows = np.arange(hg * 4, hg * 4 + 4)
        xb = x[b]
        pc = {
            "xT": np.ascontiguousarray(xb.T),
            "xhT": np.ascontiguousarray(xb[hg * HALF:(hg + 1) * HALF].T),
            "wqT": np.ascontiguousarray(Wq[hsl].T),
            "wkT": np.ascontiguousarray(Wk[krows].T),
            "wvT": np.ascontiguousarray(Wv[krows].T),
            "wbT": np.ascontiguousarray(Wb[brows].T),
            "waT": np.ascontiguousarray(Wa[arows].T),
            "wgT": np.ascontiguousarray(Wg[hsl].T),
            "alog": np.ascontiguousarray(A_log[arows].reshape(4, 1)),
            "dtb": np.ascontiguousarray(dtb[arows].reshape(4, 1)),
            "mask_me": np.full((128, 1), 1.0 - hg, f32),
            "mask_pa": np.full((128, 1), float(hg), f32),
            "dq": diag_taps(cq, np.arange(hg * 512, hg * 512 + 512), 4),
            "dk": diag_taps(ck, krows, 8),
            "dv": diag_taps(cv, krows, 8),
        }
        percore.append(pc)
    return shared, percore


class Sch:
    def __init__(self, nc, stack):
        self.nc = nc
        self.stack = stack
        self.sem = {}
        self.cnt = {}

    def s(self, name):
        if name not in self.sem:
            self.sem[name] = self.stack.enter_context(self.nc.semaphore(name))
            self.cnt[name] = 0
        return self.sem[name]

    def inc(self, instr, name, amt=1):
        self.s(name)
        self.cnt[name] += amt
        instr.then_inc(self.sem[name], amt)
        return self.cnt[name]

    def dinc(self, instr, name):
        return self.inc(instr, name, 16)

    def wait(self, eng, name, val=None):
        self.s(name)
        v = self.cnt[name] if val is None else val
        if v > 0:
            eng.wait_ge(self.sem[name], v)

    def now(self, name):
        self.s(name)
        return self.cnt[name]

    def barrier(self, engines):
        names = list(self.sem.keys())
        for eng in engines:
            for nm in names:
                if self.cnt[nm] > 0:
                    eng.wait_ge(self.sem[nm], self.cnt[nm])


class Const:
    def __init__(self, nc, stack, capacity=24):
        self.nc = nc
        self.tile = stack.enter_context(
            nc.sbuf_tensor("constpool", [128, capacity], F32))
        self.vals = {}
        self.cap = capacity

    def fill(self, v):
        if v not in self.vals:
            i = len(self.vals)
            assert i < self.cap
            self.vals[v] = i
            self.nc.vector.memset(self.tile[:, i:i + 1], float(v))
        return self.tile[:, self.vals[v]:self.vals[v] + 1]

    def ap(self, v):
        i = self.vals[v]
        return self.tile[:, i:i + 1]


DBG_SPECS = {
    "xn": ([D, L], BF16),
    "beta": ([8, L], F32),
    "g": ([4, L], F32),
    "silu_gg": ([128, 16 * 512], BF16),
    "G_S": ([128, 128], F32),
    "beta_S": ([128, 128], F32),
    "ehat_S": ([128, 128], F32),
    "lamC": ([128, 128], F32),
    "lamT": ([128, 128], F32),
    "ehatT": ([128, 128], F32),
    "gd": ([128, 128], F32),
    "gd2": ([128, 128], F32),
    "ogT": ([128, 8192], BF16),
    "qf": ([128, 8 * 1024], BF16),
    "kf": ([128, 8 * 2048], BF16),
    "vf": ([128, 8 * 2048], BF16),
    "LS": ([128, 128], F32),
    "U4": ([128, 512], BF16),
    "S4": ([128, 512], F32),
    "nA4": ([128, 512], BF16),
    "AT4": ([128, 256], BF16),
    "QB4": ([128, 256], BF16),
    "KH4": ([128, 512], BF16),
    "U4d": ([128, 512], BF16),
    "S4d": ([128, 512], F32),
    "Zf0": ([128, 1024], BF16),
    "PN2d": ([128, 512], BF16),
    "sT": ([D, HALF], BF16),
    "nT": ([D, HALF], BF16),
    "mh": ([128, 1024], F32),
    "macol": ([128, 8], F32),
    "fscols": ([128, 288], F32),
    "h0": ([128, 1024], F32),
    "nh": ([128, 45056], BF16),
    "guT": ([128, 1024], F32),
    "psibc": ([128, 1024], F32),
    "dqbc": ([128, 1024], F32),
    "colx": ([128, 40], F32),
    "hfull": ([5632, 1024], F32),
    "p5b_g": ([128, 512], F32),
    "p5b_u": ([128, 512], F32),
    "p5b_h": ([128, 512], F32),
    "p5b_t": ([128, 8192], BF16),
    "p5b_t2": ([128, 8192], BF16),
    "p5b_l": ([128, 1024], F32),
    "p5b_mp": ([128, 1024], BF16),
    "p5b_mm": ([128, 1024], BF16),
    "p5b_n": ([128, 16384], BF16),


}

ENG = ("vector", "scalar", "tensor", "gpsimd", "sync")


class Build:
    def __init__(self, dbg=()):
        self.nc = bass.Bass("TRN2", target_bir_lowering=False, debug=False,
                            detect_race_conditions=False)
        from contextlib import ExitStack
        self.stack = ExitStack()
        self.sch = Sch(self.nc, self.stack)
        self.cst = Const(self.nc, self.stack)
        self.din = {}
        self.sb = {}
        self.ps = {}
        self.scr = {}
        self.dbg = dbg
        self.dbg_t = {}
        self._declare()

    def I(self, name, shape, dt=F32):
        self.din[name] = self.nc.dram_tensor(
            name, shape, dt, kind="ExternalInput").ap()

    def S(self, name, shape, dt=F32, stack=None):
        (stack or self.stack).enter_context
        st = stack or self.stack
        self.sb[name] = st.enter_context(self.nc.sbuf_tensor(name, shape, dt))
        return self.sb[name]

    def P(self, name, shape, dt=F32, stack=None):
        st = stack or self.stack
        self.ps[name] = st.enter_context(self.nc.psum_tensor(name, shape, dt))
        return self.ps[name]

    def _declare(self):
        nc = self.nc
        I = self.I
        I("xT", [D, L]); I("xhT", [D, HALF])
        I("wqT", [D, 512]); I("wkT", [D, 1024]); I("wvT", [D, 1024])
        I("wbT", [D, 8]); I("waT", [D, 4]); I("wgT", [D, 512])
        I("woT", [1024, D]); I("wguT", [D, GU]); I("wdT", [FF, D])
        I("nw2", [128, 16]); I("onw_tile", [128, 512])
        I("alog", [4, 1]); I("dtb", [4, 1])
        I("iden_f", [128, 128]); I("iden_h", [128, 128])
        I("iden_b", [128, 128], BF16)
        I("ones1", [1, 128]); I("ones128", [128, 1])
        I("maskA", [128, 128]); I("maskQ", [128, TAU])
        I("mask_me", [128, 1]); I("mask_pa", [128, 1])
        I("dq", [128, 4 * 4 * 128], BF16)
        I("dk", [128, 8 * 4 * 128], BF16)
        I("dv", [128, 8 * 4 * 128], BF16)

        self.out_t = nc.dram_tensor("outT", [D, HALF], F32,
                                    kind="ExternalOutput").ap()
        for name in self.dbg:
            shape, dt = DBG_SPECS[name]
            self.dbg_t[name] = nc.dram_tensor(
                "dbg_" + name, shape, dt, kind="ExternalOutput").ap()

        scr = self.scr
        scr["xn"] = nc.dram_tensor("xn_scr", [D, L], BF16).ap()
        scr["ba"] = nc.dram_tensor("ba_scr", [12, L], F32).ap()
        scr["n"] = nc.dram_tensor("n_scr", [D, HALF], BF16).ap()
        scr["h"] = nc.dram_tensor("h_scr", [FF, HALF], F32).ap()
        scr["s"] = nc.dram_tensor("s_scr", [D, HALF], BF16).ap()
        scr["ogi"] = nc.dram_tensor("og_in", [512, L], BF16)
        scr["ogo"] = nc.dram_tensor("og_out", [1024, L], BF16)

        S = self.S
        S("iden_f_sb", [128, 128]); S("iden_h_sb", [128, 128])
        S("iden_b_sb", [128, 128], BF16)
        S("ones1_sb", [1, 128]); S("ones128_sb", [128, 1])
        S("maskA_sb", [128, 128]); S("maskQ_sb", [128, TAU])
        S("maskme_sb", [128, 1]); S("maskpa_sb", [128, 1])
        S("onw_sb", [128, 512]); S("nw_sb", [128, 16])
        S("alog_sb", [4, 1]); S("dtb_sb", [4, 1]); S("negea", [4, 1])
        for nm in ("GT", "lamT", "G_S", "beta_S", "ehat_S", "L_S", "GpS",
                   "nbrn_S", "ehr_S", "blr_S", "lamC", "bl_S", "LT_all"):
            S(nm, [128, 128])
        S("wld0", [128, 1024]); S("wld1", [128, 1024])

    def engines(self):
        nc = self.nc
        return nc.vector, nc.scalar, nc.tensor, nc.gpsimd, nc.sync

    def phase0(self):
        V, A_, T_, G_, SY = self.engines()
        sch, cst, sb, din = self.sch, self.cst, self.sb, self.din
        for v in (0.0, 1.0, 0.5, -0.5, 2.0, EPS, 1.0 / 2048.0, MAGIC,
                  1e-24, 1e-5, 127.0, 1.0 / 127.0, 0.03577708764, 200.0,
                  -200.0, 1.0 / 128.0, -1.0):
            cst.fill(v)
        ld = V.memset(cst.tile[:, 23:24], 0.0)
        sch.inc(ld, "dve")

        small = [("iden_f", "iden_f_sb"), ("iden_h", "iden_h_sb"),
                 ("iden_b", "iden_b_sb"),
                 ("ones1", "ones1_sb"), ("ones128", "ones128_sb"),
                 ("maskA", "maskA_sb"), ("maskQ", "maskQ_sb"),
                 ("mask_me", "maskme_sb"), ("mask_pa", "maskpa_sb"),
                 ("onw_tile", "onw_sb"), ("nw2", "nw_sb"),
                 ("alog", "alog_sb"), ("dtb", "dtb_sb"),
                 ("dq", "dq_sb"), ("dk", "dk_sb"), ("dv", "dv_sb")]
        for src, dst in small:
            if dst not in sb:
                continue
            d = SY.dma_start(out=sb[dst][:], in_=din[src])
            sch.dinc(d, "dsm")
        self.n_small = sch.now("dsm")

        A_.wait_ge(sch.s("dsm"), self.n_small)
        e1 = A_.activation(sb["negea"][:], sb["alog_sb"][:], AF.Exp)
        sch.inc(e1, "act")
        V.wait_ge(sch.s("act"), sch.now("act"))
        sch.wait(V, "dve")
        n1 = V.tensor_single_scalar(sb["negea"][:], sb["negea"][:], -1.0,
                                    ALU.mult)
        sch.inc(n1, "dve")

    def phase1(self, stA):
        V, A_, T_, G_, SY = self.engines()
        sch, cst, sb, ps, din = self.sch, self.cst, self.sb, self.ps, self.din
        scr, dbg_t = self.scr, self.dbg_t
        S = self.S
        for nm in ("pb0", "pb1", "pb2", "pb3"):
            self.P(nm, [128, 512], stack=stA)
        S("x_sb", [128, 16 * TT], stack=stA)
        S("sq0", [128, TT], stack=stA); S("sq1", [128, TT], stack=stA)
        S("xn_t", [128, 16 * TT], BF16, stack=stA)
        S("rows", [1, 6 * TT], stack=stA)
        S("beta_sb", [8, L], stack=stA); S("g_sb", [4, L], stack=stA)
        S("spA", [4, TT], stack=stA); S("spB", [4, TT], stack=stA)

        R = sb["rows"]
        def row(i):
            return R[0:1, i * TT:(i + 1) * TT]

        sch.wait(T_, "wcast")

        for tt in range(NTT):
            if tt > 0:
                SY.wait_ge(sch.s("pool"), sch.now("pool"))
                SY.wait_ge(sch.s("dve"), sch.now("dve"))
                SY.wait_ge(sch.s("pe"), sch.now("pe"))
            d = SY.dma_start(
                out=sb["x_sb"][:].rearrange("p (f t) -> p f t", f=16),
                in_=din["xT"][:, tt * TT:(tt + 1) * TT].rearrange(
                    "(f p) t -> p f t", p=128))
            sch.dinc(d, "dx")
            T_.wait_ge(sch.s("dx"), sch.now("dx"))
            A_.wait_ge(sch.s("dx"), sch.now("dx"))
            sqb = [sb["sq0"], sb["sq1"]]
            for fc in range(16):
                xc = sb["x_sb"][:, fc * TT:(fc + 1) * TT]
                tgt = sch.now("pe") - 1
                if tgt > 0:
                    A_.wait_ge(sch.s("pe"), tgt)
                q = A_.activation(sqb[fc % 2][:], xc, AF.Square)
                sch.inc(q, "act")
                T_.matmul(ps["pb0"][0:1, :], sb["ones128_sb"][:], xc,
                          start=(fc == 0), stop=(fc == 15))
                T_.wait_ge(sch.s("act"), sch.now("act"))
                mm = T_.matmul(ps["pb1"][0:1, :], sb["ones128_sb"][:],
                               sqb[fc % 2][:], start=(fc == 0),
                               stop=(fc == 15))
                sch.inc(mm, "pe")
            A_.wait_ge(sch.s("pe"), sch.now("pe"))
            m1 = A_.activation(row(0), ps["pb0"][0:1, :], AF.Copy,
                               scale=1.0 / 2048.0)
            m2 = A_.activation(row(2), ps["pb1"][0:1, :], AF.Copy,
                               scale=1.0 / 2048.0)
            sch.inc(m2, "act")
            V.wait_ge(sch.s("act"), sch.now("act"))
            V.tensor_tensor(row(4), row(0), row(0), ALU.mult)
            V.tensor_tensor(row(4), row(2), row(4), ALU.subtract)
            v3 = V.tensor_single_scalar(row(4), row(4), EPS, ALU.add)
            sch.inc(v3, "dve")
            A_.wait_ge(sch.s("dve"), sch.now("dve"))
            s1 = A_.activation(row(1), row(4), AF.Sqrt)
            sch.inc(s1, "act")
            V.wait_ge(sch.s("act"), sch.now("act"))
            r1 = V.reciprocal(row(1), row(1))
            sch.inc(r1, "dve")
            T_.wait_ge(sch.s("dve"), sch.now("dve"))
            T_.matmul(ps["pb2"][:], sb["ones1_sb"][:], row(0), start=True,
                      stop=True)
            bb = T_.matmul(ps["pb3"][:], sb["ones1_sb"][:], row(1),
                           start=True, stop=True)
            sch.inc(bb, "pe")
            A_.wait_ge(sch.s("pe"), sch.now("pe"))
            A_.copy(sb["sq0"][:], ps["pb2"][:])
            c2 = A_.copy(sb["sq1"][:], ps["pb3"][:])
            sch.inc(c2, "act")
            G_.wait_ge(sch.s("act"), sch.now("act"))
            for fc in range(16):
                xc = sb["x_sb"][:, fc * TT:(fc + 1) * TT]
                G_.tensor_tensor(xc, xc, sb["sq0"][:], ALU.subtract)
                g2 = G_.tensor_tensor(sb["xn_t"][:, fc * TT:(fc + 1) * TT],
                                      xc, sb["sq1"][:], ALU.mult)
                sch.inc(g2, "pool")
            SY.wait_ge(sch.s("pool"), sch.now("pool"))
            d2 = SY.dma_start(
                out=scr["xn"][:, tt * TT:(tt + 1) * TT].rearrange(
                    "(f p) t -> p f t", p=128),
                in_=sb["xn_t"][:].rearrange("p (f t) -> p f t", f=16))
            sch.dinc(d2, "dxn")
            T_.wait_ge(sch.s("pool"), sch.now("pool"))
            T_.wait_ge(sch.s("act"), sch.now("act"))
            for fc in range(16):
                xnc = sb["xn_t"][:, fc * TT:(fc + 1) * TT]
                T_.matmul(ps["pb0"][0:8, :],
                          sb["wba1_sb"][:, fc * 8:(fc + 1) * 8],
                          xnc, start=(fc == 0), stop=(fc == 15))
                mm = T_.matmul(ps["pb1"][0:4, :],
                               sb["wba2_sb"][:, fc * 4:(fc + 1) * 4],
                               xnc, start=(fc == 0), stop=(fc == 15))
                sch.inc(mm, "pe")
            A_.wait_ge(sch.s("pe"), sch.now("pe"))
            a0 = A_.activation(sb["beta_sb"][:, tt * TT:(tt + 1) * TT],
                               ps["pb0"][0:8, :], AF.Sigmoid)
            sch.inc(a0, "act")
            V.wait_ge(sch.s("act"), sch.now("act"))
            a1 = V.tensor_scalar(sb["spA"][:], ps["pb1"][0:4, :],
                                 sb["dtb_sb"][:], None, ALU.add)
            b2m = V.tensor_single_scalar(
                sb["beta_sb"][:, tt * TT:(tt + 1) * TT],
                sb["beta_sb"][:, tt * TT:(tt + 1) * TT], 2.0, ALU.mult)
            sch.inc(b2m, "dve")
            A_.wait_ge(sch.s("dve"), sch.now("dve"))
            A_.activation(sb["spB"][:], sb["spA"][:], AF.Abs)
            A_.activation(sb["spB"][:], sb["spB"][:], AF.Exp, scale=-1.0)
            u3 = A_.activation(sb["spB"][:], sb["spB"][:], AF.Ln,
                               bias=cst.ap(1.0)[0:4, :])
            sch.inc(u3, "act")
            V.wait_ge(sch.s("act"), sch.now("act"))
            V.tensor_single_scalar(sb["spA"][:], sb["spA"][:], 0.0, ALU.max)
            V.tensor_tensor(sb["spA"][:], sb["spA"][:], sb["spB"][:], ALU.add)
            u6 = V.tensor_scalar(sb["g_sb"][:, tt * TT:(tt + 1) * TT],
                                 sb["spA"][:], sb["negea"][:], None, ALU.mult)
            sch.inc(u6, "dve")
            for st in range(4):
                pg = ps["pb2"] if st % 2 == 0 else ps["pb3"]
                T_.wait_ge(sch.s("act"), sch.now("act"))
                T_.wait_ge(sch.s("dve"), sch.now("dve"))
                for fc in range(16):
                    xnc = sb["xn_t"][:, fc * TT + st * 128:
                                     fc * TT + st * 128 + 128]
                    mm = T_.matmul(pg[:], xnc,
                                   sb["wg_sb"][:, fc * 512:(fc + 1) * 512],
                                   start=(fc == 0), stop=(fc == 15))
                sch.inc(mm, "pe")
                A_.wait_ge(sch.s("pe"), sch.now("pe"))
                A_.wait_ge(sch.s("dve"), sch.now("dve"))
                scrt = sb["sq0"] if st % 2 == 0 else sb["sq1"]
                sl = A_.activation(scrt[:], pg[:], AF.Silu)
                sch.inc(sl, "act")
                V.wait_ge(sch.s("act"), sch.now("act"))
                gw = V.tensor_tensor(
                    sb["silu_gg"][:,
                                  (tt * 4 + st) * 512:(tt * 4 + st + 1) * 512],
                    scrt[:], sb["onw_sb"][:], ALU.mult)
                sch.inc(gw, "dve")

        SY.wait_ge(sch.s("dve"), sch.now("dve"))
        d = SY.dma_start(out=scr["ba"][0:8, :], in_=sb["beta_sb"][:])
        sch.dinc(d, "dba")
        d = SY.dma_start(out=scr["ba"][8:12, :], in_=sb["g_sb"][:])
        sch.dinc(d, "dba")

        for nm, src in (("beta", sb["beta_sb"][:]), ("g", sb["g_sb"][:]),
                        ("silu_gg", sb["silu_gg"][:])):
            if nm in dbg_t:
                dd = SY.dma_start(out=dbg_t[nm], in_=src)
                sch.dinc(dd, "dbg")
        if "xn" in dbg_t:
            SY.wait_ge(sch.s("dxn"), sch.now("dxn"))
            dd = SY.dma_start(out=dbg_t["xn"], in_=scr["xn"][:, :])
            sch.dinc(dd, "dbg")

    def load_small_weights(self, stA):
        V, A_, T_, G_, SY = self.engines()
        sch, sb, din = self.sch, self.sb, self.din
        S = self.S
        S("wba1_sb", [128, 16 * 8], BF16, stack=stA)
        S("wba2_sb", [128, 16 * 4], BF16, stack=stA)
        S("wg_sb", [128, 16 * 512], BF16, stack=stA)
        lands = [sb["wld0"], sb["wld1"]]
        widx = 0
        jobs = [("wgT", "wg_sb", 512), ("wbT", "wba1_sb", 8),
                ("waT", "wba2_sb", 4)]
        for name, dst, ncol in jobs:
            for kc in range(16):
                land = lands[widx % 2]
                if widx >= 2:
                    SY.wait_ge(sch.s("wcast"), widx - 1)
                d = SY.dma_start(out=land[:, 0:ncol],
                                 in_=din[name][kc * 128:(kc + 1) * 128, :])
                sem = f"dwm{widx % 2}"
                sch.dinc(d, sem)
                V.wait_ge(sch.s(sem), sch.now(sem))
                cc = V.tensor_copy(sb[dst][:, kc * ncol:(kc + 1) * ncol],
                                   land[:, 0:ncol])
                sch.inc(cc, "wcast")
                widx += 1
        self.widx = widx

    def phase2(self, stA):
        V, A_, T_, G_, SY = self.engines()
        sch, cst, sb, ps = self.sch, self.cst, self.sb, self.ps
        scr, dbg_t = self.scr, self.dbg_t
        S = self.S
        S("g0", [128, 128], stack=stA)
        S("bTt", [128, 128], stack=stA)

        S("gstg", [128, 192], stack=stA)
        z = V.memset(sb["g0"][:], 0.0)
        sch.inc(z, "dve")
        G_.wait_ge(sch.s("dba"), sch.now("dba"))
        for j in range(4):
            d = G_.dma_start(
                out=sb["gstg"][j * 32:(j + 1) * 32, 0:64],
                in_=scr["ba"][8 + j:9 + j, :].rearrange(
                    "r (i t) -> (r i) t", t=64))
            sch.dinc(d, "dgr")
        for n in range(2):
            for j in range(4):
                d = G_.dma_start(
                    out=sb["gstg"][j * 32:(j + 1) * 32,
                                   64 + n * 64:128 + n * 64],
                    in_=scr["ba"][n * 4 + j:n * 4 + j + 1, :].rearrange(
                        "r (i t) -> (r i) t", t=64))
                sch.dinc(d, "dgr")
        V.wait_ge(sch.s("dgr"), sch.now("dgr"))
        V.tensor_copy(sb["g0"][:].rearrange("p (t n) -> p n t", n=2)[:, 0, :],
                      sb["gstg"][:, 0:64])
        V.tensor_copy(sb["bTt"][:].rearrange("p (t n) -> p n t", n=2)[:, 0, :],
                      sb["gstg"][:, 64:128])
        cpy = V.tensor_copy(
            sb["bTt"][:].rearrange("p (t n) -> p n t", n=2)[:, 1, :],
            sb["gstg"][:, 128:192])
        sch.inc(cpy, "dve")
        sc = V.tensor_tensor_scan(sb["GT"][:], sb["g0"][:], sb["g0"][:],
                                  0.0, ALU.add, ALU.bypass)
        sch.inc(sc, "dve")
        sch.wait(T_, "dve")
        T_.transpose(ps["pb0"][:, 0:128], sb["GT"][:], sb["iden_f_sb"][:])
        t2 = T_.transpose(ps["pb1"][:, 0:128], sb["bTt"][:],
                          sb["iden_f_sb"][:])
        sch.inc(t2, "pe")
        sch.wait(A_, "pe")
        A_.copy(sb["G_S"][:], ps["pb0"][:, 0:128])
        e2 = A_.copy(sb["beta_S"][:], ps["pb1"][:, 0:128])
        sch.inc(e2, "act")
        gc_col = sb["GT"][:, 127:128]
        V.wait_ge(sch.s("dve"), sch.now("dve"))
        v1 = V.tensor_scalar(sb["g0"][:], sb["GT"][:], gc_col, None,
                             ALU.subtract)
        sch.inc(v1, "dve")
        if "gd2" in dbg_t:
            sch.wait(SY, "dve")
            dd = SY.dma_start(out=dbg_t["gd2"], in_=sb["g0"][:])
            sch.dinc(dd, "dbg")
            A_.wait_ge(sch.s("dbg"), sch.now("dbg"))
        sch.wait(A_, "dve")
        A_.activation(sb["g0"][:], sb["g0"][:], AF.Exp, scale=-1.0)
        a2 = A_.activation(sb["lamT"][:], sb["GT"][:], AF.Exp)
        sch.inc(a2, "act")
        sch.wait(T_, "act")
        t3 = T_.transpose(ps["pb2"][:, 0:128], sb["g0"][:],
                          sb["iden_f_sb"][:])
        t4 = T_.matmul(ps["pb3"][:, 0:128],
                       sb["lamT"][:, 127:128].broadcast_to((128, 128)),
                       sb["iden_f_sb"][:], start=True, stop=True)
        sch.inc(t4, "pe")
        sch.wait(A_, "pe")
        A_.copy(sb["ehat_S"][:], ps["pb2"][:, 0:128])
        e4 = A_.copy(sb["lamC"][:], ps["pb3"][:, 0:128])
        sch.inc(e4, "act")
        sch.wait(V, "act")
        lmS = V.tensor_scalar(sb["bl_S"][:], sb["G_S"][:], 0.0, None, ALU.add)
        sch.inc(lmS, "dve")
        sch.wait(A_, "dve")
        a5 = A_.activation(sb["bl_S"][:], sb["bl_S"][:], AF.Exp)
        sch.inc(a5, "act")
        sch.wait(V, "act")
        v5 = V.tensor_tensor(sb["bl_S"][:], sb["bl_S"][:], sb["beta_S"][:],
                             ALU.mult)
        sch.inc(v5, "dve")

        for nm, src in (("ehatT", sb["g0"]), ("gd", sb["GT"]),
                        ("G_S", sb["G_S"]), ("beta_S", sb["beta_S"]),
                        ("ehat_S", sb["ehat_S"]), ("lamC", sb["lamC"]),
                        ("lamT", sb["lamT"])):
            if nm in dbg_t:
                sch.wait(SY, "act"); sch.wait(SY, "dve")
                dd = SY.dma_start(out=dbg_t[nm], in_=src[:])
                sch.dinc(dd, "dbg")


    def phase3(self, stB):
        V, A_, T_, G_, SY = self.engines()
        nc = self.nc
        sch, cst, sb, ps, din = self.sch, self.cst, self.sb, self.ps, self.din
        scr, dbg_t = self.scr, self.dbg_t
        S = self.S
        P = self.P
        S("wq3", [128, 16 * 512], BF16, stack=stB)
        S("wk3", [128, 16 * 1024], BF16, stack=stB)
        S("wv3", [128, 16 * 1024], BF16, stack=stB)
        S("dq_sb", [128, 4 * 4 * 128], BF16, stack=stB)
        S("dk_sb", [128, 8 * 4 * 128], BF16, stack=stB)
        S("dv_sb", [128, 8 * 4 * 128], BF16, stack=stB)
        S("xn3", [128, 16 * SBT], BF16, stack=stB)
        S("qf", [128, 4 * SBT], BF16, stack=stB)
        S("kf", [128, 4 * 2 * SBT], BF16, stack=stB)
        S("vf", [128, 4 * 2 * SBT], BF16, stack=stB)
        S("ring", [128, SBT + 4], BF16, stack=stB)
        S("tails", [128, 60], BF16, stack=stB)
        S("rn_S", [128, 128], stack=stB)
        S("Za", [128, 1024], BF16, stack=stB)
        S("Zb", [128, 1024], BF16, stack=stB)
        S("PAa", [128, 512], BF16, stack=stB)
        S("PAb", [128, 512], BF16, stack=stB)
        S("PNa", [128, 512], BF16, stack=stB)
        S("PNb", [128, 512], BF16, stack=stB)
        S("nA4", [128, 512], BF16, stack=stB)
        S("KH4", [128, 512], BF16, stack=stB)
        S("AT4", [128, 256], BF16, stack=stB)
        S("QB4", [128, 256], BF16, stack=stB)
        S("nTW4", [128, 512], BF16, stack=stB)
        S("U4", [128, 512], BF16, stack=stB)
        S("S4", [128, 512], stack=stB)
        S("Sb4", [128, 512], BF16, stack=stB)
        S("sc0", [128, 512], stack=stB)
        S("sc1", [128, 256], stack=stB)
        S("sqk", [128, 128], stack=stB)
        S("osb", [64, 512], stack=stB)
        S("osq", [64, 128], stack=stB)
        S("ocol", [64, 12], stack=stB)
        S("ogbf", [64, 512], BF16, stack=stB)
        S("ggs", [64, 512], BF16, stack=stB)
        S("ogst", [128, 256], BF16, stack=stB)
        P("pz", [128, 1024], stack=stB)
        P("pq1", [128, 512], stack=stB)
        P("pq2", [128, 512], stack=stB)
        P("pq3", [128, 512], stack=stB)
        P("ptAB", [128, 1024], BF16, stack=stB)
        P("ptCD", [128, 1024], BF16, stack=stB)
        P("pO", [64, 512], stack=stB)

        iden = sb["iden_f_sb"]

        lands = [sb["wld0"], sb["wld1"]]
        widx = [self.widx]
        def wload(name, dst, ncol, parts):
            for kc in range(16):
                for p_ in range(parts):
                    land = lands[widx[0] % 2]
                    if widx[0] >= 2:
                        SY.wait_ge(sch.s("wcast"), widx[0] - 1)
                    cw = ncol // parts
                    d = SY.dma_start(
                        out=land[:, 0:cw],
                        in_=din[name][kc * 128:(kc + 1) * 128,
                                      p_ * cw:(p_ + 1) * cw])
                    sem = f"dwm{widx[0] % 2}"
                    sch.dinc(d, sem)
                    A_.wait_ge(sch.s(sem), sch.now(sem))
                    cc = A_.copy(
                        sb[dst][:, kc * ncol + p_ * cw:kc * ncol + (p_ + 1) * cw],
                        land[:, 0:cw])
                    sch.inc(cc, "wcast")
                    widx[0] += 1
        wload("wqT", "wq3", 512, 1)
        wload("wkT", "wk3", 1024, 1)
        wload("wvT", "wv3", 1024, 1)
        for nm, dst in (("dq", "dq_sb"), ("dk", "dk_sb"), ("dv", "dv_sb")):
            d = SY.dma_start(out=sb[dst][:], in_=din[nm])
            sch.dinc(d, "dsm")

        z1 = V.memset(sb["tails"][:], 0.0)
        z2 = V.memset(sb["S4"][:], 0.0)
        z3 = V.memset(sb["ocol"][:], 0.0)
        V.memset(sb["L_S"][:], 0.0)
        z4 = V.memset(sb["rn_S"][:], 0.0)
        sch.inc(z4, "dve")
        sch.wait(T_, "wcast"); sch.wait(T_, "dsm"); sch.wait(T_, "dve")
        sch.wait(G_, "dve")

        kf3 = sb["kf"]
        vf3 = sb["vf"]
        qf3 = sb["qf"][:].rearrange("p (j t) -> p j t", j=4)

        for sbk in range(NSB):
            self._sb_proj(sbk)
            for ci in range(NCI):
                self._chunk_iter(sbk, ci, kf3, vf3, qf3)
                self.stat_emit(9)
            if "qf" in dbg_t:
                sch.wait(SY, "act")
                dd = SY.dma_start(out=dbg_t["qf"][:, sbk * 2048:(sbk + 1) * 2048],
                                  in_=sb["qf"][:])
                sch.dinc(dd, "dbg")
                dd = SY.dma_start(out=dbg_t["kf"][:, sbk * 4096:(sbk + 1) * 4096],
                                  in_=sb["kf"][:])
                sch.dinc(dd, "dbg")
                dd = SY.dma_start(out=dbg_t["vf"][:, sbk * 4096:(sbk + 1) * 4096],
                                  in_=sb["vf"][:])
                sch.dinc(dd, "dbg")
                SY.wait_ge(sch.s("dbg"), sch.now("dbg"))
        for nm, src in (("LS", sb["L_S"]),
                        ("S4", sb["S4"])):
            if nm in dbg_t:
                sch.wait(SY, "act"); sch.wait(SY, "dve"); sch.wait(SY, "pe")
                dd = SY.dma_start(out=dbg_t[nm], in_=src[:])
                sch.dinc(dd, "dbg")

    def _sb_proj(self, sbk):
        V, A_, T_, G_, SY = self.engines()
        sch, sb, ps, din = self.sch, self.sb, self.ps, self.din
        scr = self.scr
        SY.wait_ge(sch.s("pe"), sch.now("pe"))
        d = SY.dma_start(
            out=sb["xn3"][:].rearrange("p (f t) -> p f t", f=16),
            in_=scr["xn"][:, sbk * SBT:(sbk + 1) * SBT].rearrange(
                "(f p) t -> p f t", p=128))
        sch.dinc(d, "dx3")
        T_.wait_ge(sch.s("dx3"), sch.now("dx3"))
        jobs = []
        for j in range(4):
            jobs.append(("wq3", 512, j, "dq_sb", 4, sb["qf"], j * SBT, j))
        for mc in range(8):
            n, j = mc // 4, mc % 4
            jobs.append(("wk3", 1024, mc, "dk_sb", 8, ("kf", j, n), None, mc))
        for mc in range(8):
            n, j = mc // 4, mc % 4
            jobs.append(("wv3", 1024, mc, "dv_sb", 8, ("vf", j, n), None, mc))
        tails_off = 0
        for jb_i, (wname, wcols, mc, dname, nch, dst, doff, tidx) in \
                enumerate(jobs):
            tail_ix = jb_i * 3
            pg = (ps["pq1"] if jb_i % 2 == 0 else ps["pq2"])
            pc = (ps["pq3"][:, 0:SBT] if jb_i % 2 == 0
                  else ps["pz"][:, 0:SBT])
            T_.wait_ge(sch.s("act"), sch.now("act"))
            for kc in range(16):
                mm = T_.matmul(
                    pg[:, 0:SBT] if pg.shape[1] > SBT else pg[:, :],
                    sb[wname][:, kc * wcols + mc * 128:
                              kc * wcols + (mc + 1) * 128],
                    sb["xn3"][:, kc * SBT:(kc + 1) * SBT],
                    start=(kc == 0), stop=(kc == 15))
            sch.inc(mm, "pe")
            A_.wait_ge(sch.s("pe"), sch.now("pe"))
            if sch.now("pe2") > 0:
                A_.wait_ge(sch.s("pe2"), sch.now("pe2"))
            A_.wait_ge(sch.s("dve"), sch.now("dve"))
            ev = A_.copy(sb["ring"][:, 3:3 + SBT], pg[:, 0:SBT])
            sch.inc(ev, "act")
            V.wait_ge(sch.s("act"), sch.now("act"))
            hv = V.tensor_copy(sb["ring"][:, 0:3],
                               sb["tails"][:, tail_ix:tail_ix + 3])
            sch.inc(hv, "dve")
            T_.wait_ge(sch.s("dve"), sch.now("dve"))
            T_.wait_ge(sch.s("act"), sch.now("act"))
            for tap in range(4):
                mm2 = T_.matmul(
                    pc,
                    sb[dname][:, (mc * 4 + tap) * 128:(mc * 4 + tap + 1) * 128],
                    sb["ring"][:, tap:tap + SBT],
                    start=(tap == 0), stop=(tap == 3))
            sch.inc(mm2, "pe2")
            V.wait_ge(sch.s("pe2"), sch.now("pe2"))
            tl = V.tensor_copy(sb["tails"][:, tail_ix:tail_ix + 3],
                               sb["ring"][:, SBT:SBT + 3])
            sch.inc(tl, "dve")
            A_.wait_ge(sch.s("pe2"), sch.now("pe2"))
            if doff is None:
                tname, jj, nn = dst
                dap = sb[tname][:, jj * 2 * SBT:(jj + 1) * 2 * SBT].rearrange(
                    "p (t n) -> p n t", n=2)[:, nn, :]
            else:
                dap = dst[:, doff:doff + SBT]
            sl = A_.activation(dap, pc, AF.Silu)
            sch.inc(sl, "act")

    def _chunk_iter(self, sbk, ci, kf3, vf3, qf3):
        V, A_, T_, G_, SY = self.engines()
        sch, cst, sb, ps = self.sch, self.cst, self.sb, self.ps
        scr = self.scr
        i_glob = sbk * NCI + ci
        t0 = ci * TAU
        iden = sb["iden_f_sb"]
        idb = sb["iden_b_sb"]
        idh = sb["iden_h_sb"]

        insts = [j * 32 + i_glob for j in range(4)]

        def kch(j):
            return kf3[:, j * 2 * SBT + 2 * t0:j * 2 * SBT + 2 * t0 + 128]

        def vch(j):
            return vf3[:, j * 2 * SBT + 2 * t0:j * 2 * SBT + 2 * t0 + 128]

        def qch(j):
            return qf3[:, j, t0:t0 + TAU]

        sch.wait(T_, "act")
        sch.wait(T_, "dve")
        for j in range(4):
            T_.transpose(ps["ptAB"][:, j * 128:(j + 1) * 128], kch(j), idb[:, :])
            T_.transpose(ps["ptAB"][:, 512 + j * 128:512 + (j + 1) * 128], vch(j), idb[:, :])
            mm = T_.transpose(ps["ptCD"][0:64, j * 128:(j + 1) * 128], qch(j), idb[:, :])
        sch.inc(mm, "pe")
        sch.wait(A_, "pe")
        for j in range(4):
            a = A_.activation(sb["sqk"][:], ps["ptAB"][:, j * 128:(j + 1) * 128],
                              AF.Square,
                              accum_out=sb["L_S"][:, insts[j]:insts[j] + 1])
            a2 = A_.activation(sb["osq"][:],
                               ps["ptCD"][0:64, j * 128:(j + 1) * 128],
                               AF.Square, scale=0.03577708764,
                               accum_out=sb["ocol"][0:64, 4 + j:5 + j])
        sch.inc(a2, "act")
        cs = sb["L_S"][:].rearrange("p (a i) -> p a i", i=32)[:, :, i_glob]
        rn = sb["rn_S"][:].rearrange("p (a i) -> p a i", i=32)[:, :, i_glob]
        gp = sb["GpS"][:].rearrange("p (a i) -> p a i", i=32)[:, :, i_glob]
        gs = sb["G_S"][:].rearrange("p (a i) -> p a i", i=32)[:, :, i_glob]
        nb = sb["nbrn_S"][:].rearrange("p (a i) -> p a i", i=32)[:, :, i_glob]
        bs = sb["beta_S"][:].rearrange("p (a i) -> p a i", i=32)[:, :, i_glob]
        er = sb["ehr_S"][:].rearrange("p (a i) -> p a i", i=32)[:, :, i_glob]
        es = sb["ehat_S"][:].rearrange("p (a i) -> p a i", i=32)[:, :, i_glob]
        br = sb["blr_S"][:].rearrange("p (a i) -> p a i", i=32)[:, :, i_glob]
        bl = sb["bl_S"][:].rearrange("p (a i) -> p a i", i=32)[:, :, i_glob]
        sch.wait(V, "act")
        u = V.tensor_single_scalar(cs, cs, 1e-24, ALU.max)
        sch.inc(u, "dve")
        sch.wait(A_, "dve")
        ln = A_.activation(cs, cs, AF.Ln)
        rne = A_.activation(rn, cs, AF.Exp, scale=-0.5)
        sch.inc(rne, "act")
        sch.wait(V, "act")
        V.scalar_tensor_tensor(nb, bs, -1.0, rn, ALU.mult, ALU.mult)
        V.tensor_tensor(er, es, rn, ALU.mult)
        lv = V.tensor_tensor(br, bl, rn, ALU.mult)
        sch.inc(lv, "dve")
        sch.wait(T_, "dve"); sch.wait(T_, "act")
        tl = T_.transpose(ps["pq3"][:, 0:128], sb["rn_S"][:], iden[:, :])
        sch.inc(tl, "pe")
        sch.wait(A_, "pe")
        le = A_.copy(sb["LT_all"][:], ps["pq3"][:, 0:128])
        sch.inc(le, "act")

        sch.wait(T_, "act")
        for j in range(4):
            T_.matmul(ps["pq1"][:, j * 128:(j + 1) * 128], kch(j), kch(j),
                      start=True, stop=True)
            T_.matmul(ps["pq2"][:, j * 128:(j + 1) * 128],
                      iden[:, insts[j]:insts[j] + 1].broadcast_to((128, 128)),
                      sb["GT"][:], start=True, stop=True)
            mm = T_.matmul(ps["pz"][:, 512 + j * 128:512 + (j + 1) * 128],
                           iden[:, insts[j]:insts[j] + 1].broadcast_to(
                               (128, 128)),
                           sb["LT_all"][:], start=True, stop=True)
        sch.inc(mm, "pe")
        sch.wait(V, "pe")
        for j in range(4):
            st1 = V.scalar_tensor_tensor(
                sb["sc0"][:, j * 128:(j + 1) * 128],
                ps["pq2"][:, j * 128:(j + 1) * 128],
                sb["G_S"][:, insts[j]:insts[j] + 1],
                sb["maskA_sb"][:], ALU.subtract, ALU.max)
        sch.inc(st1, "dve")
        sch.wait(A_, "dve")
        ae = A_.activation(sb["sc0"][:], sb["sc0"][:], AF.Exp, scale=-1.0)
        sch.inc(ae, "act")
        sch.wait(V, "act")
        for j in range(4):
            st2 = V.tensor_tensor(
                sb["sc0"][:, j * 128:(j + 1) * 128],
                sb["sc0"][:, j * 128:(j + 1) * 128],
                ps["pz"][:, 512 + j * 128:512 + (j + 1) * 128], ALU.mult)
        V.wait_ge(sch.s("dve"), sch.now("dve"))
        for j in range(4):
            st2 = V.scalar_tensor_tensor(
                sb["nA4"][:, j * 128:(j + 1) * 128],
                sb["sc0"][:, j * 128:(j + 1) * 128],
                sb["nbrn_S"][:, insts[j]:insts[j] + 1],
                ps["pq1"][:, j * 128:(j + 1) * 128], ALU.mult, ALU.mult)
        sch.inc(st2, "dve")

        sch.wait(T_, "dve")
        lam_odd = sb["lamT"][:].rearrange("p (t n) -> p t n", n=2)[:, :, 1]
        gt_odd = sb["GT"][:].rearrange("p (t n) -> p t n", n=2)[:, :, 1]
        for j in range(4):
            T_.matmul(ps["pq3"][:, j * 64:(j + 1) * 64], kch(j), qch(j),
                      start=True, stop=True)
            T_.matmul(ps["pq3"][:, 256 + j * 64:256 + (j + 1) * 64],
                      iden[:, insts[j]:insts[j] + 1].broadcast_to((128, 128)),
                      gt_odd, start=True, stop=True)
            mm = T_.matmul(ps["pz"][:, j * 64:(j + 1) * 64],
                           iden[:, insts[j]:insts[j] + 1].broadcast_to(
                               (128, 128)),
                           lam_odd, start=True, stop=True)
        sch.inc(mm, "pe")
        sch.wait(V, "pe")
        for j in range(4):
            st3 = V.scalar_tensor_tensor(
                sb["sc1"][:, j * 64:(j + 1) * 64],
                ps["pq3"][:, 256 + j * 64:256 + (j + 1) * 64],
                sb["G_S"][:, insts[j]:insts[j] + 1],
                sb["maskQ_sb"][:], ALU.subtract, ALU.min)
        sch.inc(st3, "dve")
        sch.wait(A_, "dve")
        aq = A_.activation(sb["sc1"][:], sb["sc1"][:], AF.Exp)
        sch.inc(aq, "act")
        sch.wait(V, "act")
        for j in range(4):
            tt1 = V.scalar_tensor_tensor(
                sb["AT4"][:, j * 64:(j + 1) * 64],
                sb["sc1"][:, j * 64:(j + 1) * 64],
                sb["rn_S"][:, insts[j]:insts[j] + 1],
                ps["pq3"][:, j * 64:(j + 1) * 64], ALU.mult, ALU.mult)
            tt2 = V.tensor_tensor(sb["QB4"][:, j * 64:(j + 1) * 64],
                                  qch(j),
                                  ps["pz"][:, j * 64:(j + 1) * 64],
                                  ALU.mult)
        sch.inc(tt2, "dve")

        sch.wait(A_, "dve")
        for j in range(4):
            A_.activation(sb["Za"][:, j * 256:j * 256 + 128],
                          ps["ptAB"][:, 512 + j * 128:512 + (j + 1) * 128], AF.Copy,
                          scale=sb["beta_S"][:, insts[j]:insts[j] + 1])
            A_.activation(sb["Za"][:, j * 256 + 128:(j + 1) * 256],
                          ps["ptAB"][:, j * 128:(j + 1) * 128], AF.Copy,
                          scale=sb["blr_S"][:, insts[j]:insts[j] + 1])
            kh = A_.activation(sb["KH4"][:, j * 128:(j + 1) * 128],
                               ps["ptAB"][:, j * 128:(j + 1) * 128], AF.Copy,
                               scale=sb["ehr_S"][:, insts[j]:insts[j] + 1])
        sch.inc(kh, "act")

        sch.wait(T_, "dve"); sch.wait(T_, "act")
        for j in range(4):
            tr = T_.transpose(ps["ptCD"][:, 512 + j * 128:512 + (j + 1) * 128],
                              sb["nA4"][:, j * 128:(j + 1) * 128], idb[:, :])
        sch.inc(tr, "pe")
        sch.wait(A_, "pe")
        pn1 = A_.copy(sb["PNa"][:], ps["ptCD"][:, 512:1024])
        sch.inc(pn1, "act")

        PA = [sb["nA4"], sb["PAa"], sb["PAb"]]
        PN = [sb["PNa"], sb["PNb"]]
        Z = [sb["Za"], sb["Zb"]]
        zi = 0
        pa_cur, pn_cur = 0, 0
        for step, k in enumerate([1, 2, 4, 8, 16, 32, 64]):
            need_pa = (k < 32)
            sch.wait(T_, "act"); sch.wait(T_, "dve")
            for j in range(4):
                mm = T_.matmul(ps["pz"][:, j * 256:(j + 1) * 256],
                               PN[pn_cur][:, j * 128:(j + 1) * 128],
                               Z[zi][:, j * 256:(j + 1) * 256],
                               start=True, stop=True)
            sch.inc(mm, "pe")
            if k < 64:
                pa_src = PA[0] if step == 0 else PA[1 + (pa_cur % 2)]
                pn_src = PN[pn_cur]
                if need_pa:
                    for j in range(4):
                        mm = T_.matmul(ps["pq1"][:, j * 128:(j + 1) * 128],
                                       pn_src[:, j * 128:(j + 1) * 128],
                                       pa_src[:, j * 128:(j + 1) * 128],
                                       start=True, stop=True)
                    sch.inc(mm, "pe")
                for j in range(4):
                    mm = T_.matmul(ps["pq2"][:, j * 128:(j + 1) * 128],
                                   pa_src[:, j * 128:(j + 1) * 128],
                                   pn_src[:, j * 128:(j + 1) * 128],
                                   start=True, stop=True)
                sch.inc(mm, "pe")
            sch.wait(V, "pe")
            for j in range(4):
                zt = V.tensor_tensor(
                    Z[1 - zi][:, j * 256:(j + 1) * 256],
                    ps["pz"][:, j * 256:(j + 1) * 256],
                    Z[zi][:, j * 256:(j + 1) * 256], ALU.add)
            sch.inc(zt, "dve")
            if k < 64:
                sch.wait(A_, "pe")
                if need_pa:
                    pa_nxt = PA[1 + ((pa_cur + 1) % 2)]
                    pe1 = A_.copy(pa_nxt[:], ps["pq1"][:])
                    sch.inc(pe1, "act")
                    pa_cur += 1
                pn_nxt = PN[1 - pn_cur]
                pe2 = A_.copy(pn_nxt[:], ps["pq2"][:])
                sch.inc(pe2, "act")
                pn_cur = 1 - pn_cur
            zi = 1 - zi
        Zf = Z[zi]

        sch.wait(T_, "dve")
        for j in range(4):
            tw = T_.transpose(ps["ptCD"][:, 512 + j * 128:512 + (j + 1) * 128],
                              Zf[:, j * 256 + 128:(j + 1) * 256], idb[:, :])
        sch.inc(tw, "pe")
        sch.wait(A_, "pe")
        ng = A_.activation(sb["nTW4"][:], ps["ptCD"][:, 512:1024], AF.Copy,
                           scale=cst.ap(-1.0))
        sch.inc(ng, "act")

        sch.wait(V, "pe"); sch.wait(V, "act")
        cb = V.tensor_copy(sb["Sb4"][:], sb["S4"][:])
        sch.inc(cb, "dve")
        sch.wait(T_, "act"); sch.wait(T_, "dve")
        for j in range(4):
            mm = T_.matmul(ps["pq1"][:, j * 128:(j + 1) * 128],
                           sb["nTW4"][:, j * 128:(j + 1) * 128],
                           sb["Sb4"][:, j * 128:(j + 1) * 128],
                           start=True, stop=True)
        sch.inc(mm, "pe")
        sch.wait(V, "pe")
        uo = V.tensor_tensor(sb["U4"][:].rearrange("p (j d) -> p j d", d=128),
                             ps["pq1"][:].rearrange("p (j d) -> p j d", d=128),
                             Zf[:].rearrange("p (j c) -> p j c",
                                             c=256)[:, :, 0:128],
                             ALU.add)
        sch.inc(uo, "dve")
        sch.wait(T_, "dve")
        for j in range(4):
            mm = T_.matmul(ps["pq2"][:, j * 128:(j + 1) * 128],
                           sb["KH4"][:, j * 128:(j + 1) * 128],
                           sb["U4"][:, j * 128:(j + 1) * 128],
                           start=True, stop=True)
            T_.matmul(ps["pO"][:, j * 128:(j + 1) * 128],
                      sb["QB4"][:, j * 64:(j + 1) * 64],
                      sb["Sb4"][:, j * 128:(j + 1) * 128],
                      start=True, stop=False)
            mo = T_.matmul(ps["pO"][:, j * 128:(j + 1) * 128],
                           sb["AT4"][:, j * 64:(j + 1) * 64],
                           sb["U4"][:, j * 128:(j + 1) * 128],
                           start=False, stop=True)
        sch.inc(mo, "pe")
        sch.wait(V, "pe")
        for j in range(4):
            su = V.scalar_tensor_tensor(
                sb["S4"][:, j * 128:(j + 1) * 128],
                sb["S4"][:, j * 128:(j + 1) * 128],
                sb["lamC"][:, insts[j]:insts[j] + 1],
                ps["pq2"][:, j * 128:(j + 1) * 128], ALU.mult, ALU.add)
        sch.inc(su, "dve")
        sch.wait(A_, "pe")
        for j in range(4):
            A_.copy(sb["osb"][:, j * 128:(j + 1) * 128],
                    ps["pO"][:, j * 128:(j + 1) * 128])
            oa = A_.activation(sb["osq"][:], ps["pO"][:, j * 128:(j + 1) * 128],
                               AF.Square,
                               accum_out=sb["ocol"][0:64, j:j + 1])
        sch.inc(oa, "act")
        sch.wait(V, "act")
        rv = V.scalar_tensor_tensor(sb["ocol"][0:64, 8:12],
                                    sb["ocol"][0:64, 0:4], 1.0 / 128.0,
                                    sb["ocol"][0:64, 4:8],
                                    ALU.mult, ALU.add)
        sch.inc(rv, "dve")
        sch.wait(A_, "dve")
        rsq = A_.activation(sb["ocol"][0:64, 8:12], sb["ocol"][0:64, 8:12],
                            AF.Sqrt)
        sch.inc(rsq, "act")
        sch.wait(V, "act")
        rrec = V.reciprocal(sb["ocol"][0:64, 8:12], sb["ocol"][0:64, 8:12])
        sch.inc(rrec, "dve")
        V.wait_ge(sch.s("dve"), sch.now("dve"))
        tok0 = (sbk * SBT + ci * TAU)
        gbase = (tok0 % 128)
        gtile = tok0 // 128
        gcp = V.tensor_copy(
            sb["ggs"][:],
            sb["silu_gg"][gbase:gbase + 64, gtile * 512:(gtile + 1) * 512])
        sch.inc(gcp, "dve")
        V.wait_ge(sch.s("dve"), sch.now("dve"))
        for j in range(4):
            og = V.scalar_tensor_tensor(
                sb["ogbf"][:, j * 128:(j + 1) * 128],
                sb["osb"][:, j * 128:(j + 1) * 128],
                sb["ocol"][0:64, 8 + j:9 + j],
                sb["ggs"][:, j * 128:(j + 1) * 128], ALU.mult, ALU.mult)
        sch.inc(og, "dve")
        sch.wait(T_, "dve")
        for j in range(4):
            to = T_.transpose(ps["ptCD"][:, 512 + j * 128:512 + j * 128 + 64],
                              sb["ogbf"][:, j * 128:(j + 1) * 128],
                              idb[0:64, 0:64])
        sch.inc(to, "pe")
        sch.wait(A_, "pe")
        A_.wait_ge(sch.s("dog"), sch.now("dog"))
        for j in range(4):
            oe = A_.copy(sb["ogst"][:, j * 64:(j + 1) * 64],
                         ps["ptCD"][:, 512 + j * 128:512 + j * 128 + 64])
        sch.inc(oe, "act")
        SY.wait_ge(sch.s("act"), sch.now("act"))
        dg = SY.dma_start(
            out=scr["ogi"].ap().rearrange("(j p) t -> p j t",
                                          p=128)[:, :, tok0:tok0 + 64],
            in_=sb["ogst"][:].rearrange("p (j t) -> p j t", t=64))
        sch.dinc(dg, "dog")
        if sbk == 0 and ci == 0:
            dbg_t = self.dbg_t
            for nm, src in (("nA4", sb["nA4"][:]), ("AT4", sb["AT4"][:]),
                            ("QB4", sb["QB4"][:]), ("KH4", sb["KH4"][:]),
                            ("U4d", sb["U4"][:]), ("S4d", sb["S4"][:]),
                            ("Zf0", Zf[:]), ("PN2d", sb["PNb"][:])):
                if nm in dbg_t:
                    sch.wait(SY, "act"); sch.wait(SY, "dve")
                    dd = SY.dma_start(out=dbg_t[nm], in_=src)
                    sch.dinc(dd, "dbg")
            if sch.now("dbg"):
                SY.wait_ge(sch.s("dbg"), sch.now("dbg"))


    def stat_prep(self):
        S = self.S
        S("fabs", [128, 1024])
        S("fscols", [128, 288])
        S("macol", [128, 8])
        z = self.nc.vector.memset(self.sb["fscols"][:], 0.0)
        self.sch.inc(z, "dve")
        self.stat_jobs = []
        ci = 0
        for kc in range(16):
            for p_ in range(11):
                self.stat_jobs.append(("wguT", kc * 128, p_ * 1024, 1024, ci))
                ci += 1
        for kc in range(44):
            for p_ in range(2):
                self.stat_jobs.append(("wdT", kc * 128, p_ * 1024, 1024, ci))
                ci += 1
        self.stat_pos = 0
        self.n_gu_jobs = 176

    def stat_emit(self, count):
        V, A_, T_, G_, SY = self.engines()
        sch, sb, din = self.sch, self.sb, self.din
        lands = [sb["wld0"], sb["wld1"]]
        for _ in range(count):
            if self.stat_pos >= len(self.stat_jobs):
                return
            name, r0, c0, w, slot = self.stat_jobs[self.stat_pos]
            land = lands[self.stat_pos % 2]
            if self.stat_pos < 2:
                SY.wait_ge(sch.s("wcast"), sch.now("wcast"))
            else:
                SY.wait_ge(sch.s("fs_done"), self.stat_pos - 1)
            d = SY.dma_start(out=land[:, 0:w],
                             in_=din[name][r0:r0 + 128, c0:c0 + w])
            dsem = f"dfs{self.stat_pos % 2}"
            sch.dinc(d, dsem)
            if self.stat_pos % 2 == 0:
                A_.wait_ge(sch.s(dsem), sch.now(dsem))
                a = A_.activation(sb["fabs"][:, 0:w], land[:, 0:w], AF.Abs,
                                  accum_out=sb["fscols"][:, slot:slot + 1])
                sch.inc(a, "fs_done")
            else:
                V.wait_ge(sch.s(dsem), sch.now(dsem))
                g2 = V.scalar_tensor_tensor(
                    sb["fabs"][:, 0:w], land[:, 0:w], -1.0, land[:, 0:w],
                    ALU.mult, ALU.max,
                    accum_out=sb["fscols"][:, slot:slot + 1])
                sch.inc(g2, "fs_done")
            self.stat_pos += 1

    def stat_final(self):
        V, A_, T_, G_, SY = self.engines()
        sch, cst, sb, ps = self.sch, self.cst, self.sb, self.ps
        sch.wait(A_, "fs_done"); sch.wait(A_, "dve"); sch.wait(A_, "pool")
        st = A_.copy(sb["fabs"][:, 0:264], sb["fscols"][:, 0:264])
        sch.inc(st, "act")
        sch.wait(V, "act")
        r1 = V.tensor_reduce(sb["macol"][:, 6:7], sb["fabs"][:, 0:176],
                             mybir.AxisListType.X, ALU.add)
        r2 = V.tensor_reduce(sb["macol"][:, 7:8], sb["fabs"][:, 176:264],
                             mybir.AxisListType.X, ALU.add)
        sch.inc(r2, "dve")
        sch.wait(T_, "dve")
        t1 = T_.matmul(ps["pw0"][0:1, 0:2], sb["ones128_sb"][:],
                       sb["macol"][:, 6:8], start=True, stop=True)
        sch.inc(t1, "pe")
        sch.wait(A_, "pe")
        ev = A_.copy(sb["macol"][0:1, 4:6], ps["pw0"][0:1, 0:2])
        sch.inc(ev, "act")
        sch.wait(T_, "act")
        t2 = T_.matmul(ps["pw0"][:, 0:2], sb["ones1_sb"][:],
                       sb["macol"][0:1, 4:6], start=True, stop=True)
        sch.inc(t2, "pe")
        sch.wait(V, "pe")
        c1 = V.tensor_scalar(sb["macol"][:, 0:2], ps["pw0"][:, 0:2],
                             1.0 / 23068672.0, 1e-5, ALU.mult, ALU.max)
        c2 = V.tensor_scalar(sb["macol"][:, 1:2], ps["pw0"][:, 1:2],
                             1.0 / 11534336.0, 1e-5, ALU.mult, ALU.max)
        sch.inc(c2, "dve")
        A_.wait_ge(sch.s("dve"), sch.now("dve"))
        A_.activation(sb["macol"][:, 2:3], sb["macol"][:, 0:1], AF.Copy,
                      scale=0.5)
        A_.activation(sb["macol"][:, 3:4], sb["macol"][:, 0:1], AF.Copy,
                      scale=-0.5)
        A_.activation(sb["macol"][:, 4:5], sb["macol"][:, 1:2], AF.Copy,
                      scale=0.5)
        c6 = A_.activation(sb["macol"][:, 5:6], sb["macol"][:, 1:2], AF.Copy,
                           scale=-0.5)
        sch.inc(c6, "macol_r")

    def phase4(self, stC):
        V, A_, T_, G_, SY = self.engines()
        nc = self.nc
        sch, cst, sb, ps, din = self.sch, self.cst, self.sb, self.ps, self.din
        scr = self.scr
        S, P = self.S, self.P
        S("wo_b", [128, 16384], BF16, stack=stC)
        S("worhs", [128, 8192], BF16, stack=stC)
        S("bl0", [128, 512], BF16, stack=stC)
        S("bl1", [128, 512], BF16, stack=stC)
        S("sevac", [128, 512], BF16, stack=stC)
        P("pw0", [128, 512], stack=stC); P("pw1", [128, 512], stack=stC)
        self.stat_final()

        G_.wait_ge(sch.s("dog"), sch.now("dog"))
        cc = G_.collective_compute(
            "AllGather", ALU.bypass,
            replica_groups=[[0, 1], [2, 3], [4, 5], [6, 7]],
            ins=[scr["ogi"].ap().opt()],
            outs=[scr["ogo"].ap().opt()])
        cc.then_inc(sch.s("ccs"))
        sch.cnt["ccs"] += 1
        lands = [sb["wld0"], sb["wld1"]]
        for kc in range(8):
            for p_ in range(2):
                li = (kc * 2 + p_)
                land = lands[li % 2]
                if li >= 2:
                    SY.wait_ge(sch.s("wocast"), li - 1)
                wsem = f"dwo{li % 2}"
                dd = SY.dma_start(
                    out=land[:, 0:1024],
                    in_=din["woT"][kc * 128:(kc + 1) * 128,
                                   p_ * 1024:(p_ + 1) * 1024])
                sch.dinc(dd, wsem)
                V.wait_ge(sch.s(wsem), sch.now(wsem))
                cv = V.tensor_copy(
                    sb["wo_b"][:, kc * 2048 + p_ * 1024:
                               kc * 2048 + (p_ + 1) * 1024],
                    land[:, 0:1024])
                sch.inc(cv, "wocast")
        SY.wait_ge(sch.s("ccs"), sch.now("ccs"))
        for kc in range(8):
            for tokt in range(2):
                dd = SY.dma_start(
                    out=sb["bl0"][:],
                    in_=scr["ogo"].ap()[kc * 128:(kc + 1) * 128,
                                        tokt * 512:(tokt + 1) * 512])
                sch.dinc(dd, "dgo")
                dd = SY.dma_start(
                    out=sb["bl1"][:],
                    in_=scr["ogo"].ap()[kc * 128:(kc + 1) * 128,
                                        1024 + tokt * 512:
                                        1024 + (tokt + 1) * 512])
                sch.dinc(dd, "dgo")
                G_.wait_ge(sch.s("dgo"), sch.now("dgo"))
                gb = G_.tensor_tensor(
                    sb["bl0"][:], sb["bl0"][:],
                    sb["maskme_sb"][:].broadcast_to((128, 512)), ALU.mult)
                sch.inc(gb, "pool")
                V.wait_ge(sch.s("pool"), sch.now("pool"))
                V.wait_ge(sch.s("dgo"), sch.now("dgo"))
                bv = V.scalar_tensor_tensor(
                    sb["worhs"][:, kc * 1024 + tokt * 512:
                                kc * 1024 + (tokt + 1) * 512],
                    sb["bl1"][:], sb["maskpa_sb"][:], sb["bl0"][:],
                    ALU.mult, ALU.add)
                sch.inc(bv, "dve")
                SY.wait_ge(sch.s("dve"), sch.now("dve"))
        sch.wait(T_, "dve"); sch.wait(T_, "wocast")
        for m in range(16):
            for tokt in range(2):
                pw = ps["pw0"] if (m * 2 + tokt) % 2 == 0 else ps["pw1"]
                if m * 2 + tokt >= 2:
                    T_.wait_ge(sch.s("act"), sch.now("act"))
                for kc in range(8):
                    mm = T_.matmul(
                        pw[:],
                        sb["wo_b"][:, kc * 2048 + m * 128:
                                   kc * 2048 + (m + 1) * 128],
                        sb["worhs"][:, kc * 1024 + tokt * 512:
                                    kc * 1024 + (tokt + 1) * 512],
                        start=(kc == 0), stop=(kc == 7))
                sch.inc(mm, "pe")
                A_.wait_ge(sch.s("pe"), sch.now("pe"))
                ev = A_.copy(sb["sevac"][:], pw[:])
                sch.inc(ev, "act")
                SY.wait_ge(sch.s("act"), sch.now("act"))
                dd = SY.dma_start(
                    out=scr["s"][m * 128:(m + 1) * 128,
                                 tokt * 512:(tokt + 1) * 512],
                    in_=sb["sevac"][:])
                sch.dinc(dd, "dso")
                A_.wait_ge(sch.s("dso"), sch.now("dso"))
        if "sT" in self.dbg_t:
            SY.wait_ge(sch.s("dso"), sch.now("dso"))
            dd = SY.dma_start(out=self.dbg_t["sT"], in_=scr["s"][:, :])
            sch.dinc(dd, "dbg")
            SY.wait_ge(sch.s("dbg"), sch.now("dbg"))

    def phase5(self, stD):
        V, A_, T_, G_, SY = self.engines()
        sch, cst, sb, ps, din = self.sch, self.cst, self.sb, self.ps, self.din
        scr, dbg_t = self.scr, self.dbg_t
        S, P = self.S, self.P
        S("xh", [128, 8192], stack=stD)
        S("nsb", [128, 8192], BF16, stack=stD)
        S("Mx", [128, 1024], stack=stD)

        S("rows5", [1, 3072], stack=stD)
        S("mubc", [128, 512], stack=stD)
        S("fpbc", [128, 512], stack=stD)
        S("sxbc", [128, 1024], stack=stD)
        S("sqs", [128, 512], stack=stD)
        R5 = sb["rows5"]
        def row5(i):
            return R5[0:1, i * 512:(i + 1) * 512]
        zm = V.memset(sb["Mx"][:], 0.0)
        sch.inc(zm, "dve")

        for tht in range(2):
            SY.wait_ge(sch.s("dve"), sch.now("dve"))
            SY.wait_ge(sch.s("pool"), sch.now("pool"))
            d = SY.dma_start(
                out=sb["xh"][:].rearrange("p (f t) -> p f t", f=16),
                in_=din["xhT"][:, tht * 512:(tht + 1) * 512].rearrange(
                    "(f p) t -> p f t", p=128))
            sch.dinc(d, "dx5")
            T_.wait_ge(sch.s("dx5"), sch.now("dx5"))
            A_.wait_ge(sch.s("dx5"), sch.now("dx5"))
            for fc in range(16):
                xc = sb["xh"][:, fc * 512:(fc + 1) * 512]
                tgt = sch.now("pe") - 1
                if tgt > 0:
                    A_.wait_ge(sch.s("pe"), tgt)
                q = A_.activation(sb["sqs"][:], xc, AF.Square)
                sch.inc(q, "act")
                T_.matmul(ps["pg0"][0:1, :], sb["ones128_sb"][:], xc,
                          start=(fc == 0), stop=(fc == 15))
                T_.wait_ge(sch.s("act"), sch.now("act"))
                mm = T_.matmul(ps["pg1"][0:1, :], sb["ones128_sb"][:],
                               sb["sqs"][:], start=(fc == 0), stop=(fc == 15))
                sch.inc(mm, "pe")
            A_.wait_ge(sch.s("pe"), sch.now("pe"))
            m1 = A_.activation(row5(0), ps["pg0"][0:1, :], AF.Copy,
                               scale=1.0 / 2048.0)
            m2 = A_.activation(row5(2), ps["pg1"][0:1, :], AF.Copy,
                               scale=1.0 / 2048.0)
            sch.inc(m2, "act")
            V.wait_ge(sch.s("act"), sch.now("act"))
            V.tensor_tensor(row5(4), row5(0), row5(0), ALU.mult)
            V.tensor_tensor(row5(4), row5(2), row5(4), ALU.subtract)
            v3 = V.tensor_single_scalar(row5(3), row5(4), EPS, ALU.add)
            sch.inc(v3, "dve")
            A_.wait_ge(sch.s("dve"), sch.now("dve"))
            s1 = A_.activation(row5(1), row5(3), AF.Sqrt)
            sch.inc(s1, "act")
            V.wait_ge(sch.s("act"), sch.now("act"))
            r1 = V.reciprocal(row5(1), row5(1))
            sch.inc(r1, "dve")
            V.wait_ge(sch.s("dve"), sch.now("dve"))
            V.tensor_tensor(row5(2), row5(4), row5(1), ALU.mult)
            V.tensor_tensor(row5(2), row5(2), row5(1), ALU.mult)
            v5 = V.tensor_single_scalar(row5(2), row5(2), EPS, ALU.add)
            sch.inc(v5, "dve")
            A_.wait_ge(sch.s("dve"), sch.now("dve"))
            s2 = A_.activation(row5(2), row5(2), AF.Sqrt)
            sch.inc(s2, "act")
            V.wait_ge(sch.s("act"), sch.now("act"))
            V.reciprocal(row5(2), row5(2))
            v6 = V.tensor_tensor(row5(2), row5(2), row5(1), ALU.mult)
            sch.inc(v6, "dve")
            T_.wait_ge(sch.s("dve"), sch.now("dve"))
            T_.matmul(ps["pg2"][:], sb["ones1_sb"][:], row5(0), start=True,
                      stop=True)
            bb = T_.matmul(ps["pg3"][:], sb["ones1_sb"][:], row5(2),
                           start=True, stop=True)
            sch.inc(bb, "pe")
            A_.wait_ge(sch.s("pe"), sch.now("pe"))
            A_.copy(sb["mubc"][:], ps["pg2"][:])
            c2 = A_.copy(sb["fpbc"][:], ps["pg3"][:])
            sch.inc(c2, "act")
            G_.wait_ge(sch.s("act"), sch.now("act"))
            for fc in range(16):
                xc = sb["xh"][:, fc * 512:(fc + 1) * 512]
                G_.tensor_tensor(xc, xc, sb["mubc"][:], ALU.subtract)
                G_.tensor_tensor(
                    xc, xc,
                    sb["nw_sb"][:, fc:fc + 1].broadcast_to((128, 512)),
                    ALU.mult)
                g2 = G_.tensor_tensor(xc, xc, sb["fpbc"][:], ALU.mult)
                sch.inc(g2, "pool")
                V.wait_ge(sch.s("pool"), sch.now("pool"))
                V.scalar_tensor_tensor(xc, xc, -1.0, xc, ALU.mult, ALU.max)
                mv = V.tensor_tensor(sb["Mx"][:, tht * 512:(tht + 1) * 512],
                                     xc,
                                     sb["Mx"][:, tht * 512:(tht + 1) * 512],
                                     ALU.max)
                sch.inc(mv, "dve")
            sch.wait(T_, "dve")
            for tr in range(4):
                mm = T_.transpose(ps["pg4"][:, tr * 128:(tr + 1) * 128],
                                  sb["Mx"][:, tht * 512 + tr * 128:
                                           tht * 512 + (tr + 1) * 128],
                                  sb["iden_f_sb"][:])
            sch.inc(mm, "pe")
            sch.wait(V, "pe")
            for tr in range(4):
                rv = V.tensor_reduce(sb["colx"][:, tht * 4 + tr:tht * 4 + tr + 1],
                                     ps["pg4"][:, tr * 128:(tr + 1) * 128],
                                     mybir.AxisListType.X, ALU.max)
            sch.inc(rv, "dve")
            V.wait_ge(sch.s("dve"), sch.now("dve"))
            cl = V.tensor_single_scalar(
                sb["colx"][:, 8 + tht * 4:8 + tht * 4 + 4],
                sb["colx"][:, tht * 4:tht * 4 + 4], 1e-5, ALU.max)
            sch.inc(cl, "dve")
            V.wait_ge(sch.s("dve"), sch.now("dve"))
            V.tensor_single_scalar(sb["colx"][:, 16 + tht * 4:16 + tht * 4 + 4],
                                   sb["colx"][:, 8 + tht * 4:8 + tht * 4 + 4],
                                   1.0 / 127.0, ALU.mult)
            rc = V.reciprocal(sb["colx"][:, 24 + tht * 4:24 + tht * 4 + 4],
                              sb["colx"][:, 8 + tht * 4:8 + tht * 4 + 4])
            sch.inc(rc, "dve")
            V.wait_ge(sch.s("dve"), sch.now("dve"))
            sx = V.tensor_single_scalar(
                sb["colx"][:, 24 + tht * 4:24 + tht * 4 + 4],
                sb["colx"][:, 24 + tht * 4:24 + tht * 4 + 4],
                127.0, ALU.mult)
            sch.inc(sx, "dve")
            sch.wait(T_, "dve")
            for tr in range(4):
                mm = T_.matmul(
                    ps["pg5"][:, tr * 128:(tr + 1) * 128],
                    sb["colx"][:, 24 + tht * 4 + tr:25 + tht * 4 + tr
                               ].broadcast_to((128, 128)),
                    sb["iden_f_sb"][:], start=True, stop=True)
            sch.inc(mm, "pe")
            sch.wait(A_, "pe")
            sbc = A_.copy(sb["sxbc"][:, tht * 512:(tht + 1) * 512], ps["pg5"][:])
            sch.inc(sbc, "act")
            SY.wait_ge(sch.s("pool"), sch.now("pool"))
            d = SY.dma_start(
                out=sb["xh"][:].rearrange("p (f t) -> p f t", f=16),
                in_=din["xhT"][:, tht * 512:(tht + 1) * 512].rearrange(
                    "(f p) t -> p f t", p=128))
            sch.dinc(d, "dx5")
            G_.wait_ge(sch.s("dx5"), sch.now("dx5"))
            G_.wait_ge(sch.s("act"), sch.now("act"))
            G_.wait_ge(sch.s("dve"), sch.now("dve"))
            gx = G_.tensor_tensor(sb["fpbc"][:], sb["fpbc"][:],
                                  sb["sxbc"][:, tht * 512:(tht + 1) * 512],
                                  ALU.mult)
            sch.inc(gx, "pool")
            V.wait_ge(sch.s("dx5"), sch.now("dx5"))
            V.wait_ge(sch.s("act"), sch.now("act"))
            for fc in range(16):
                xc = sb["xh"][:, fc * 512:(fc + 1) * 512]
                v1 = V.tensor_tensor(xc, xc, sb["mubc"][:], ALU.subtract)
                sch.inc(v1, "dve")
                G_.wait_ge(sch.s("dve"), sch.now("dve"))
                G_.tensor_tensor(
                    xc, xc,
                    sb["nw_sb"][:, fc:fc + 1].broadcast_to((128, 512)),
                    ALU.mult)
                g3 = G_.tensor_tensor(xc, xc, sb["fpbc"][:], ALU.mult)
                sch.inc(g3, "pool")
                V.wait_ge(sch.s("pool"), sch.now("pool"))
                nv = V.tensor_scalar(sb["nsb"][:, fc * 512:(fc + 1) * 512],
                                     xc, MAGIC, MAGIC, ALU.add, ALU.subtract)
                sch.inc(nv, "dve")
            SY.wait_ge(sch.s("dve"), sch.now("dve"))
            d = SY.dma_start(
                out=scr["n"][:, tht * 512:(tht + 1) * 512].rearrange(
                    "(f p) t -> p f t", p=128),
                in_=sb["nsb"][:].rearrange("p (f t) -> p f t", f=16))
            sch.dinc(d, "dn5")
        if "nT" in dbg_t:
            SY.wait_ge(sch.s("dn5"), sch.now("dn5"))
            dd = SY.dma_start(out=dbg_t["nT"], in_=scr["n"][:, :])
            sch.dinc(dd, "dbg")
            SY.wait_ge(sch.s("dbg"), sch.now("dbg"))

    def phase5b(self, stD):
        V, A_, T_, G_, SY = self.engines()
        sch, cst, sb, ps, din = self.sch, self.cst, self.sb, self.ps, self.din
        scr, dbg_t = self.scr, self.dbg_t
        S = self.S
        S("ngu", [128, 16384], BF16, stack=stD)
        S("terng", [128, 8192], BF16, stack=stD)
        S("ternu", [128, 8192], BF16, stack=stD)

        S("hst", [128, 512], stack=stD)
        S("rsc", [128, 512], stack=stD)
        S("rsc2", [128, 512], stack=stD)

        zm = V.memset(sb["mh"][:], 0.0)
        sch.inc(zm, "dve")
        SY.wait_ge(sch.s("dn5"), sch.now("dn5"))
        d = SY.dma_start(out=sb["ngu"][:].rearrange("p (f t) -> p f t", f=16),
                         in_=scr["n"][:, :].rearrange("(f p) t -> p f t",
                                                     p=128))
        sch.dinc(d, "dn5")
        lands = [sb["wld0"], sb["wld1"]]
        T_.wait_ge(sch.s("dn5"), sch.now("dn5"))
        for grp in range(11):
            for kc in range(16):
                li = grp * 16 + kc
                land = lands[li % 2]
                if li >= 2:
                    SY.wait_ge(sch.s("gtern"), li - 1)
                gsem = f"dwgu{li % 2}"
                dd = SY.dma_start(
                    out=land[:, 0:512],
                    in_=din["wguT"][kc * 128:(kc + 1) * 128,
                                    grp * 512:(grp + 1) * 512])
                sch.dinc(dd, gsem)
                dd = SY.dma_start(
                    out=land[:, 512:1024],
                    in_=din["wguT"][kc * 128:(kc + 1) * 128,
                                    5632 + grp * 512:5632 + (grp + 1) * 512])
                sch.dinc(dd, gsem)
                V.wait_ge(sch.s(gsem), sch.now(gsem))
                V.wait_ge(sch.s("macol_r"), 1)
                if li >= 1:
                    V.wait_ge(sch.s("gtern"), li)
                mp = V.tensor_scalar(sb["mplus"][:], land[:, 0:1024],
                                     sb["macol"][:, 2:3], None, ALU.is_ge)
                sch.inc(mp, "dve")
                V.wait_ge(sch.s("macol_r"), 1)
                gm = V.tensor_scalar(sb["mminus"][:], land[:, 0:1024],
                                     sb["macol"][:, 3:4], None, ALU.is_le)
                sch.inc(gm, "dve")
                G_.wait_ge(sch.s("dve"), sch.now("dve"))
                if li == 0 and "p5b_l" in dbg_t:
                    for nm, src in (("p5b_l", land[:, 0:1024]),
                                    ("p5b_mp", sb["mplus"][:]),
                                    ("p5b_mm", sb["mminus"][:])):
                        dd = SY.dma_start(out=dbg_t[nm], in_=src)
                        sch.dinc(dd, "dbg")
                    SY.wait_ge(sch.s("dbg"), sch.now("dbg"))
                    V.wait_ge(sch.s("dbg"), sch.now("dbg"))
                G_.wait_ge(sch.s("pe"), sch.now("pe"))
                t1 = G_.tensor_tensor(sb["terng"][:, kc * 512:(kc + 1) * 512],
                                      sb["mplus"][:, 0:512],
                                      sb["mminus"][:, 0:512], ALU.subtract)
                t2 = G_.tensor_tensor(sb["ternu"][:, kc * 512:(kc + 1) * 512],
                                      sb["mplus"][:, 512:1024],
                                      sb["mminus"][:, 512:1024], ALU.subtract)
                sch.inc(t2, "gtern")
            for tokt in range(2):
                it = grp * 2 + tokt
                for m4 in range(4):
                    if it >= 1:
                        T_.wait_ge(sch.s("h_done"), (it - 1) * 4 + m4 + 1)
                    for kc in range(16):
                        if tokt == 0 and m4 == 0:
                            T_.wait_ge(sch.s("gtern"), grp * 16 + kc + 1)
                        T_.matmul(ps[f"pg{m4}"][:],
                                  sb["terng"][:, kc * 512 + m4 * 128:
                                              kc * 512 + (m4 + 1) * 128],
                                  sb["ngu"][:, kc * 1024 + tokt * 512:
                                            kc * 1024 + (tokt + 1) * 512],
                                  start=(kc == 0), stop=(kc == 15))
                    for kc in range(16):
                        mm = T_.matmul(
                            ps[f"pg{4 + m4}"][:],
                            sb["ternu"][:, kc * 512 + m4 * 128:
                                        kc * 512 + (m4 + 1) * 128],
                            sb["ngu"][:, kc * 1024 + tokt * 512:
                                      kc * 1024 + (tokt + 1) * 512],
                            start=(kc == 0), stop=(kc == 15))
                sch.inc(mm, "pe")
                for m4 in range(4):
                    A_.wait_ge(sch.s("pe"), sch.now("pe"))
                    A_.wait_ge(sch.s("h_done"), sch.now("h_done"))
                    ar = A_.activation(sb["rsc"][:], ps[f"pg{m4}"][:],
                                       AF.Relu)
                    sch.inc(ar, "act")
                    G_.wait_ge(sch.s("act"), sch.now("act"))
                    G_.wait_ge(sch.s("dve"), sch.now("dve"))
                    sq = G_.tensor_tensor(sb["rsc2"][:], sb["rsc"][:],
                                          sb["rsc"][:], ALU.mult)
                    sch.inc(sq, "pool")
                    V.wait_ge(sch.s("pool"), sch.now("pool"))
                    V.wait_ge(sch.s("h_dma"), sch.now("h_dma"))
                    hv = V.tensor_tensor(sb["hst"][:], sb["rsc2"][:],
                                         ps[f"pg{4 + m4}"][:], ALU.mult)
                    sch.inc(hv, "dve")
                    A_.wait_ge(sch.s("dve"), sch.now("dve"))
                    ab = A_.activation(sb["rsc"][:], sb["hst"][:], AF.Abs)
                    sch.inc(ab, "act")
                    V.wait_ge(sch.s("act"), sch.now("act"))
                    mv = V.tensor_tensor(
                        sb["mh"][:, tokt * 512:(tokt + 1) * 512],
                        sb["rsc"][:],
                        sb["mh"][:, tokt * 512:(tokt + 1) * 512], ALU.max)
                    sch.inc(mv, "h_done")
                    SY.wait_ge(sch.s("h_done"), sch.now("h_done"))
                    dd = SY.dma_start(
                        out=scr["h"][(grp * 4 + m4) * 128:
                                     (grp * 4 + m4 + 1) * 128,
                                     tokt * 512:(tokt + 1) * 512],
                        in_=sb["hst"][:])
                    sch.dinc(dd, "h_dma")
                    if grp == 0 and tokt == 0 and m4 == 0:
                        for nm, src in (("p5b_h", sb["hst"][:]),
                                        ("p5b_t", sb["terng"][:]),
                                        ("p5b_t2", sb["ternu"][:]),
                                        ("p5b_n", sb["ngu"][:])):
                            if nm in dbg_t:
                                dd = SY.dma_start(out=dbg_t[nm], in_=src)
                                sch.dinc(dd, "dbg")
                        if sch.now("dbg"):
                            SY.wait_ge(sch.s("dbg"), sch.now("dbg"))

    def phase5b_dbg(self):
        V, A_, T_, G_, SY = self.engines()
        sch, sb = self.sch, self.sb
        scr, dbg_t = self.scr, self.dbg_t
        for nm, src in (("mh", sb["mh"][:]), ("macol", sb["macol"][:]),
                        ("fscols", sb["fscols"][:])):
            if nm in dbg_t:
                sch.wait(SY, "h_done"); sch.wait(SY, "dve")
                dd = SY.dma_start(out=dbg_t[nm], in_=src)
                sch.dinc(dd, "dbg")
        if "h0" in dbg_t:
            sch.wait(SY, "h_dma")
            dd = SY.dma_start(out=dbg_t["h0"], in_=scr["h"][0:128, :])
            sch.dinc(dd, "dbg")
        if "hfull" in dbg_t:
            sch.wait(SY, "h_dma")
            dd = SY.dma_start(out=dbg_t["hfull"], in_=scr["h"][:, :])
            sch.dinc(dd, "dbg")
        if sch.now("dbg"):
            SY.wait_ge(sch.s("dbg"), sch.now("dbg"))

    def phase6(self, stE):
        V, A_, T_, G_, SY = self.engines()
        sch, cst, sb, ps, din = self.sch, self.cst, self.sb, self.ps, self.din
        scr, dbg_t = self.scr, self.dbg_t
        S = self.S
        S("nh", [128, 45056], BF16, stack=stE)
        S("dtern", [128, 22528], BF16, stack=stE)
        S("psibc", [128, 1024], stack=stE)
        S("dqbc", [128, 1024], stack=stE)
        S("hld", [128, 1024], stack=stE)
        S("hld2", [128, 1024], stack=stE)
        S("gsc", [128, 512], stack=stE)
        S("xld", [128, 512], stack=stE)
        S("sld", [128, 512], BF16, stack=stE)
        S("oev", [128, 512], stack=stE)
        sch.wait(T_, "dve")
        for tr in range(8):
            mm = T_.transpose(ps[f"pg{tr % 4}"][:, 0:128],
                              sb["mh"][:, tr * 128:(tr + 1) * 128],
                              sb["iden_f_sb"][:])
            sch.inc(mm, "pe")
            sch.wait(V, "pe")
            rv = V.tensor_reduce(sb["colx"][:, 32:33],
                                 ps[f"pg{tr % 4}"][:, 0:128],
                                 mybir.AxisListType.X, ALU.max)
            sch.inc(rv, "dve")
            V.wait_ge(sch.s("dve"), sch.now("dve"))
            V.wait_ge(sch.s("macol_r"), 1)
            p1 = V.tensor_tensor(sb["colx"][:, 33:34], sb["macol"][:, 0:1],
                                 sb["colx"][:, 16 + tr:17 + tr], ALU.mult)
            sch.inc(p1, "dve")
            V.wait_ge(sch.s("dve"), sch.now("dve"))
            p2 = V.tensor_scalar(sb["colx"][:, 34:35], sb["colx"][:, 33:34],
                                 sb["colx"][:, 33:34], sb["colx"][:, 33:34],
                                 ALU.mult, ALU.mult)
            sch.inc(p2, "dve")
            V.wait_ge(sch.s("dve"), sch.now("dve"))
            p3 = V.tensor_scalar(sb["colx"][:, 35:36], sb["colx"][:, 32:33],
                                 sb["colx"][:, 34:35], 1e-5,
                                 ALU.mult, ALU.max)
            sch.inc(p3, "dve")
            V.wait_ge(sch.s("dve"), sch.now("dve"))
            rc = V.reciprocal(sb["colx"][:, 36:37], sb["colx"][:, 35:36])
            sch.inc(rc, "dve")
            V.wait_ge(sch.s("dve"), sch.now("dve"))
            p4 = V.tensor_scalar(sb["colx"][:, 37:38], sb["colx"][:, 36:37],
                                 sb["colx"][:, 34:35], 127.0,
                                 ALU.mult, ALU.mult)
            p5 = V.tensor_scalar(sb["colx"][:, 38:39], sb["colx"][:, 35:36],
                                 sb["macol"][:, 1:2], 1.0 / 127.0,
                                 ALU.mult, ALU.mult)
            sch.inc(p5, "dve")
            V.wait_ge(sch.s("dve"), sch.now("dve"))
            sch.wait(T_, "dve")
            sch.wait(T_, "act")
            T_.matmul(ps["pg4"][:, 0:128],
                      sb["colx"][:, 37:38].broadcast_to((128, 128)),
                      sb["iden_f_sb"][:], start=True, stop=True)
            mm = T_.matmul(ps["pg5"][:, 0:128],
                           sb["colx"][:, 38:39].broadcast_to((128, 128)),
                           sb["iden_f_sb"][:], start=True, stop=True)
            sch.inc(mm, "pe")
            sch.wait(A_, "pe")
            A_.copy(sb["psibc"][:, tr * 128:(tr + 1) * 128], ps["pg4"][:, 0:128])
            ac = A_.copy(sb["dqbc"][:, tr * 128:(tr + 1) * 128],
                         ps["pg5"][:, 0:128])
            sch.inc(ac, "act")
        sch.wait(G_, "act"); sch.wait(G_, "h_dma")
        for nm in ("colx",):
            if nm in dbg_t:
                sch.wait(SY, "dve"); sch.wait(SY, "act")
                dd = SY.dma_start(out=dbg_t[nm], in_=sb[nm][:])
                sch.dinc(dd, "dbg")
        for nm in ("psibc", "dqbc"):
            if nm in dbg_t:
                sch.wait(SY, "act")
                dd = SY.dma_start(out=dbg_t[nm], in_=sb[nm][:])
                sch.dinc(dd, "dbg")
        if "nh" in dbg_t:
            sch.wait(SY, "nh_c")
            dd = SY.dma_start(out=dbg_t["nh"], in_=sb["nh"][:])
            sch.dinc(dd, "dbg")
            SY.wait_ge(sch.s("dbg"), sch.now("dbg"))
        lands = [sb["wld0"], sb["wld1"]]
        for md in range(4):
            for kc in range(44):
                li = md * 44 + kc
                land = lands[li % 2]
                if li >= 2:
                    SY.wait_ge(sch.s("dtern_c"), li - 1)
                dsem = f"dwd{li % 2}"
                dd = SY.dma_start(
                    out=land[:, 0:512],
                    in_=din["wdT"][kc * 128:(kc + 1) * 128,
                                   md * 512:(md + 1) * 512])
                sch.dinc(dd, dsem)
                V.wait_ge(sch.s(dsem), sch.now(dsem))
                V.wait_ge(sch.s("macol_r"), 1)
                mpl = sb["mplus"] if li % 2 == 0 else sb["mplus2"]
                mmn = sb["mminus"] if li % 2 == 0 else sb["mminus2"]
                if li >= 2:
                    V.wait_ge(sch.s("dtern_c"), li - 1)
                mp = V.tensor_scalar(mpl[:, 0:512], land[:, 0:512],
                                     sb["macol"][:, 4:5], None, ALU.is_ge)
                sch.inc(mp, "dve")
                V.wait_ge(sch.s("macol_r"), 1)
                gm = V.tensor_scalar(mmn[:, 0:512], land[:, 0:512],
                                     sb["macol"][:, 5:6], None, ALU.is_le)
                sch.inc(gm, "dve")
                G_.wait_ge(sch.s("dve"), sch.now("dve"))
                if md > 0:
                    G_.wait_ge(sch.s("pe"), sch.now("pe"))
                t1 = G_.tensor_tensor(sb["dtern"][:, kc * 512:(kc + 1) * 512],
                                      mpl[:, 0:512],
                                      mmn[:, 0:512], ALU.subtract)
                sch.inc(t1, "dtern_c")
                if md == 0:
                    hl = sb["hld"] if kc % 2 == 0 else sb["hld2"]
                    hsem = f"dh6{kc % 2}"
                    if kc >= 2:
                        SY.wait_ge(sch.s("nh_c"), kc - 1)
                    dd = SY.dma_start(out=hl[:],
                                      in_=scr["h"][kc * 128:(kc + 1) * 128,
                                                   :])
                    sch.dinc(dd, hsem)
                    G_.wait_ge(sch.s(hsem), sch.now(hsem))
                    gq = G_.tensor_tensor(hl[:], hl[:], sb["psibc"][:],
                                          ALU.mult)
                    sch.inc(gq, "pool")
                    V.wait_ge(sch.s("pool"), sch.now("pool"))
                    nv = V.tensor_scalar(sb["nh"][:, kc * 1024:
                                                  (kc + 1) * 1024],
                                         hl[:], MAGIC, MAGIC, ALU.add,
                                         ALU.subtract)
                    sch.inc(nv, "nh_c")
            for tokt in range(2):
                for m in range(4):
                    pgx = ps[f"pg{m}"]
                    if md + tokt > 0:
                        T_.wait_ge(sch.s("act"), sch.now("act"))
                    for kc in range(44):
                        if tokt == 0 and m == 0:
                            T_.wait_ge(sch.s("dtern_c"), md * 44 + kc + 1)
                            if md == 0:
                                T_.wait_ge(sch.s("nh_c"), kc + 1)
                        mm = T_.matmul(
                            pgx[:],
                            sb["dtern"][:, kc * 512 + m * 128:
                                        kc * 512 + (m + 1) * 128],
                            sb["nh"][:, kc * 1024 + tokt * 512:
                                     kc * 1024 + (tokt + 1) * 512],
                            start=(kc == 0), stop=(kc == 43))
                    sch.inc(mm, "pe")
                    V.wait_ge(sch.s("pe"), sch.now("pe"))
                    V.wait_ge(sch.s("act"), sch.now("act"))
                    gv = V.tensor_tensor(
                        sb["gsc"][:], ps[f"pg{m}"][:],
                        sb["dqbc"][:, tokt * 512:(tokt + 1) * 512], ALU.mult)
                    sch.inc(gv, "dve")
                    A_.wait_ge(sch.s("dve"), sch.now("dve"))
                    av = A_.activation(sb["gsc"][:], sb["gsc"][:], AF.Silu)
                    sch.inc(av, "act")
                    fc_out = md * 4 + m
                    dd = SY.dma_start(
                        out=sb["xld"][:],
                        in_=din["xhT"][fc_out * 128:(fc_out + 1) * 128,
                                       tokt * 512:(tokt + 1) * 512])
                    dd2 = SY.dma_start(
                        out=sb["sld"][:],
                        in_=scr["s"][fc_out * 128:(fc_out + 1) * 128,
                                     tokt * 512:(tokt + 1) * 512])
                    sch.dinc(dd, "dx6"); sch.dinc(dd2, "dx6")
                    V.wait_ge(sch.s("dx6"), sch.now("dx6"))
                    V.wait_ge(sch.s("act"), sch.now("act"))
                    V.wait_ge(sch.s("dout"), sch.now("dout"))
                    o1 = V.tensor_tensor(sb["oev"][:], sb["sld"][:],
                                         sb["gsc"][:], ALU.mult)
                    sch.inc(o1, "dve")
                    V.wait_ge(sch.s("dve"), sch.now("dve"))
                    ov = V.tensor_tensor(sb["oev"][:], sb["oev"][:],
                                         sb["xld"][:], ALU.add)
                    sch.inc(ov, "dve")
                    SY.wait_ge(sch.s("dve"), sch.now("dve"))
                    dd = SY.dma_start(
                        out=self.out_t[fc_out * 128:(fc_out + 1) * 128,
                                       tokt * 512:(tokt + 1) * 512],
                        in_=sb["oev"][:])
                    sch.dinc(dd, "dout")
        SY.wait_ge(sch.s("dout"), sch.now("dout"))
        for nm in ("dbg", "dba", "dxn", "dso", "dn5"):
            if sch.now(nm):
                SY.wait_ge(sch.s(nm), sch.now(nm))

    def phase3_stub(self):
        V, A_, T_, G_, SY = self.engines()
        sch, sb = self.sch, self.sb
        S = self.S
        S("zero_out", [128, HALF])
        sch.barrier([V])
        z = V.memset(sb["zero_out"][:], 0.0)
        sch.inc(z, "dve")
        SY.wait_ge(sch.s("dve"), sch.now("dve"))
        for fc in range(16):
            dd = SY.dma_start(out=self.out_t[fc * 128:(fc + 1) * 128, :],
                              in_=sb["zero_out"][:])
            sch.dinc(dd, "dout")
        SY.wait_ge(sch.s("dout"), sch.now("dout"))
        for nm in ("dbg", "dba", "dxn"):
            if sch.now(nm):
                SY.wait_ge(sch.s(nm), sch.now(nm))


def _build(dbg=()):
    b = Build(dbg=dbg)
    from contextlib import ExitStack
    with b.nc.Block() as block:
        @block.sync
        def _(sync):
            b.phase0()
            b.stat_prep()
            with ExitStack() as stM:
                b.S("silu_gg", [128, 16 * 512], BF16, stack=stM)
                with ExitStack() as stA:
                    b.load_small_weights(stA)
                    b.phase1(stA)
                    b.phase2(stA)
                    b.sch.barrier(b.engines())
                with ExitStack() as stB:
                    b.phase3(stB)
                    b.sch.barrier(b.engines())
            with ExitStack() as stC:
                b.phase4(stC)
                b.sch.barrier(b.engines())
            with ExitStack() as stP:
                for i in range(8):
                    b.P(f"pg{i}", [128, 512], stack=stP)
                b.S("colx", [128, 40], stack=stP)
                b.S("mh", [128, 1024], stack=stP)
                b.S("mplus", [128, 1024], BF16, stack=stP)
                b.S("mminus", [128, 1024], BF16, stack=stP)
                b.S("mplus2", [128, 1024], BF16, stack=stP)
                b.S("mminus2", [128, 1024], BF16, stack=stP)
                with ExitStack() as stD:
                    b.phase5(stD)
                    b.sch.barrier(b.engines())
                with ExitStack() as stD2:
                    b.phase5b(stD2)
                    b.phase5b_dbg()
                    b.sch.barrier(b.engines())
                with ExitStack() as stE:
                    b.phase6(stE)
    b.stack.close()
    return b.nc, b.dbg_t


def kernel(**inputs):
    shared, percore = _prep(inputs)
    nc, _ = _build()
    in_maps = []
    for c in range(8):
        m = dict(shared)
        m.update(percore[c])
        in_maps.append(m)
    res = run_bass_kernel_spmd(nc, in_maps, core_ids=list(range(8)))
    x = np.asarray(inputs["L0_attn_x"], np.float32)
    out = np.empty_like(x)
    for c in range(8):
        b, hg = c // 2, c % 2
        out[b, hg * HALF:(hg + 1) * HALF, :] = \
            np.asarray(res.results[c]["outT"]).T
    return out
